# version 02
# baseline (speedup 1.0000x reference)
import numpy as np
import concourse.bass as bass
import concourse.mybir as mybir
from concourse.bass_utils import run_bass_kernel_spmd

F32 = mybir.dt.float32
F32R = mybir.dt.float32r
AT = mybir.AluOpType
AF = mybir.ActivationFunctionType

D = 2048
L = 2048
Bsz = 4
NH = 16
H2 = 32
M4 = 4
NL = 128
I5 = 5632
DC = 16
VELT = 4097.0
EPS_LN = 1e-5
SC_A = float(1.0 / np.sqrt(128.0))
SC_B = float(1.0 / np.sqrt(64.0))

PHASE2 = False
TOK = L // 2 if PHASE2 else L
QT = TOK // 128
OWNLM = NL // 2 if PHASE2 else NL
TC = 512
NTC = TOK // TC
ICH = I5 // 128
DEV_STOP = 4
DEV_DBG = False
DEV_SUB = 0


class Bld:
    def __init__(self, nc, blk, sems):
        self.nc, self.blk, self.sem = nc, blk, sems
        self.c = {k: 0 for k in sems}

    def inc(self, instr, s, n=1):
        instr.then_inc(self.sem[s], n)
        self.c[s] += n
        return self.c[s]

    def dma(self, eng, out, in_, q):
        return self.inc(eng.dma_start(out=out, in_=in_), q, 16)

    def chk(self, s, expected):
        assert self.c[s] == expected, f"sem {s}: emitted {self.c[s]} != planned {expected}"

    def barrier(self):
        tgt = self.c['bar'] + 5
        syncq = [(q, self.c[q]) for q in ('sq', 'w0', 'w1', 'w2', 'w3')]
        gpq = [(q, self.c[q]) for q in ('gq', 'g0', 'g1', 'g2', 'g3', 'g4', 'f0', 'f1', 'f2', 'f3')]
        bar = self.sem['bar']

        def t_f(t):
            t.wait_ge(self.sem['sq'], self.prolog_sq)
            self.inc(t.matmul(self.scratch_ps[96:97, 508:509],
                              self.ident[96:97, 96:97], self.ident[96:97, 96:97],
                              start=True, stop=True, tile_position=(96, 96)), 'bar', 1)
            t.wait_ge(bar, tgt)

        def v_f(v):
            self.inc(v.memset(self.scratch_sm[64:65, 3071:3072], 0.0), 'bar', 1)
            v.wait_ge(bar, tgt)

        def s_f(s):
            self.inc(s.memzero(self.scratch_sm[64:65, 3070:3071]), 'bar', 1)
            s.wait_ge(bar, tgt)

        def q_f(eng):
            for q, v in syncq:
                if v:
                    eng.wait_ge(self.sem[q], v)
            eng.sem_inc(bar, 1)
            self.c['bar'] += 1
            eng.wait_ge(bar, tgt)

        def g_f(eng):
            for q, v in gpq:
                if v:
                    eng.wait_ge(self.sem[q], v)
            eng.sem_inc(bar, 1)
            self.c['bar'] += 1
            eng.wait_ge(bar, tgt)
        self.blk.tensor(t_f)
        self.blk.vector(v_f)
        self.blk.scalar(s_f)
        self.blk.sync(q_f)
        self.blk.gpsimd(g_f)
        assert self.c['bar'] == tgt


def _rope_perm():
    perm = np.zeros(D, dtype=np.int64)
    for h in range(H2):
        for j in range(32):
            perm[32 * h + j] = 64 * h + 2 * j
            perm[1024 + 32 * h + j] = 64 * h + 2 * j + 1
    return perm


def stage_inputs(inputs):
    f32 = np.float32
    perm = _rope_perm()

    wqkv_a = np.ascontiguousarray(inputs['L0_attn_a_Wqkv'], dtype=f32)
    bqkv_a = np.ascontiguousarray(np.asarray(inputs['L0_attn_a_bqkv'], dtype=f32).reshape(48, 128).T)
    wo_a = np.ascontiguousarray(inputs['L0_attn_a_Wo'], dtype=f32)
    bo_a = np.ascontiguousarray(np.asarray(inputs['L0_attn_a_bo'], dtype=f32).reshape(16, 128).T)

    wqkv_b = np.array(inputs['L0_attn_b_w_qkv'], dtype=f32)
    bqkv_b = np.array(inputs['L0_attn_b_b_qkv'], dtype=f32)
    wqkv_b[:, 0:D] = wqkv_b[:, 0:D][:, perm]
    wqkv_b[:, D:2 * D] = wqkv_b[:, D:2 * D][:, perm]
    bqkv_b[0:D] = bqkv_b[0:D][perm]
    bqkv_b[D:2 * D] = bqkv_b[D:2 * D][perm]
    wqkv_b = np.ascontiguousarray(wqkv_b)
    bqkv_b = np.ascontiguousarray(bqkv_b.reshape(48, 128).T)

    pqT = np.ascontiguousarray(np.asarray(inputs['L0_attn_b_pool_q'], dtype=f32).T[perm, :])

    wlmqkv = np.ascontiguousarray(inputs['L0_attn_b_w_lm_qkv'], dtype=f32)
    blmqkv = np.ascontiguousarray(np.asarray(inputs['L0_attn_b_b_lm_qkv'], dtype=f32).reshape(48, 128).T)
    wlmout = np.ascontiguousarray(inputs['L0_attn_b_w_lm_out'], dtype=f32)
    blmout = np.ascontiguousarray(np.asarray(inputs['L0_attn_b_b_lm_out'], dtype=f32).reshape(16, 128).T)
    wl2tq = np.ascontiguousarray(np.asarray(inputs['L0_attn_b_w_l2t_q'], dtype=f32)[:, perm])
    bl2tq = np.ascontiguousarray(np.asarray(inputs['L0_attn_b_b_l2t_q'], dtype=f32)[perm].reshape(16, 128).T)
    wl2tout = np.ascontiguousarray(inputs['L0_attn_b_w_l2t_out'], dtype=f32)
    bl2tout = np.ascontiguousarray(np.asarray(inputs['L0_attn_b_b_l2t_out'], dtype=f32).reshape(16, 128).T)
    wlmkv = np.array(inputs['L0_attn_b_w_lm_kv'], dtype=f32)
    blmkv = np.array(inputs['L0_attn_b_b_lm_kv'], dtype=f32)
    wlmkv[:, 0:D] = wlmkv[:, 0:D][:, perm]
    blmkv[0:D] = blmkv[0:D][perm]
    wlmkv = np.ascontiguousarray(wlmkv)
    blmkv = np.ascontiguousarray(blmkv.reshape(32, 128).T)
    wout = np.ascontiguousarray(inputs['L0_attn_b_w_out'], dtype=f32)
    bout = np.ascontiguousarray(np.asarray(inputs['L0_attn_b_b_out'], dtype=f32).reshape(16, 128).T)
    wgate = np.ascontiguousarray(inputs['L0_attn_b_w_gate'], dtype=f32)
    bgate_v = np.asarray(inputs['L0_attn_b_b_gate'], dtype=f32)
    bgate = np.zeros((128, 1), dtype=f32)
    bgate[96:128, 0] = bgate_v

    gwT = np.ascontiguousarray(np.asarray(inputs['L0_ffn_gate_w'], dtype=f32).T)
    uwT = np.ascontiguousarray(np.asarray(inputs['L0_ffn_up_w'], dtype=f32).T)
    dwT = np.ascontiguousarray(np.asarray(inputs['L0_ffn_down_w'], dtype=f32).T)
    gsv = np.ascontiguousarray(np.asarray(inputs['L0_ffn_gate_s'], dtype=f32).reshape(44, 128).T)
    usv = np.ascontiguousarray(np.asarray(inputs['L0_ffn_up_s'], dtype=f32).reshape(44, 128).T)
    dsv = np.ascontiguousarray(np.asarray(inputs['L0_ffn_down_s'], dtype=f32).reshape(16, 128).T)

    ident = np.eye(128, dtype=f32)
    tri = (np.arange(128)[:, None] <= np.arange(128)[None, :]).astype(f32)
    si = np.arange(128)
    blktri = (((si[:, None] // 64) == (si[None, :] // 64)) &
              ((si[:, None] % 64) <= (si[None, :] % 64))).astype(f32)
    l2tm = np.ascontiguousarray(
        np.tile(((si[:, None] // 64) == (np.arange(8)[None, :] // 4)).astype(f32), (1, QT)))
    inv = 1.0 / 10000.0 ** (np.arange(0, 64, 2) / 64.0)

    x_full = np.asarray(inputs['L0_attn_a_x'], dtype=f32)

    in_maps = []
    for core in range(8):
        s = core // 2
        par = core % 2
        own_tiles = list(range(par, 16, 2)) if PHASE2 else list(range(16))
        own_tok = np.concatenate([np.arange(128 * t, 128 * t + 128) for t in own_tiles])
        own_blocks = [t for tt in own_tiles for t in (2 * tt, 2 * tt + 1)]

        xT = np.ascontiguousarray(x_full[s][own_tok].T)
        ang = own_tok[None, :].astype(np.float64) * inv[:, None]
        cos128 = np.ascontiguousarray(np.tile(np.cos(ang), (4, 1)).astype(f32))
        sin128 = np.ascontiguousarray(np.tile(np.sin(ang), (4, 1)).astype(f32))

        if PHASE2:
            ev = [t for tt in range(0, 16, 2) for t in (2 * tt, 2 * tt + 1)]
            od = [t for tt in range(1, 16, 2) for t in (2 * tt, 2 * tt + 1)]
            lm_blk = np.repeat(np.array(ev + od), M4)
        else:
            lm_blk = np.repeat(np.arange(32), M4)
        own_lm_blk = np.repeat(np.array(own_blocks), M4)
        lmask = (lm_blk[:, None] <= own_lm_blk[None, :]).astype(f32)
        gm = np.zeros((128, QT * 128), dtype=f32)
        for ii, tt in enumerate(own_tiles):
            q_blk = np.repeat(np.array([2 * tt, 2 * tt + 1]), 64)
            gm[:, ii * 128:(ii + 1) * 128] = (lm_blk[:, None] < q_blk[None, :]).astype(f32)

        pqa = np.zeros((64, 128), dtype=f32)
        for h in range(H2):
            pqa[0:32, 4 * h:4 * h + 4] = pqT[32 * h:32 * h + 32, :]
            pqa[32:64, 4 * h:4 * h + 4] = pqT[1024 + 32 * h:1024 + 32 * h + 32, :]

        m = {
            'x0': xT,
            'wqkv_a': wqkv_a, 'bqkv_a': bqkv_a, 'wo_a': wo_a, 'bo_a': bo_a,
            'wqkv_b': wqkv_b, 'bqkv_b': bqkv_b, 'pqT': pqT,
            'wlmqkv': wlmqkv, 'blmqkv': blmqkv, 'wlmout': wlmout, 'blmout': blmout,
            'wl2tq': wl2tq, 'bl2tq': bl2tq, 'wl2tout': wl2tout, 'bl2tout': bl2tout,
            'wlmkv': wlmkv, 'blmkv': blmkv, 'wout': wout, 'bout': bout,
            'wgate': wgate, 'bgate': bgate,
            'gwT': gwT, 'uwT': uwT, 'dwT': dwT, 'gs': gsv, 'us': usv, 'ds': dsv,
            'c_ident': ident, 'c_tri': tri, 'c_blktri': blktri, 'c_l2tm': l2tm,
            'c_lmask': np.ascontiguousarray(lmask), 'c_gmask': np.ascontiguousarray(gm),
            'c_cos': cos128, 'c_sin': sin128, 'c_pqa': np.ascontiguousarray(pqa),
        }
        in_maps.append(m)
    return in_maps


def build_program():
    nc = bass.Bass("TRN2", target_bir_lowering=False)

    def din(name, shape, dt=F32):
        return nc.dram_tensor(name, shape, dt, kind="ExternalInput").ap()

    x0 = din('x0', [D, TOK])
    wqkv_a = din('wqkv_a', [D, 3 * D]); bqkv_a = din('bqkv_a', [128, 48])
    wo_a = din('wo_a', [D, D]); bo_a = din('bo_a', [128, 16])
    wqkv_b = din('wqkv_b', [D, 3 * D]); bqkv_b = din('bqkv_b', [128, 48])
    pqT = din('pqT', [D, M4])
    wlmqkv = din('wlmqkv', [D, 3 * D]); blmqkv = din('blmqkv', [128, 48])
    wlmout = din('wlmout', [D, D]); blmout = din('blmout', [128, 16])
    wl2tq = din('wl2tq', [D, D]); bl2tq = din('bl2tq', [128, 16])
    wl2tout = din('wl2tout', [D, D]); bl2tout = din('bl2tout', [128, 16])
    wlmkv = din('wlmkv', [D, 2 * D]); blmkv = din('blmkv', [128, 32])
    wout = din('wout', [D, D]); bout = din('bout', [128, 16])
    wgate = din('wgate', [D, H2]); bgate = din('bgate', [128, 1])
    gwT = din('gwT', [D, I5])
    uwT = din('uwT', [D, I5])
    dwT = din('dwT', [I5, D])
    gs = din('gs', [128, 44]); us = din('us', [128, 44]); ds = din('ds', [128, 16])
    c_ident = din('c_ident', [128, 128]); c_tri = din('c_tri', [128, 128])
    c_blktri = din('c_blktri', [128, 128]); c_l2tm = din('c_l2tm', [128, 8 * QT])
    c_lmask = din('c_lmask', [128, OWNLM]); c_gmask = din('c_gmask', [128, QT * 128])
    c_cos = din('c_cos', [128, TOK]); c_sin = din('c_sin', [128, TOK])
    c_pqa = din('c_pqa', [64, 128])
    dbg = (nc.dram_tensor('dbg', [128, 4096], F32, kind="ExternalOutput").ap()
           if DEV_DBG else None)

    y = nc.dram_tensor('y', [D, TOK], F32, kind="ExternalOutput").ap()

    xres = nc.dram_tensor('xres', [D, TOK], F32).ap()
    scr_ao = nc.dram_tensor('scr_ao', [D, TOK], F32).ap()
    scr_q = nc.dram_tensor('scr_q', [D, TOK], F32).ap()
    ag1_in = nc.dram_tensor('ag1_in', [2 * D, TOK], F32).ap()
    kvb = nc.dram_tensor('kvb', [2 * D, TOK], F32).ap()
    ag3_in = nc.dram_tensor('ag3_in', [2 * D, OWNLM], F32).ap()
    ag4_in = nc.dram_tensor('ag4_in', [2 * D, OWNLM], F32).ap()

    def d3(ap_, C):
        return ap_.rearrange("(c p) t -> p c t", p=128)

    def sb3(t_, C, w):
        return t_.rearrange("p (c t) -> p c t", c=C)

    import contextlib
    with contextlib.ExitStack() as _st:
        A1 = _st.enter_context(nc.sbuf_tensor('A1', [128, 8192], F32))
        A2 = _st.enter_context(nc.sbuf_tensor('A2', [128, 8192], F32))
        A3 = _st.enter_context(nc.sbuf_tensor('A3', [128, 3072], F32))
        A4 = _st.enter_context(nc.sbuf_tensor('A4', [128, 4096], F32))
        A5 = _st.enter_context(nc.sbuf_tensor('A5', [128, 8192], F32))
        A6 = _st.enter_context(nc.sbuf_tensor('A6', [128, 8192], F32))
        RW = _st.enter_context(nc.sbuf_tensor('RW', [128, 8192], mybir.dt.bfloat16))
        SM = _st.enter_context(nc.sbuf_tensor('SM', [128, 3072], F32))
        pqa_sb = _st.enter_context(nc.sbuf_tensor('pqa_sb', [64, 128], F32))
        gate_t_sb = _st.enter_context(nc.sbuf_tensor('gate_t_sb', [128, 32 * QT], F32))
        ident_sb = _st.enter_context(nc.sbuf_tensor('ident_sb', [128, 128], F32))
        tri_sb = _st.enter_context(nc.sbuf_tensor('tri_sb', [128, 128], F32))
        blktri_sb = _st.enter_context(nc.sbuf_tensor('blktri_sb', [128, 128], F32))
        l2tm_sb = _st.enter_context(nc.sbuf_tensor('l2tm_sb', [128, 8 * QT], F32))
        lmask_sb = _st.enter_context(nc.sbuf_tensor('lmask_sb', [128, OWNLM], F32))
        ones_c = _st.enter_context(nc.sbuf_tensor('ones_c', [128, 1], F32))
        ones_r = _st.enter_context(nc.sbuf_tensor('ones_r', [128, 128], F32))
        bias_sb = _st.enter_context(nc.sbuf_tensor('bias_sb', [128, 48], F32))
        bias2_sb = _st.enter_context(nc.sbuf_tensor('bias2_sb', [128, 1], F32))
        gsc_sb = _st.enter_context(nc.sbuf_tensor('gsc_sb', [128, 44], F32))
        usc_sb = _st.enter_context(nc.sbuf_tensor('usc_sb', [128, 44], F32))
        dsc_sb = _st.enter_context(nc.sbuf_tensor('dsc_sb', [128, 16], F32))
        PS0 = _st.enter_context(nc.psum_tensor('PS0', [128, 512], F32))
        PS1 = _st.enter_context(nc.psum_tensor('PS1', [128, 512], F32))
        PS2 = _st.enter_context(nc.psum_tensor('PS2', [128, 512], F32))
        PS3 = _st.enter_context(nc.psum_tensor('PS3', [128, 512], F32))
        PS4 = _st.enter_context(nc.psum_tensor('PS4', [128, 512], F32))
        PS5 = _st.enter_context(nc.psum_tensor('PS5', [128, 512], F32))
        PS6 = _st.enter_context(nc.psum_tensor('PS6', [128, 512], F32))
        PS7 = _st.enter_context(nc.psum_tensor('PS7', [128, 512], F32))
        _sq = _st.enter_context(nc.semaphore('sq'))
        _gq = _st.enter_context(nc.semaphore('gq'))
        _w0 = _st.enter_context(nc.semaphore('w0'))
        _w1 = _st.enter_context(nc.semaphore('w1'))
        _g0 = _st.enter_context(nc.semaphore('g0'))
        _g1 = _st.enter_context(nc.semaphore('g1'))
        _g2 = _st.enter_context(nc.semaphore('g2'))
        _g3 = _st.enter_context(nc.semaphore('g3'))
        _g4 = _st.enter_context(nc.semaphore('g4'))
        _w2 = _st.enter_context(nc.semaphore('w2'))
        _w3 = _st.enter_context(nc.semaphore('w3'))
        _f0 = _st.enter_context(nc.semaphore('f0'))
        _f1 = _st.enter_context(nc.semaphore('f1'))
        _f2 = _st.enter_context(nc.semaphore('f2'))
        _f3 = _st.enter_context(nc.semaphore('f3'))
        _tp = _st.enter_context(nc.semaphore('tp'))
        _vp = _st.enter_context(nc.semaphore('vp'))
        _sp = _st.enter_context(nc.semaphore('sp'))
        _cc = _st.enter_context(nc.semaphore('cc'))
        _bar = _st.enter_context(nc.semaphore('bar'))
        blk = _st.enter_context(nc.Block())
        b = Bld(nc, blk, {'sq': _sq, 'gq': _gq, 'tp': _tp, 'vp': _vp,
                          'sp': _sp, 'cc': _cc, 'bar': _bar,
                          'w0': _w0, 'w1': _w1, 'w2': _w2, 'w3': _w3,
                          'f0': _f0, 'f1': _f1, 'f2': _f2, 'f3': _f3,
                          'g0': _g0, 'g1': _g1, 'g2': _g2, 'g3': _g3, 'g4': _g4})
        WS2 = [_w2, _w3]
        wsn2 = ['w2', 'w3']
        FS = [_f0, _f1]
        fsn = ['f0', 'f1']
        FS2 = [_f2, _f3]
        fsn2 = ['f2', 'f3']
        b.scratch_ps = PS7
        b.scratch_sm = SM
        b.ones_c = ones_c
        b.ident = ident_sb
        WS = [_w0, _w1]
        GS = [_g0, _g1, _g2, _g3]
        wsn = ['w0', 'w1']
        gsn = ['g0', 'g1', 'g2', 'g3']
        st_m = SM[0:1, 0:TOK]
        tmp_a = SM[0:1, 2048:2048 + TC]
        tmp_b = SM[0:1, 2560:2560 + TC]
        st_r = SM[32:33, 0:TOK]
        cvec = SM[32:33, 2048:3072]
        rden = SM[64:65, 0:TOK]
        gate_sm = SM[96:128, 0:TOK]

        def prolog_q(q):
            b.dma(q, ident_sb[:], c_ident, 'sq')
            b.dma(q, tri_sb[:], c_tri, 'sq')
            b.dma(q, blktri_sb[:], c_blktri, 'sq')
            b.dma(q, l2tm_sb[:], c_l2tm, 'sq')
            b.dma(q, lmask_sb[:], c_lmask, 'sq')
        blk.sync(prolog_q)
        b.prolog_sq = b.c['sq']

        def prolog_v(v):
            v.memset(ones_c[:], 1.0)
            v.memset(ones_r[:], 1.0)
        blk.vector(prolog_v)
        b.barrier()

        def emit_ln_stats(xbuf, tc0):
            sp0, tp0, vp0 = b.c['sp'], b.c['tp'], b.c['vp']

            def s_body(s):
                for dc in range(DC):
                    if dc >= 2:
                        s.wait_ge(_tp, tp0 + dc - 1)
                    b.inc(s.activation(out=A3[:, (dc % 2) * TC:(dc % 2 + 1) * TC],
                                       in_=xbuf[:, dc * TC:(dc + 1) * TC],
                                       func=AF.Square), 'sp', 1)
            blk.scalar(s_body)

            def t_body(t):
                for dc in range(DC):
                    t.matmul(PS6[0:1, 0:TC], ones_c[:], xbuf[:, dc * TC:(dc + 1) * TC],
                             start=(dc == 0), stop=(dc == DC - 1))
                    t.wait_ge(_sp, sp0 + dc + 1)
                    b.inc(t.matmul(PS7[0:1, 0:TC], ones_c[:],
                                   A3[:, (dc % 2) * TC:(dc % 2 + 1) * TC],
                                   start=(dc == 0), stop=(dc == DC - 1)), 'tp', 1)
            blk.tensor(t_body)
            b.chk('tp', tp0 + DC)
            b.chk('sp', sp0 + DC)
            emit_stats_finish(tc0)

        def emit_stats_finish(tc0):
            tdone = b.c['tp']

            def v1(v):
                v.wait_ge(_tp, tdone)
                v.tensor_scalar_mul(out=st_m[:, tc0:tc0 + TC], in0=PS6[0:1, 0:TC], scalar1=1.0 / D)
                m1 = b.inc(v.tensor_scalar(out=tmp_a, in0=PS7[0:1, 0:TC], scalar1=1.0 / D,
                                           scalar2=EPS_LN, op0=AT.mult, op1=AT.add), 'vp', 1)
                v.wait_ge(_vp, m1)
                m2 = b.inc(v.tensor_mul(out=tmp_b, in0=st_m[:, tc0:tc0 + TC],
                                        in1=st_m[:, tc0:tc0 + TC]), 'vp', 1)
                v.wait_ge(_vp, m2)
                b.inc(v.tensor_sub(out=tmp_a, in0=tmp_a, in1=tmp_b), 'vp', 1)
            blk.vector(v1)
            vdone = b.c['vp']

            def s1(s):
                s.wait_ge(_vp, vdone)
                b.inc(s.activation(out=tmp_b, in_=tmp_a, func=AF.Sqrt), 'sp', 1)
            blk.scalar(s1)
            sdone = b.c['sp']

            def v2(v):
                v.wait_ge(_sp, sdone)
                b.inc(v.reciprocal(out=st_r[:, tc0:tc0 + TC], in_=tmp_b), 'vp', 1)
            blk.vector(v2)

        def emit_ln_apply(xbuf, xnbuf, tc0):
            vp0 = b.c['vp']

            def t1(t):
                t.wait_ge(_vp, vp0)
                t.matmul(PS4[:, 0:TC], ones_r[0:1, :], st_m[:, tc0:tc0 + TC], start=True, stop=True)
                b.inc(t.matmul(PS5[:, 0:TC], ones_r[32:33, :], st_r[:, tc0:tc0 + TC],
                               start=True, stop=True), 'tp', 1)
            blk.tensor(t1)
            tdone = b.c['tp']

            def v1(v):
                v.wait_ge(_tp, tdone)
                for dc in range(DC):
                    i = v.tensor_sub(out=xnbuf[:, dc * TC:(dc + 1) * TC],
                                     in0=xbuf[:, dc * TC:(dc + 1) * TC], in1=PS4[:, 0:TC])
                    if dc == DC - 1:
                        m1 = b.inc(i, 'vp', 1)
                        v.wait_ge(_vp, m1)
                for dc in range(DC):
                    i = v.tensor_mul(out=xnbuf[:, dc * TC:(dc + 1) * TC],
                                     in0=xnbuf[:, dc * TC:(dc + 1) * TC], in1=PS5[:, 0:TC])
                b.inc(i, 'vp', 1)
            blk.vector(v1)

        def emit_qkv_stage(W_ap, Bias_ap, nfc, dst, xsrc, ln_here, rope=False,
                           gate=False):
            def q_bias(q):
                b.dma(q, bias_sb[:, 0:nfc], Bias_ap, 'sq')
                if gate:
                    b.dma(q, bias2_sb[:, 0:1], bgate, 'sq')
            blk.sync(q_bias)

            rope_pairs = []
            rope_mem = {}
            if rope:
                for c in range(8):
                    rope_pairs.append((c, c + 8))
                for c in range(8):
                    rope_pairs.append((16 + c, 24 + c))
                for (fe, fo) in rope_pairs:
                    rope_mem[fe] = (fe, fo)
                    rope_mem[fo] = (fe, fo)
                order = [f for pr in rope_pairs for f in pr] + list(range(32, nfc))
            else:
                order = list(range(nfc))
            if gate:
                order = order + [nfc]

            total = nfc + (1 if gate else 0)
            for tc in range(NTC):
                tc0 = tc * TC

                def q_x(q):
                    b.dma(q, sb3(A1[:], DC, TC)[:, :, 0:TC], d3(xsrc, DC)[:, :, tc0:tc0 + TC], 'sq')
                blk.sync(q_x)
                xld = b.c['sq']

                def tw(t):
                    t.wait_ge(_sq, xld)
                blk.tensor(tw)

                def sw(s):
                    s.wait_ge(_sq, xld)
                blk.scalar(sw)
                if ln_here:
                    emit_ln_stats(A1, tc0)
                emit_ln_apply(A1, A2, tc0)
                xn_ready = b.c['vp']

                tp0 = b.c['tp']
                sp0 = b.c['sp']
                vp0 = b.c['vp']
                wb0 = [b.c['w0'], b.c['w1']]
                gb0 = [b.c['g0'], b.c['g1'], b.c['g2'], b.c['g3']]
                g40 = b.c['g4']

                def pair_idx(fc):
                    pr = rope_mem[fc]
                    return rope_pairs.index(pr)

                def w_mark(idx):
                    return wb0[idx % 2] + 16 * (idx // 2 + 1)

                pos = {fc: idx for idx, fc in enumerate(order)}

                slot_use = {0: [], 1: [], 2: [], 3: []}
                for idx, fc in enumerate(order):
                    if gate and fc == nfc:
                        continue
                    if rope and fc in rope_mem:
                        continue
                    slot_use[idx % 4].append(idx)

                def g_mark(idx):
                    sl = idx % 4
                    k = slot_use[sl].index(idx) + 1
                    return gb0[sl] + 16 * k

                def g4_mark(pi):
                    return g40 + 32 * (pi + 1)

                def q_w(q):
                    for idx, fc in enumerate(order):
                        bb = idx % 2
                        if idx >= 2:
                            q.wait_ge(_tp, tp0 + idx - 1)
                        if gate and fc == nfc:
                            b.dma(q, sb3(A4[:, bb * 2048:bb * 2048 + DC * 32], DC, 32),
                                  d3(wgate, DC), wsn[bb])
                        else:
                            b.dma(q, sb3(A4[:, bb * 2048:(bb + 1) * 2048], DC, 128),
                                  d3(W_ap, DC)[:, :, fc * 128:(fc + 1) * 128], wsn[bb])
                blk.sync(q_w)

                def t_mm(t):
                    t.wait_ge(_vp, xn_ready)
                    for idx, fc in enumerate(order):
                        bb = idx % 2
                        is_gate = gate and fc == nfc
                        wid = 32 if is_gate else 128
                        ps = PS0 if bb == 0 else PS1
                        t.wait_ge(WS[bb], w_mark(idx))
                        if idx >= 2:
                            t.wait_ge(_sp, sp0 + idx - 1)
                        for dc in range(DC):
                            i = t.matmul(ps[0:wid, 0:TC],
                                         A4[:, bb * 2048 + dc * wid:bb * 2048 + (dc + 1) * wid],
                                         A2[:, dc * TC:(dc + 1) * TC],
                                         start=(dc == 0), stop=(dc == DC - 1))
                        b.inc(i, 'tp', 1)
                blk.tensor(t_mm)
                b.chk('tp', tp0 + total)

                def s_evac(s):
                    for idx, fc in enumerate(order):
                        ps = PS0 if idx % 2 == 0 else PS1
                        slot = idx % 4
                        s.wait_ge(_tp, tp0 + idx + 1)
                        if idx >= 4:
                            pidx = idx - 4
                            pfc = order[pidx]
                            if rope and pfc in rope_mem:
                                s.wait_ge(_vp, vp0 + 2 * (pair_idx(pfc) + 1))
                            elif not (gate and pfc == nfc):
                                s.wait_ge(GS[pidx % 4], g_mark(pidx))
                        if gate and fc == nfc:
                            i = s.activation(out=gate_sm[:, tc0:tc0 + TC], in_=ps[0:32, 0:TC],
                                             func=AF.Sigmoid, bias=bias2_sb[96:128, 0:1])
                        else:
                            i = s.activation(out=A3[:, slot * TC:(slot + 1) * TC],
                                             in_=ps[0:128, 0:TC],
                                             func=AF.Identity, bias=bias_sb[:, fc:fc + 1])
                        b.inc(i, 'sp', 1)
                blk.scalar(s_evac)
                b.chk('sp', sp0 + total)

                if rope:
                    def v_rope(v):
                        for pi, (fe, fo) in enumerate(rope_pairs):
                            v.wait_ge(_sp, sp0 + max(pos[fe], pos[fo]) + 1)
                            if pi >= 1:
                                v.wait_ge(_g4, g4_mark(pi - 1))
                            xe = A3[:, (pos[fe] % 4) * TC:(pos[fe] % 4 + 1) * TC]
                            xo = A3[:, (pos[fo] % 4) * TC:(pos[fo] % 4 + 1) * TC]
                            oe = A5[:, 0:TC]
                            oo = A5[:, TC:2 * TC]
                            t1_ = A5[:, 2 * TC:3 * TC]
                            t2_ = A5[:, 3 * TC:4 * TC]
                            cosT = A6[:, tc0:tc0 + TC]
                            sinT = A6[:, 4096 + tc0:4096 + tc0 + TC]
                            v.tensor_mul(out=oe, in0=xe, in1=cosT)
                            v.tensor_mul(out=t1_, in0=xo, in1=sinT)
                            v.tensor_mul(out=oo, in0=xe, in1=sinT)
                            m = b.inc(v.tensor_mul(out=t2_, in0=xo, in1=cosT), 'vp', 1)
                            v.wait_ge(_vp, m)
                            v.tensor_sub(out=oe, in0=oe, in1=t1_)
                            b.inc(v.tensor_add(out=oo, in0=oo, in1=t2_), 'vp', 1)
                    blk.vector(v_rope)
                    b.chk('vp', vp0 + 2 * len(rope_pairs))

                def g_st(g):
                    for idx, fc in enumerate(order):
                        if gate and fc == nfc:
                            continue
                        if rope and fc in rope_mem:
                            pr = rope_mem[fc]
                            if fc != pr[1]:
                                continue
                            g.wait_ge(_vp, vp0 + 2 * (pair_idx(fc) + 1))
                            da_e, r0_e = dst(pr[0])
                            da_o, r0_o = dst(pr[1])
                            b.dma(g, d3(da_e, 0)[:, r0_e // 128, tc0:tc0 + TC], A5[:, 0:TC], 'g4')
                            b.dma(g, d3(da_o, 0)[:, r0_o // 128, tc0:tc0 + TC], A5[:, TC:2 * TC], 'g4')
                        else:
                            slot = idx % 4
                            g.wait_ge(_sp, sp0 + idx + 1)
                            da, r0 = dst(fc)
                            b.dma(g, d3(da, 0)[:, r0 // 128, tc0:tc0 + TC],
                                  A3[:, slot * TC:(slot + 1) * TC], gsn[slot])
                blk.gpsimd(g_st)
                b.barrier()

        def emit_attn_a():
            for h in range(NH):
                slot = A1 if h % 2 == 0 else A5

                def q_head(q, h=h, slot=slot):
                    b.dma(q, slot[:, 0:TOK], ag1_in[h * 128:(h + 1) * 128, :], 'sq')
                    b.dma(q, slot[:, TOK:2 * TOK], ag1_in[D + h * 128:D + (h + 1) * 128, :], 'sq')
                    b.dma(q, slot[:, 2 * TOK:3 * TOK], scr_q[h * 128:(h + 1) * 128, :], 'sq')
                blk.sync(q_head)
                ld_mark = b.c['sq']

                tp0 = b.c['tp']
                vp0 = b.c['vp']

                def t_vt(t, slot=slot, ld_mark=ld_mark, tp0=tp0, vp0=vp0):
                    t.wait_ge(_sq, ld_mark)
                    for St in range(QT):
                        ps = PS2 if St % 2 == 0 else PS3
                        if St >= 2:
                            t.wait_ge(_vp, vp0 + St - 1)
                        b.inc(t.transpose(ps[0:128, 0:128],
                                          slot[:, TOK + St * 128:TOK + (St + 1) * 128],
                                          ident_sb[:]), 'tp', 1)
                blk.tensor(t_vt)

                def v_vt(v, slot=slot, tp0=tp0):
                    for St in range(QT):
                        ps = PS2 if St % 2 == 0 else PS3
                        v.wait_ge(_tp, tp0 + St + 1)
                        b.inc(v.tensor_copy(out=slot[:, 3 * TOK + St * 128:3 * TOK + (St + 1) * 128],
                                            in_=ps[0:128, 0:128]), 'vp', 1)
                blk.vector(v_vt)
                b.chk('tp', tp0 + QT)
                b.chk('vp', vp0 + QT)
                vt_done = b.c['vp']

                T0 = b.c['tp']
                S0 = b.c['sp']
                V0 = b.c['vp']

                def t_off(i):
                    return T0 + sum(2 * (j + 1) + 1 for j in range(i))

                def sc_mark(i, St):
                    return t_off(i) + 2 * St + 1

                def od_mark(i, St):
                    return t_off(i) + 2 * St + 2

                def bc_mark(i):
                    return t_off(i) + 2 * (i + 1) + 1

                def s_off(i):
                    return S0 + sum(j + 2 for j in range(i))

                def exp_mark(i, St):
                    return s_off(i) + St + 1

                def cpy_mark(i):
                    return s_off(i) + i + 2

                def mask_mark(i):
                    return V0 + 3 * i + 1

                def rden_mark(i):
                    return V0 + 3 * i + 2

                def outev_mark(i):
                    return V0 + 3 * i + 3

                def t_all(t, h=h, slot=slot, vt_done=vt_done):
                    t.wait_ge(_vp, vt_done)
                    for i in range(QT):
                        nS = i + 1
                        out_ps = PS2 if i % 2 == 0 else PS3
                        den_ps = PS6 if i % 2 == 0 else PS7
                        if i >= 2:
                            t.wait_ge(_vp, outev_mark(i - 2))
                        for St in range(nS):
                            ps = PS0 if St % 2 == 0 else PS1
                            if St >= 2:
                                t.wait_ge(_sp, exp_mark(i, St - 2))
                            b.inc(t.matmul(ps[0:128, 0:128],
                                           slot[:, St * 128:(St + 1) * 128],
                                           slot[:, 2 * TOK + i * 128:2 * TOK + (i + 1) * 128],
                                           start=True, stop=True), 'tp', 1)
                            if St == nS - 1:
                                t.wait_ge(_vp, mask_mark(i))
                            else:
                                t.wait_ge(_sp, exp_mark(i, St))
                            e_ap = A2[:, (St % 4) * 128:(St % 4 + 1) * 128]
                            t.matmul(out_ps[0:128, 0:128],
                                     slot[:, 3 * TOK + St * 128:3 * TOK + (St + 1) * 128],
                                     e_ap, start=(St == 0), stop=(St == nS - 1))
                            b.inc(t.matmul(den_ps[0:1, 0:128], ones_c[:], e_ap,
                                           start=(St == 0), stop=(St == nS - 1)), 'tp', 1)
                        t.wait_ge(_vp, rden_mark(i))
                        if i >= 1:
                            t.wait_ge(_sp, cpy_mark(i - 1))
                        b.inc(t.matmul(PS4[:, 0:128], ones_r[64:65, :],
                                       rden[:, i * 128:(i + 1) * 128], start=True, stop=True),
                              'tp', 1)
                blk.tensor(t_all)

                def s_all(s):
                    for i in range(QT):
                        nS = i + 1
                        for St in range(nS):
                            ps = PS0 if St % 2 == 0 else PS1
                            s.wait_ge(_tp, sc_mark(i, St))
                            if St >= 4:
                                s.wait_ge(_tp, od_mark(i, St - 4))
                            elif i >= 1:
                                s.wait_ge(_tp, od_mark(i - 1, i - 1))
                            b.inc(s.activation(out=A2[:, (St % 4) * 128:(St % 4 + 1) * 128],
                                               in_=ps[0:128, 0:128], func=AF.Exp, scale=SC_A),
                                  'sp', 1)
                        s.wait_ge(_tp, bc_mark(i))
                        if i >= 2:
                            s.wait_ge(_vp, outev_mark(i - 2))
                        b.inc(s.activation(out=A3[:, (i % 2) * 128:(i % 2 + 1) * 128],
                                           in_=PS4[:, 0:128], func=AF.Copy), 'sp', 1)
                blk.scalar(s_all)

                def v_all(v, h=h):
                    for i in range(QT):
                        nS = i + 1
                        St = nS - 1
                        v.wait_ge(_sp, exp_mark(i, St))
                        b.inc(v.tensor_mul(out=A2[:, (St % 4) * 128:(St % 4 + 1) * 128],
                                           in0=A2[:, (St % 4) * 128:(St % 4 + 1) * 128],
                                           in1=tri_sb[:]), 'vp', 1)
                        v.wait_ge(_tp, od_mark(i, nS - 1))
                        den_ps = PS6 if i % 2 == 0 else PS7
                        b.inc(v.reciprocal(out=rden[:, i * 128:(i + 1) * 128],
                                           in_=den_ps[0:1, 0:128]), 'vp', 1)
                        v.wait_ge(_sp, cpy_mark(i))
                        out_ps = PS2 if i % 2 == 0 else PS3
                        b.inc(v.tensor_mul(out=A6[:, i * 128:(i + 1) * 128],
                                           in0=out_ps[0:128, 0:128],
                                           in1=A3[:, (i % 2) * 128:(i % 2 + 1) * 128]), 'vp', 1)
                blk.vector(v_all)

                nmm = sum(2 * (j + 1) + 1 for j in range(QT))
                b.chk('tp', T0 + nmm)
                b.chk('sp', S0 + sum(j + 2 for j in range(QT)))
                b.chk('vp', V0 + 3 * QT)

                st_mark = b.c['vp']

                def g_out(g, h=h, st_mark=st_mark):
                    g.wait_ge(_vp, st_mark)
                    b.dma(g, scr_ao[h * 128:(h + 1) * 128, :], A6[:, 0:TOK], 'gq')
                blk.gpsimd(g_out)
                b.barrier()

        def emit_proj_res(src_ap, W_ap, Bias_ap, xold_ap, dst_ap, want_stats):
            def q_bias(q):
                b.dma(q, bias_sb[:, 0:16], Bias_ap, 'sq')
            blk.sync(q_bias)

            for tc in range(NTC):
                tc0 = tc * TC

                def q_in(q):
                    b.dma(q, sb3(A1[:], DC, TC)[:, :, 0:TC], d3(src_ap, DC)[:, :, tc0:tc0 + TC], 'sq')
                    b.dma(q, sb3(A5[:], DC, TC)[:, :, 0:TC], d3(xold_ap, DC)[:, :, tc0:tc0 + TC], 'sq')
                blk.sync(q_in)
                xld = b.c['sq']

                K_ = 2 if want_stats else 1
                tp0 = b.c['tp']
                vp0 = b.c['vp']
                sp0 = b.c['sp']
                wb0 = [b.c['w0'], b.c['w1']]
                gb0 = [b.c['g0'], b.c['g1'], b.c['g2'], b.c['g3']]

                def q_w(q):
                    for fc in range(DC):
                        bb = fc % 2
                        if fc >= 2:
                            q.wait_ge(_tp, tp0 + K_ * (fc - 1))
                        b.dma(q, sb3(A4[:, bb * 2048:(bb + 1) * 2048], DC, 128),
                              d3(W_ap, DC)[:, :, fc * 128:(fc + 1) * 128], wsn[bb])
                blk.sync(q_w)

                def t_mm(t):
                    t.wait_ge(_sq, xld)
                    for fc in range(DC):
                        bb = fc % 2
                        ps = PS0 if bb == 0 else PS1
                        t.wait_ge(WS[bb], wb0[bb] + 16 * (fc // 2 + 1))
                        if fc >= 2:
                            t.wait_ge(_vp, vp0 + fc - 1)
                        for dc in range(DC):
                            i = t.matmul(ps[0:128, 0:TC],
                                         A4[:, bb * 2048 + dc * 128:bb * 2048 + (dc + 1) * 128],
                                         A1[:, dc * TC:(dc + 1) * TC],
                                         start=(dc == 0), stop=(dc == DC - 1))
                        b.inc(i, 'tp', 1)
                        if want_stats:
                            t.wait_ge(_vp, vp0 + fc + 1)
                            t.matmul(PS6[0:1, 0:TC], ones_c[:],
                                     A2[:, (fc % 4) * TC:(fc % 4 + 1) * TC],
                                     start=(fc == 0), stop=(fc == DC - 1))
                            t.wait_ge(_sp, sp0 + fc + 1)
                            b.inc(t.matmul(PS7[0:1, 0:TC], ones_c[:],
                                           A3[:, (fc % 2) * TC:(fc % 2 + 1) * TC],
                                           start=(fc == 0), stop=(fc == DC - 1)), 'tp', 1)
                blk.tensor(t_mm)

                def v_evac(v):
                    for fc in range(DC):
                        ps = PS0 if fc % 2 == 0 else PS1
                        v.wait_ge(_tp, tp0 + K_ * fc + 1)
                        if fc >= 4:
                            v.wait_ge(GS[fc % 4], gb0[fc % 4] + 16 * (fc // 4))
                            if want_stats:
                                v.wait_ge(_sp, sp0 + fc - 3)
                        b.inc(v.scalar_tensor_tensor(
                            out=A2[:, (fc % 4) * TC:(fc % 4 + 1) * TC],
                            in0=ps[0:128, 0:TC], scalar=bias_sb[:, fc:fc + 1],
                            in1=A5[:, fc * TC:(fc + 1) * TC], op0=AT.add, op1=AT.add), 'vp', 1)
                blk.vector(v_evac)

                if want_stats:
                    def s_sq(s):
                        for fc in range(DC):
                            s.wait_ge(_vp, vp0 + fc + 1)
                            if fc >= 2:
                                s.wait_ge(_tp, tp0 + 2 * fc)
                            b.inc(s.activation(out=A3[:, (fc % 2) * TC:(fc % 2 + 1) * TC],
                                               in_=A2[:, (fc % 4) * TC:(fc % 4 + 1) * TC],
                                               func=AF.Square), 'sp', 1)
                    blk.scalar(s_sq)

                def g_st(g):
                    for fc in range(DC):
                        g.wait_ge(_vp, vp0 + fc + 1)
                        b.dma(g, d3(dst_ap, DC)[:, fc, tc0:tc0 + TC],
                              A2[:, (fc % 4) * TC:(fc % 4 + 1) * TC], gsn[fc % 4])
                blk.gpsimd(g_st)

                b.chk('tp', tp0 + K_ * DC)
                b.chk('vp', vp0 + DC)
                if want_stats:
                    b.chk('sp', sp0 + DC)
                    emit_stats_finish(tc0)
                b.barrier()

        def emit_ffn(xold_ap, dst_ap, want_stats):
            GRP = 4
            NG = ICH // GRP
            BF16 = mybir.dt.bfloat16
            xh_bf = A1[:, 0:4096].bitcast(BF16)
            xl_bf = A1[:, 4096:8192].bitcast(BF16)
            hh_bf = A5[:, 0:1024].bitcast(BF16)
            hl_bf = A5[:, 1024:2048].bitcast(BF16)
            dw_bf = A4[:, 0:512].bitcast(BF16)

            def q_sc(q):
                b.dma(q, gsc_sb[:, 0:44], gs, 'sq')
                b.dma(q, usc_sb[:, 0:44], us, 'sq')
                b.dma(q, dsc_sb[:, 0:16], ds, 'sq')
            blk.sync(q_sc)

            for tc in range(NTC):
                tc0 = tc * TC

                def q_x(q):
                    b.dma(q, sb3(A6[:], DC, TC)[:, :, 0:TC], d3(xold_ap, DC)[:, :, tc0:tc0 + TC], 'sq')
                blk.sync(q_x)
                xld = b.c['sq']

                def tw(t):
                    t.wait_ge(_sq, xld)
                blk.tensor(tw)
                emit_ln_apply(A6, A2, tc0)
                xn_ready = b.c['vp']

                def v_split(v):
                    v.wait_ge(_vp, xn_ready)
                    for dc in range(DC):
                        m = b.inc(v.tensor_copy(out=xh_bf[:, dc * TC:(dc + 1) * TC],
                                                in_=A2[:, dc * TC:(dc + 1) * TC]), 'vp', 1)
                        v.wait_ge(_vp, m)
                        b.inc(v.tensor_sub(out=xl_bf[:, dc * TC:(dc + 1) * TC],
                                           in0=A2[:, dc * TC:(dc + 1) * TC],
                                           in1=xh_bf[:, dc * TC:(dc + 1) * TC]), 'vp', 1)
                blk.vector(v_split)
                split_done = b.c['vp']

                def xh_r(dc):
                    return xh_bf[:, dc * TC:(dc + 1) * TC]

                def xl_r(dc):
                    return xl_bf[:, dc * TC:(dc + 1) * TC]

                tp0 = b.c['tp']
                sp0 = b.c['sp']
                vp0 = b.c['vp']
                wb0 = [b.c['f0'], b.c['f1']]
                wb20 = [b.c['f2'], b.c['f3']]

                gu_mark = {}
                dn_tp_mark = {}
                tpc = tp0
                for ic in range(ICH):
                    tpc += 1; gu_mark[(ic, 0)] = tpc
                    tpc += 1; gu_mark[(ic, 1)] = tpc
                    if ic % GRP == GRP - 1:
                        grp = ic // GRP
                        for fcp in range(DC):
                            tpc += 1; dn_tp_mark[(grp, fcp)] = tpc
                tp_end = tpc

                hl_mark = {}
                dn_vp_mark = {}
                vpc = vp0
                for ic in range(ICH):
                    vpc += 5; hl_mark[ic] = vpc
                    if ic % GRP == GRP - 1:
                        grp = ic // GRP
                        for fcp in range(DC):
                            vpc += 1; dn_vp_mark[(grp, fcp)] = vpc
                vp_end = vpc

                def silu_mark(ic, which):
                    return sp0 + 2 * ic + which + 1

                def w_mark_gu(ic):
                    return wb0[ic % 2] + 32 * (ic // 2 + 1)

                def w_mark_dn(grp, fcp):
                    k = grp * DC + fcp
                    return wb20[k % 2] + 16 * (k // 2 + 1)

                def q_w(g):
                    for ic in range(ICH):
                        bb = ic % 2
                        if ic >= 2:
                            g.wait_ge(_tp, gu_mark[(ic - 2, 1)])
                        b.dma(g, sb3(RW[:, bb * 2048:(bb + 1) * 2048], DC, 128),
                              d3(gwT, DC)[:, :, ic * 128:(ic + 1) * 128], fsn[bb])
                        b.dma(g, sb3(RW[:, 4096 + bb * 2048:4096 + (bb + 1) * 2048], DC, 128),
                              d3(uwT, DC)[:, :, ic * 128:(ic + 1) * 128], fsn[bb])
                        if ic % GRP == GRP - 1:
                            grp = ic // GRP
                            for fcp in range(DC):
                                k = grp * DC + fcp
                                bb2 = k % 2
                                if k >= 2:
                                    kk = k - 2
                                    g.wait_ge(_tp, dn_tp_mark[(kk // DC, kk % DC)])
                                b.dma(g, sb3(dw_bf[:, bb2 * 512:(bb2 + 1) * 512], GRP, 128),
                                      dwT.rearrange("(g p) t -> p g t", p=128)[
                                          :, 4 * grp:4 * grp + GRP, fcp * 128:(fcp + 1) * 128],
                                      fsn2[bb2])
                blk.gpsimd(q_w)

                def t_mm(t):
                    t.wait_ge(_vp, split_done)
                    for ic in range(ICH):
                        bb = ic % 2
                        t.wait_ge(FS[bb], w_mark_gu(ic))
                        if ic >= 1:
                            t.wait_ge(_sp, silu_mark(ic - 1, 0))
                        for p in range(2):
                            for dc in range(DC):
                                i = t.matmul(PS0[0:128, 0:TC],
                                             RW[:, bb * 2048 + dc * 128:bb * 2048 + (dc + 1) * 128],
                                             xh_r(dc) if p == 0 else xl_r(dc),
                                             start=(p == 0 and dc == 0),
                                             stop=(p == 1 and dc == DC - 1))
                        b.inc(i, 'tp', 1)
                        if ic >= 1:
                            t.wait_ge(_sp, silu_mark(ic - 1, 1))
                        for p in range(2):
                            for dc in range(DC):
                                i = t.matmul(PS1[0:128, 0:TC],
                                             RW[:, 4096 + bb * 2048 + dc * 128:4096 + bb * 2048 + (dc + 1) * 128],
                                             xh_r(dc) if p == 0 else xl_r(dc),
                                             start=(p == 0 and dc == 0),
                                             stop=(p == 1 and dc == DC - 1))
                        b.inc(i, 'tp', 1)
                        if ic % GRP == GRP - 1:
                            grp = ic // GRP
                            t.wait_ge(_vp, hl_mark[ic])
                            for fcp in range(DC):
                                k = grp * DC + fcp
                                bb2 = k % 2
                                ps = PS2 if fcp % 2 == 0 else PS3
                                t.wait_ge(FS2[bb2], w_mark_dn(grp, fcp))
                                if k >= 2:
                                    kk = k - 2
                                    t.wait_ge(_vp, dn_vp_mark[(kk // DC, kk % DC)])
                                for p in range(2):
                                    for icg in range(GRP):
                                        hsrc = (hh_bf[:, icg * TC:(icg + 1) * TC] if p == 0
                                                else hl_bf[:, icg * TC:(icg + 1) * TC])
                                        i = t.matmul(ps[0:128, 0:TC],
                                                     dw_bf[:, bb2 * 512 + icg * 128:bb2 * 512 + (icg + 1) * 128],
                                                     hsrc,
                                                     start=(p == 0 and icg == 0),
                                                     stop=(p == 1 and icg == GRP - 1))
                                b.inc(i, 'tp', 1)
                blk.tensor(t_mm)
                b.chk('tp', tp_end)

                def s_ev(s):
                    for ic in range(ICH):
                        s.wait_ge(_tp, gu_mark[(ic, 0)])
                        if ic >= 1:
                            s.wait_ge(_vp, vp0 + 5 * (ic - 1) + 1 + 16 * ((ic - 1) // GRP))
                        b.inc(s.activation(out=A3[:, 4 * TC:5 * TC], in_=PS0[0:128, 0:TC],
                                           func=AF.Silu, scale=gsc_sb[:, ic:ic + 1]), 'sp', 1)
                        s.wait_ge(_tp, gu_mark[(ic, 1)])
                        b.inc(s.activation(out=A3[:, 5 * TC:6 * TC], in_=PS1[0:128, 0:TC],
                                           func=AF.Copy, scale=usc_sb[:, ic:ic + 1]), 'sp', 1)
                blk.scalar(s_ev)
                b.chk('sp', sp0 + 2 * ICH)

                def v_h(v):
                    for ic in range(ICH):
                        icg = ic % GRP
                        v.wait_ge(_sp, silu_mark(ic, 1))
                        if ic >= 1:
                            v.wait_ge(_vp, hl_mark[ic - 1])
                        if ic >= GRP:
                            v.wait_ge(_tp, dn_tp_mark[(ic // GRP - 1, DC - 1)])
                        hbuf = A3[:, 0:TC]
                        m = b.inc(v.tensor_mul(out=hbuf, in0=A3[:, 4 * TC:5 * TC],
                                               in1=A3[:, 5 * TC:6 * TC]), 'vp', 1)
                        v.wait_ge(_vp, m)
                        m = b.inc(v.tensor_copy(out=hh_bf[:, icg * TC:(icg + 1) * TC],
                                                in_=hbuf), 'vp', 1)
                        v.wait_ge(_vp, m)
                        m = b.inc(v.tensor_sub(out=hl_bf[:, icg * TC:(icg + 1) * TC],
                                               in0=hbuf,
                                               in1=hh_bf[:, icg * TC:(icg + 1) * TC]), 'vp', 1)
                        v.wait_ge(_vp, m)
                        b.inc(v.memset(SM[64:65, 3068:3069], 0.0), 'vp', 1)
                        b.inc(v.memset(SM[64:65, 3067:3068], 0.0), 'vp', 1)
                        if ic % GRP == GRP - 1:
                            grp = ic // GRP
                            for fcp in range(DC):
                                ps = PS2 if fcp % 2 == 0 else PS3
                                v.wait_ge(_tp, dn_tp_mark[(grp, fcp)])
                                if grp >= 1:
                                    v.wait_ge(_vp, dn_vp_mark[(grp - 1, fcp)])
                                if grp == 0:
                                    i = v.tensor_copy(out=A6[:, fcp * TC:(fcp + 1) * TC],
                                                      in_=ps[0:128, 0:TC])
                                else:
                                    i = v.tensor_add(out=A6[:, fcp * TC:(fcp + 1) * TC],
                                                     in0=A6[:, fcp * TC:(fcp + 1) * TC],
                                                     in1=ps[0:128, 0:TC])
                                b.inc(i, 'vp', 1)
                blk.vector(v_h)
                b.chk('vp', vp_end)
                acc_done = b.c['vp']

                def q_xold(q):
                    q.wait_ge(_vp, acc_done)
                    b.dma(q, sb3(A5[:], DC, TC)[:, :, 0:TC], d3(xold_ap, DC)[:, :, tc0:tc0 + TC], 'sq')
                blk.sync(q_xold)
                xold_ld = b.c['sq']

                fin_vp0 = b.c['vp']
                fin_sp0 = b.c['sp']
                fin_tp0 = b.c['tp']
                gb0 = [b.c['g0'], b.c['g1'], b.c['g2'], b.c['g3']]

                def v_fin(v):
                    v.wait_ge(_sq, xold_ld)
                    v.wait_ge(_vp, acc_done)
                    for fcp in range(DC):
                        if fcp >= 4:
                            v.wait_ge(GS[fcp % 4], gb0[fcp % 4] + 16 * (fcp // 4))
                            if want_stats:
                                v.wait_ge(_sp, fin_sp0 + fcp - 3)
                                v.wait_ge(_tp, fin_tp0 + fcp - 3)
                        b.inc(v.scalar_tensor_tensor(
                            out=A2[:, (fcp % 4) * TC:(fcp % 4 + 1) * TC],
                            in0=A6[:, fcp * TC:(fcp + 1) * TC], scalar=dsc_sb[:, fcp:fcp + 1],
                            in1=A5[:, fcp * TC:(fcp + 1) * TC], op0=AT.mult, op1=AT.add), 'vp', 1)
                blk.vector(v_fin)

                def g_fin2(g):
                    for fcp in range(DC):
                        g.wait_ge(_vp, fin_vp0 + fcp + 1)
                        b.dma(g, d3(dst_ap, DC)[:, fcp, tc0:tc0 + TC],
                              A2[:, (fcp % 4) * TC:(fcp % 4 + 1) * TC], gsn[fcp % 4])
                blk.gpsimd(g_fin2)

                if want_stats:
                    def s_sq2(s):
                        for fcp in range(DC):
                            s.wait_ge(_vp, fin_vp0 + fcp + 1)
                            if fcp >= 2:
                                s.wait_ge(_tp, fin_tp0 + fcp - 1)
                            b.inc(s.activation(out=A3[:, (fcp % 2) * TC:(fcp % 2 + 1) * TC],
                                               in_=A2[:, (fcp % 4) * TC:(fcp % 4 + 1) * TC],
                                               func=AF.Square), 'sp', 1)
                    blk.scalar(s_sq2)

                    def t_st(t):
                        for fcp in range(DC):
                            t.wait_ge(_vp, fin_vp0 + fcp + 1)
                            t.matmul(PS6[0:1, 0:TC], ones_c[:],
                                     A2[:, (fcp % 4) * TC:(fcp % 4 + 1) * TC],
                                     start=(fcp == 0), stop=(fcp == DC - 1))
                            t.wait_ge(_sp, fin_sp0 + fcp + 1)
                            b.inc(t.matmul(PS7[0:1, 0:TC], ones_c[:],
                                           A3[:, (fcp % 2) * TC:(fcp % 2 + 1) * TC],
                                           start=(fcp == 0), stop=(fcp == DC - 1)), 'tp', 1)
                    blk.tensor(t_st)
                    emit_stats_finish(tc0)
                b.barrier()

        def emit_small_proj(W_ap, Bias_ap, nfc, rhs, dstf, mode='copy', base=None,
                            store=None, bias_n=None):
            bias_n = bias_n or nfc

            def q_bias(q):
                b.dma(q, bias_sb[:, 0:bias_n], Bias_ap, 'sq')
            blk.sync(q_bias)
            bld = b.c['sq']

            tp0 = b.c['tp']
            sp0 = b.c['sp']
            vp0 = b.c['vp']
            wb0 = [b.c['w0'], b.c['w1']]
            gb0 = b.c['g0']

            def q_w(q):
                for fc in range(nfc):
                    bb = fc % 2
                    if fc >= 2:
                        q.wait_ge(_tp, tp0 + fc - 1)
                    b.dma(q, sb3(A4[:, bb * 2048:(bb + 1) * 2048], DC, 128),
                          d3(W_ap, DC)[:, :, fc * 128:(fc + 1) * 128], wsn[bb])
            blk.sync(q_w)

            def t_mm(t):
                t.wait_ge(_sq, bld)
                for fc in range(nfc):
                    bb = fc % 2
                    ps = PS0 if bb == 0 else PS1
                    t.wait_ge(WS[bb], wb0[bb] + 16 * (fc // 2 + 1))
                    if fc >= 2:
                        if mode == 'copy':
                            t.wait_ge(_sp, sp0 + fc - 1)
                        else:
                            t.wait_ge(_vp, vp0 + fc - 1)
                    for dc in range(DC):
                        i = t.matmul(ps[0:128, 0:OWNLM],
                                     A4[:, bb * 2048 + dc * 128:bb * 2048 + (dc + 1) * 128],
                                     rhs(dc), start=(dc == 0), stop=(dc == DC - 1))
                    b.inc(i, 'tp', 1)
            blk.tensor(t_mm)

            if mode == 'copy':
                def s_ev(s):
                    for fc in range(nfc):
                        ps = PS0 if fc % 2 == 0 else PS1
                        s.wait_ge(_tp, tp0 + fc + 1)
                        b.inc(s.activation(out=dstf(fc), in_=ps[0:128, 0:OWNLM],
                                           func=AF.Identity, bias=bias_sb[:, fc:fc + 1]), 'sp', 1)
                blk.scalar(s_ev)
            else:
                def v_ev(v):
                    for fc in range(nfc):
                        ps = PS0 if fc % 2 == 0 else PS1
                        v.wait_ge(_tp, tp0 + fc + 1)
                        b.inc(v.scalar_tensor_tensor(out=dstf(fc), in0=ps[0:128, 0:OWNLM],
                                                     scalar=bias_sb[:, fc:fc + 1],
                                                     in1=base(fc), op0=AT.add, op1=AT.add),
                              'vp', 1)
                blk.vector(v_ev)

            if store is not None:
                da, row0f = store

                def g_s(g):
                    for fc in range(nfc):
                        if mode == 'copy':
                            g.wait_ge(_sp, sp0 + fc + 1)
                        else:
                            g.wait_ge(_vp, vp0 + fc + 1)
                        b.dma(g, d3(da, 0)[:, row0f(fc) // 128, :], dstf(fc), 'g0')
                blk.gpsimd(g_s)

        def emit_attn_b():
            def q_cs(q):
                b.dma(q, A6[:, 0:TOK], c_cos, 'sq')
                b.dma(q, A6[:, 4096:4096 + TOK], c_sin, 'sq')
                b.dma(q, pqa_sb[:], c_pqa, 'sq')
            blk.sync(q_cs)

            def dst_qkv_b(fc):
                if fc < 16:
                    return (scr_q, fc * 128)
                if fc < 32:
                    return (kvb, (fc - 16) * 128)
                return (kvb, D + (fc - 32) * 128)

            emit_qkv_stage(wqkv_b, bqkv_b, 48, dst_qkv_b, xres, ln_here=False,
                           rope=True, gate=True)

            gt_tp0 = b.c['tp']
            gt_vp0 = b.c['vp']

            def t_gt(t):
                for i in range(QT):
                    ps = PS2 if i % 2 == 0 else PS3
                    if i >= 2:
                        t.wait_ge(_vp, gt_vp0 + i - 1)
                    b.inc(t.transpose(ps[0:128, 0:32],
                                      gate_sm[:, i * 128:(i + 1) * 128],
                                      ident_sb[96:128, 96:128],
                                      tile_position=(96, 0)), 'tp', 1)
            blk.tensor(t_gt)

            def v_gt(v):
                for i in range(QT):
                    ps = PS2 if i % 2 == 0 else PS3
                    v.wait_ge(_tp, gt_tp0 + i + 1)
                    b.inc(v.tensor_copy(out=gate_t_sb[:, i * 32:(i + 1) * 32],
                                        in_=ps[0:128, 0:32]), 'vp', 1)
            blk.vector(v_gt)
            b.barrier()

            def kv_load(q, h, coloff, with_q=False):
                b.dma(q, A1[0:32, coloff:coloff + TOK], kvb[32 * h:32 * h + 32, :], wsn[h % 2])
                b.dma(q, A1[32:64, coloff:coloff + TOK],
                      kvb[1024 + 32 * h:1024 + 32 * h + 32, :], wsn[h % 2])
                b.dma(q, A1[64:128, coloff:coloff + TOK],
                      kvb[2048 + 64 * h:2048 + 64 * h + 64, :], wsn[h % 2])
                if with_q:
                    qoff = coloff + 2 * TOK
                    b.dma(q, A1[0:32, qoff:qoff + TOK], scr_q[32 * h:32 * h + 32, :], wsn[h % 2])
                    b.dma(q, A1[32:64, qoff:qoff + TOK],
                          scr_q[1024 + 32 * h:1024 + 32 * h + 32, :], wsn[h % 2])

            def emit_vt(t, coloff, vtoff, tpb, vpb):
                for i in range(QT):
                    ps = PS2 if i % 2 == 0 else PS3
                    if i >= 2:
                        t.wait_ge(_vp, vpb + i - 1)
                    b.inc(t.transpose(ps[0:128, 0:64],
                                      A1[64:128, coloff + i * 128:coloff + (i + 1) * 128],
                                      ident_sb[64:128, 64:128], tile_position=(64, 0)), 'tp', 1)

            def emit_vt_ev(v, vtoff, tpb):
                for i in range(QT):
                    ps = PS2 if i % 2 == 0 else PS3
                    v.wait_ge(_tp, tpb + i + 1)
                    b.inc(v.tensor_copy(out=A5[:, vtoff + 66 * i:vtoff + 66 * i + 64],
                                        in_=ps[0:128, 0:64]), 'vp', 1)

            def v_ones(v):
                v.memset(A5[:, 64:66 * QT:66], 1.0)
                b.inc(v.memset(A5[:, 2112 + 64:2112 + 66 * QT:66], 1.0), 'vp', 1)
            blk.vector(v_ones)
            b.barrier()

            for h in range(H2):
                coloff = (h % 2) * TOK
                vtoff = (h % 2) * 2112
                hb = 64 * (h % 2)
                hc = h // 2

                def q_h(q, h=h, coloff=coloff):
                    kv_load(q, h, coloff)
                blk.sync(q_h)
                ld_mark = b.c[wsn[h % 2]]

                tpb = b.c['tp']
                vpb = b.c['vp']

                def t_p(t, coloff=coloff, vtoff=vtoff, ld_mark=ld_mark,
                        tpb=tpb, vpb=vpb, h=h):
                    t.wait_ge(WS[h % 2], ld_mark)
                    emit_vt(t, coloff, vtoff, tpb, vpb)
                blk.tensor(t_p)

                def v_p(v, vtoff=vtoff, tpb=tpb):
                    emit_vt_ev(v, vtoff, tpb)
                blk.vector(v_p)
                vt_done = b.c['vp']

                def t_sc(t, h=h, coloff=coloff, vt_done=vt_done):
                    t.wait_ge(_vp, vt_done)
                    for i in range(QT):
                        b.inc(t.matmul(PS0[0:128, 4 * i:4 * i + 4],
                                       A1[0:64, coloff + i * 128:coloff + (i + 1) * 128],
                                       pqa_sb[0:64, 4 * h:4 * h + 4],
                                       start=True, stop=True), 'tp', 1)
                blk.tensor(t_sc)
                sc_done = b.c['tp']

                def s_e(s, sc_done=sc_done):
                    s.wait_ge(_tp, sc_done)
                    b.inc(s.activation(out=A3[:, 0:4 * QT], in_=PS0[0:128, 0:4 * QT],
                                       func=AF.Exp, scale=SC_B), 'sp', 1)
                blk.scalar(s_e)
                e_done = b.c['sp']

                def t_lm(t, vtoff=vtoff, e_done=e_done):
                    t.wait_ge(_sp, e_done)
                    for i in range(QT):
                        for bb_ in range(2):
                            psB = PS1 if bb_ == 0 else PS5
                            b.inc(t.matmul(psB[0:65, 4 * i:4 * i + 4],
                                           A5[64 * bb_:64 * bb_ + 64,
                                              vtoff + 66 * i:vtoff + 66 * i + 65],
                                           A3[64 * bb_:64 * bb_ + 64, 4 * i:4 * i + 4],
                                           start=True, stop=True,
                                           tile_position=(64 * bb_, 0)), 'tp', 1)
                blk.tensor(t_lm)
                lm_done = b.c['tp']

                def v_r(v, lm_done=lm_done):
                    v.wait_ge(_tp, lm_done)
                    m = b.inc(v.reciprocal(out=rden[:, 0:4 * QT], in_=PS1[64:65, 0:4 * QT]), 'vp', 1)
                    b.inc(v.reciprocal(out=rden[:, 256:256 + 4 * QT],
                                       in_=PS5[64:65, 0:4 * QT]), 'vp', 1)
                blk.vector(v_r)
                r_done = b.c['vp']

                def t_bc(t, r_done=r_done):
                    t.wait_ge(_vp, r_done)
                    t.matmul(PS4[0:64, 0:4 * QT], ones_r[64:65, 0:64],
                             rden[:, 0:4 * QT], start=True, stop=True,
                             tile_position=(64, 0))
                    b.inc(t.matmul(PS4[0:64, 256:256 + 4 * QT], ones_r[64:65, 0:64],
                                   rden[:, 256:256 + 4 * QT], start=True, stop=True,
                                   tile_position=(64, 0)), 'tp', 1)
                blk.tensor(t_bc)
                bc_done = b.c['tp']

                def s_c(s, bc_done=bc_done):
                    s.wait_ge(_tp, bc_done)
                    s.activation(out=A3[0:64, 256:256 + 4 * QT], in_=PS4[0:64, 0:4 * QT],
                                 func=AF.Copy)
                    b.inc(s.activation(out=A3[0:64, 384:384 + 4 * QT],
                                       in_=PS4[0:64, 256:256 + 4 * QT], func=AF.Copy), 'sp', 1)
                blk.scalar(s_c)
                c_done = b.c['sp']

                def v_out(v, hb=hb, hc=hc, c_done=c_done):
                    v.wait_ge(_sp, c_done)
                    lmf = A2[hb:hb + 64, hc * OWNLM:(hc + 1) * OWNLM]
                    lmf3 = lmf.rearrange("p (i m) -> p i m", m=8)
                    v.tensor_mul(out=lmf3[:, :, 0:4],
                                 in0=PS1[0:64, 0:4 * QT].rearrange("p (i m) -> p i m", m=4),
                                 in1=A3[0:64, 256:256 + 4 * QT].rearrange("p (i m) -> p i m", m=4))
                    b.inc(v.tensor_mul(out=lmf3[:, :, 4:8],
                                       in0=PS5[0:64, 0:4 * QT].rearrange("p (i m) -> p i m", m=4),
                                       in1=A3[0:64, 384:384 + 4 * QT].rearrange("p (i m) -> p i m", m=4)),
                          'vp', 1)
                blk.vector(v_out)
                b.barrier()


            if DEV_SUB == 31:
                def g_sub31(g):
                    b.dma(g, y[0:128, 0:2048], A2[:, 0:2048], 'gq')
                    g.wait_ge(_gq, b.c['gq'])
                blk.gpsimd(g_sub31)
                b.barrier()
                return

            def rhs_lmf0(dc):
                return A2[:, dc * OWNLM:(dc + 1) * OWNLM]

            def dst_lqkv(fc):
                return A5[:, fc * OWNLM:(fc + 1) * OWNLM]

            emit_small_proj(wlmqkv, blmqkv, 48, rhs_lmf0, dst_lqkv, mode='copy',
                            store=None, bias_n=48)
            st_sp = b.c['sp']

            def g_lkv(g):
                g.wait_ge(_sp, st_sp)
                for fc in range(16, 48):
                    b.dma(g, d3(ag3_in, 0)[:, fc - 16, :], dst_lqkv(fc), 'gq')
            blk.gpsimd(g_lkv)
            b.barrier()

            def q_lkv(q):
                b.dma(q, sb3(A6[:, 0:32 * NL], 32, NL), d3(ag3_in, 32), 'sq')
            blk.sync(q_lkv)
            lkv_ld = b.c['sq']

            def v_ones2(v):
                b.inc(v.memset(A3[:, 576:577], 1.0), 'vp', 1)
            blk.vector(v_ones2)

            for h in range(H2):
                hb = 64 * (h % 2)
                hc = h // 2
                tp_l0 = b.c['tp']
                vp_l0 = b.c['vp']
                sp_l0 = b.c['sp']

                def t_l(t, h=h, hb=hb, hc=hc, tp_l0=tp_l0, vp_l0=vp_l0):
                    t.wait_ge(_sq, lkv_ld)
                    b.inc(t.matmul(PS0[0:128, 0:OWNLM],
                                   A6[hb:hb + 64, hc * NL:(hc + 1) * NL],
                                   A5[hb:hb + 64, (h // 2) * OWNLM + 0:(h // 2) * OWNLM + OWNLM]
                                   if False else
                                   A5[hb:hb + 64, hc * OWNLM:(hc + 1) * OWNLM],
                                   start=True, stop=True, tile_position=(hb, 0)), 'tp', 1)
                    b.inc(t.transpose(PS2[0:128, 0:64],
                                      A6[hb:hb + 64, (16 + hc) * NL:(16 + hc) * NL + 128],
                                      ident_sb[hb:hb + 64, hb:hb + 64],
                                      tile_position=(hb, 0)), 'tp', 1)
                    t.wait_ge(_vp, vp_l0 + 2)
                    b.inc(t.matmul(PS1[0:65, 0:OWNLM], A3[:, 512:577],
                                   A3[:, 0:OWNLM], start=True, stop=True), 'tp', 1)
                    t.wait_ge(_vp, vp_l0 + 4)
                    b.inc(t.matmul(PS4[0:64, 0:OWNLM], ones_r[64:65, 0:64],
                                   rden[:, 0:OWNLM], start=True, stop=True,
                                   tile_position=(64, 0)), 'tp', 1)
                blk.tensor(t_l)

                def s_l(s, tp_l0=tp_l0):
                    s.wait_ge(_tp, tp_l0 + 1)
                    b.inc(s.activation(out=A3[:, 0:OWNLM], in_=PS0[0:128, 0:OWNLM],
                                       func=AF.Exp, scale=SC_B), 'sp', 1)
                    s.wait_ge(_tp, tp_l0 + 4)
                    b.inc(s.activation(out=A3[0:64, 768:768 + OWNLM], in_=PS4[0:64, 0:OWNLM],
                                       func=AF.Copy), 'sp', 1)
                blk.scalar(s_l)

                def v_l(v, hb=hb, hc=hc, tp_l0=tp_l0, sp_l0=sp_l0):
                    v.wait_ge(_sp, sp_l0 + 1)
                    b.inc(v.tensor_mul(out=A3[:, 0:OWNLM], in0=A3[:, 0:OWNLM],
                                       in1=lmask_sb[:]), 'vp', 1)
                    v.wait_ge(_tp, tp_l0 + 2)
                    b.inc(v.tensor_copy(out=A3[:, 512:576], in_=PS2[0:128, 0:64]), 'vp', 1)
                    v.wait_ge(_tp, tp_l0 + 3)
                    m = b.inc(v.tensor_scalar_max(out=rden[:, 0:OWNLM],
                                                  in0=PS1[64:65, 0:OWNLM], scalar1=1e-9), 'vp', 1)
                    v.wait_ge(_vp, m)
                    b.inc(v.reciprocal(out=rden[:, 0:OWNLM], in_=rden[:, 0:OWNLM]), 'vp', 1)
                    v.wait_ge(_sp, sp_l0 + 2)
                    b.inc(v.tensor_mul(out=A2[hb:hb + 64, 2048 + hc * OWNLM:2048 + (hc + 1) * OWNLM],
                                       in0=PS1[0:64, 0:OWNLM],
                                       in1=A3[0:64, 768:768 + OWNLM]), 'vp', 1)
                blk.vector(v_l)
                b.barrier()

            if DEV_SUB == 32:
                def g_sub32(g):
                    b.dma(g, y[0:128, 0:2048], A2[:, 2048:4096], 'gq')
                    g.wait_ge(_gq, b.c['gq'])
                blk.gpsimd(g_sub32)
                b.barrier()
                return

            def rhs_lo(dc):
                return A2[:, 2048 + dc * OWNLM:2048 + (dc + 1) * OWNLM]

            def dst_lmf1(fc):
                return A2[:, 4096 + fc * OWNLM:4096 + (fc + 1) * OWNLM]

            def base_lmf0(fc):
                return A2[:, fc * OWNLM:(fc + 1) * OWNLM]

            emit_small_proj(wlmout, blmout, 16, rhs_lo, dst_lmf1, mode='addres',
                            base=base_lmf0, bias_n=16)
            b.barrier()

            def rhs_lmf1(dc):
                return A2[:, 4096 + dc * OWNLM:4096 + (dc + 1) * OWNLM]

            def dst_l2q(fc):
                return A2[:, 6144 + fc * OWNLM:6144 + (fc + 1) * OWNLM]

            emit_small_proj(wl2tq, bl2tq, 16, rhs_lmf1, dst_l2q, mode='copy', bias_n=16)
            b.barrier()

            def v_ones_r(v):
                v.memset(A5[:, 64:66 * QT:66], 1.0)
                b.inc(v.memset(A5[:, 2112 + 64:2112 + 66 * QT:66], 1.0), 'vp', 1)
            blk.vector(v_ones_r)
            b.barrier()

            for h in range(H2):
                coloff = (h % 2) * TOK
                vtoff = (h % 2) * 2112
                hb = 64 * (h % 2)
                hc = h // 2
                xeb = (32 * h) % 128
                xec = (32 * h) // 128
                xob = (1024 + 32 * h) % 128
                xoc = (1024 + 32 * h) // 128

                def q_h2(q, h=h, coloff=coloff):
                    kv_load(q, h, coloff)
                blk.sync(q_h2)
                ld_mark = b.c[wsn[h % 2]]

                tpb = b.c['tp']
                vpb = b.c['vp']

                def t_v2(t, coloff=coloff, vtoff=vtoff, ld_mark=ld_mark,
                         tpb=tpb, vpb=vpb, h=h):
                    t.wait_ge(WS[h % 2], ld_mark)
                    emit_vt(t, coloff, vtoff, tpb, vpb)
                blk.tensor(t_v2)

                def v_v2(v, vtoff=vtoff, tpb=tpb):
                    emit_vt_ev(v, vtoff, tpb)
                blk.vector(v_v2)
                vt_done = b.c['vp']

                def v_cq(v, xeb=xeb, xec=xec, xob=xob, xoc=xoc, vt_done=vt_done):
                    v.wait_ge(_vp, vt_done)
                    v.tensor_copy(out=A3[0:32, 896:896 + OWNLM],
                                  in_=A2[xeb:xeb + 32, 6144 + xec * OWNLM:6144 + (xec + 1) * OWNLM])
                    b.inc(v.tensor_copy(out=A3[32:64, 896:896 + OWNLM],
                                        in_=A2[xob:xob + 32,
                                               6144 + xoc * OWNLM:6144 + (xoc + 1) * OWNLM]),
                          'vp', 1)
                blk.vector(v_cq)
                cq_done = b.c['vp']

                def t_r(t, coloff=coloff, cq_done=cq_done):
                    t.wait_ge(_vp, cq_done)
                    for i in range(QT):
                        b.inc(t.matmul(PS0[0:128, 8 * i:8 * i + 8],
                                       A1[0:64, coloff + i * 128:coloff + (i + 1) * 128],
                                       A3[0:64, 896 + 8 * i:896 + 8 * i + 8],
                                       start=True, stop=True), 'tp', 1)
                blk.tensor(t_r)
                rsc_done = b.c['tp']

                def s_r(s, rsc_done=rsc_done):
                    s.wait_ge(_tp, rsc_done)
                    b.inc(s.activation(out=A3[:, 0:8 * QT], in_=PS0[0:128, 0:8 * QT],
                                       func=AF.Exp, scale=SC_B), 'sp', 1)
                blk.scalar(s_r)
                er_done = b.c['sp']

                def v_m(v, er_done=er_done):
                    v.wait_ge(_sp, er_done)
                    b.inc(v.tensor_mul(out=A3[:, 0:8 * QT], in0=A3[:, 0:8 * QT],
                                       in1=l2tm_sb[:]), 'vp', 1)
                blk.vector(v_m)
                vm_done = b.c['vp']

                def t_ro(t, vtoff=vtoff, vm_done=vm_done):
                    t.wait_ge(_vp, vm_done)
                    for i in range(QT):
                        for bb_ in range(2):
                            psB = PS1 if bb_ == 0 else PS5
                            b.inc(t.matmul(psB[0:65, 4 * i:4 * i + 4],
                                           A5[64 * bb_:64 * bb_ + 64,
                                              vtoff + 66 * i:vtoff + 66 * i + 65],
                                           A3[64 * bb_:64 * bb_ + 64,
                                              8 * i + 4 * bb_:8 * i + 4 * bb_ + 4],
                                           start=True, stop=True,
                                           tile_position=(64 * bb_, 0)), 'tp', 1)
                blk.tensor(t_ro)
                ro_done = b.c['tp']

                def v_r2(v, ro_done=ro_done):
                    v.wait_ge(_tp, ro_done)
                    v.reciprocal(out=rden[:, 0:4 * QT], in_=PS1[64:65, 0:4 * QT])
                    b.inc(v.reciprocal(out=rden[:, 256:256 + 4 * QT],
                                       in_=PS5[64:65, 0:4 * QT]), 'vp', 1)
                blk.vector(v_r2)
                r2_done = b.c['vp']

                def t_bc2(t, r2_done=r2_done):
                    t.wait_ge(_vp, r2_done)
                    t.matmul(PS4[0:64, 0:4 * QT], ones_r[64:65, 0:64],
                             rden[:, 0:4 * QT], start=True, stop=True,
                             tile_position=(64, 0))
                    b.inc(t.matmul(PS4[0:64, 256:256 + 4 * QT], ones_r[64:65, 0:64],
                                   rden[:, 256:256 + 4 * QT], start=True, stop=True,
                                   tile_position=(64, 0)), 'tp', 1)
                blk.tensor(t_bc2)
                bc2_done = b.c['tp']

                def s_c2(s, bc2_done=bc2_done):
                    s.wait_ge(_tp, bc2_done)
                    s.activation(out=A3[0:64, 256:256 + 4 * QT], in_=PS4[0:64, 0:4 * QT],
                                 func=AF.Copy)
                    b.inc(s.activation(out=A3[0:64, 384:384 + 4 * QT],
                                       in_=PS4[0:64, 256:256 + 4 * QT], func=AF.Copy), 'sp', 1)
                blk.scalar(s_c2)
                c2_done = b.c['sp']

                def v_o2(v, hb=hb, hc=hc, c2_done=c2_done):
                    v.wait_ge(_sp, c2_done)
                    ro_ = A2[hb:hb + 64, 2048 + hc * OWNLM:2048 + (hc + 1) * OWNLM]
                    ro3 = ro_.rearrange("p (i m) -> p i m", m=8)
                    v.tensor_mul(out=ro3[:, :, 0:4],
                                 in0=PS1[0:64, 0:4 * QT].rearrange("p (i m) -> p i m", m=4),
                                 in1=A3[0:64, 256:256 + 4 * QT].rearrange("p (i m) -> p i m", m=4))
                    b.inc(v.tensor_mul(out=ro3[:, :, 4:8],
                                       in0=PS5[0:64, 0:4 * QT].rearrange("p (i m) -> p i m", m=4),
                                       in1=A3[0:64, 384:384 + 4 * QT].rearrange("p (i m) -> p i m", m=4)),
                          'vp', 1)
                blk.vector(v_o2)
                b.barrier()

            if DEV_SUB == 33:
                def g_sub33(g):
                    b.dma(g, y[0:128, 0:2048], A2[:, 2048:4096], 'gq')
                    g.wait_ge(_gq, b.c['gq'])
                blk.gpsimd(g_sub33)
                b.barrier()
                return

            def dst_lmf2(fc):
                return A2[:, fc * OWNLM:(fc + 1) * OWNLM]

            emit_small_proj(wl2tout, bl2tout, 16, rhs_lo, dst_lmf2, mode='addres',
                            base=rhs_lmf1, bias_n=16)
            b.barrier()

            def rhs_lmf2(dc):
                return A2[:, dc * OWNLM:(dc + 1) * OWNLM]

            def dst_gkv(fc):
                return A5[:, fc * OWNLM:(fc + 1) * OWNLM]

            emit_small_proj(wlmkv, blmkv, 32, rhs_lmf2, dst_gkv, mode='copy',
                            store=(ag4_in, lambda fc: fc * 128), bias_n=32)
            b.barrier()

            if DEV_DBG:
                def g_dbg2(g):
                    b.dma(g, dbg[:, 0:2048], A2[:, 2048:4096], 'gq')
                    b.dma(g, dbg[:, 2048:4096], A2[:, 0:2048], 'gq')
                    g.wait_ge(_gq, b.c['gq'])
                blk.gpsimd(g_dbg2)
                b.barrier()

            if DEV_SUB == 34:
                def g_sub34(g):
                    b.dma(g, y[0:2048, 0:128], ag4_in[0:2048, :], 'gq')
                    g.wait_ge(_gq, b.c['gq'])
                blk.gpsimd(g_sub34)
                b.barrier()
                return

            def q_gm(q):
                b.dma(q, A6[:, 0:QT * 128], c_gmask, 'sq')
            blk.sync(q_gm)
            gmask_sb = A6
            b.barrier()
            def v_ones_r2(v):
                v.memset(A5[:, 64:66 * QT:66], 1.0)
                b.inc(v.memset(A5[:, 2112 + 64:2112 + 66 * QT:66], 1.0), 'vp', 1)
            blk.vector(v_ones_r2)
            b.barrier()

            for h in range(H2):
                coloff = (h % 2) * TOK
                qoff = coloff + 2 * TOK
                vtoff = (h % 2) * 2112
                hb = 64 * (h % 2)

                def q_h3(q, h=h, coloff=coloff):
                    kv_load(q, h, coloff, with_q=True)
                    b.dma(q, A3[0:32, 896:896 + NL], ag4_in[32 * h:32 * h + 32, :], wsn[h % 2])
                    b.dma(q, A3[32:64, 896:896 + NL],
                          ag4_in[1024 + 32 * h:1024 + 32 * h + 32, :], wsn[h % 2])
                    b.dma(q, A3[64:128, 1024:1024 + NL],
                          ag4_in[2048 + 64 * h:2048 + 64 * h + 64, :], wsn[h % 2])
                blk.sync(q_h3)
                ld_mark = b.c[wsn[h % 2]]

                tpb = b.c['tp']
                vpb = b.c['vp']

                def t_v3(t, coloff=coloff, vtoff=vtoff, ld_mark=ld_mark,
                         tpb=tpb, vpb=vpb, h=h):
                    t.wait_ge(WS[h % 2], ld_mark)
                    emit_vt(t, coloff, vtoff, tpb, vpb)
                    b.inc(t.transpose(PS5[0:128, 0:64], A3[64:128, 1024:1024 + NL],
                                      ident_sb[64:128, 64:128], tile_position=(64, 0)), 'tp', 1)
                blk.tensor(t_v3)
                gvt_tp = b.c['tp']

                def v_v3(v, vtoff=vtoff, tpb=tpb, gvt_tp=gvt_tp):
                    emit_vt_ev(v, vtoff, tpb)
                    v.wait_ge(_tp, gvt_tp)
                    b.inc(v.tensor_copy(out=A3[:, 512:576], in_=PS5[0:128, 0:64]), 'vp', 1)
                blk.vector(v_v3)
                vt_done = b.c['vp']

                for i in range(QT):
                    t0 = b.c['tp']
                    s0 = b.c['sp']
                    v0 = b.c['vp']

                    def t_a(t, h=h, i=i, coloff=coloff, qoff=qoff, vtoff=vtoff,
                            vt_done=vt_done, t0=t0, s0=s0, v0=v0):
                        t.wait_ge(_vp, vt_done)
                        if i >= 1:
                            t.wait_ge(_sp, s0)
                        b.inc(t.matmul(PS0[0:128, 0:128],
                                       A1[0:64, coloff + i * 128:coloff + (i + 1) * 128],
                                       A1[0:64, qoff + i * 128:qoff + (i + 1) * 128],
                                       start=True, stop=True), 'tp', 1)
                        b.inc(t.matmul(PS2[0:128, 0:128],
                                       A3[0:64, 896:896 + NL],
                                       A1[0:64, qoff + i * 128:qoff + (i + 1) * 128],
                                       start=True, stop=True), 'tp', 1)
                        t.wait_ge(_vp, v0 + 1)
                        b.inc(t.matmul(PS1[0:128, 0:65], A3[:, 0:128],
                                       A5[:, vtoff + 66 * i:vtoff + 66 * i + 65],
                                       start=True, stop=True), 'tp', 1)
                        t.wait_ge(_vp, v0 + 2)
                        b.inc(t.matmul(PS3[0:128, 0:65], A3[:, 128:256],
                                       A3[:, 512:577], start=True, stop=True), 'tp', 1)
                        t.wait_ge(_vp, v0 + 10)
                        b.inc(t.transpose(PS4[0:64, 0:128], A3[:, 640:704],
                                          ident_sb[:]), 'tp', 1)
                    blk.tensor(t_a)

                    def s_a(s, t0=t0):
                        s.wait_ge(_tp, t0 + 1)
                        b.inc(s.activation(out=A3[:, 0:128], in_=PS0[0:128, 0:128],
                                           func=AF.Exp, scale=SC_B), 'sp', 1)
                        s.wait_ge(_tp, t0 + 2)
                        b.inc(s.activation(out=A3[:, 128:256], in_=PS2[0:128, 0:128],
                                           func=AF.Exp, scale=SC_B), 'sp', 1)
                    blk.scalar(s_a)

                    def v_a(v, h=h, i=i, hb=hb, t0=t0, s0=s0, v0=v0):
                        v.wait_ge(_sp, s0 + 1)
                        b.inc(v.tensor_mul(out=A3[:, 0:128], in0=A3[:, 0:128],
                                           in1=blktri_sb[:]), 'vp', 1)
                        v.wait_ge(_sp, s0 + 2)
                        b.inc(v.tensor_mul(out=A3[:, 128:256], in0=A3[:, 128:256],
                                           in1=gmask_sb[:, i * 128:(i + 1) * 128]), 'vp', 1)
                        v.wait_ge(_tp, t0 + 3)
                        m = b.inc(v.tensor_scalar_max(out=A3[:, 704:705],
                                                      in0=PS1[0:128, 64:65], scalar1=1e-9), 'vp', 1)
                        v.wait_ge(_vp, m)
                        m = b.inc(v.reciprocal(out=A3[:, 705:706], in_=A3[:, 704:705]), 'vp', 1)
                        v.wait_ge(_tp, t0 + 4)
                        m2 = b.inc(v.tensor_scalar_max(out=A3[:, 706:707],
                                                       in0=PS3[0:128, 64:65], scalar1=1e-9), 'vp', 1)
                        v.wait_ge(_vp, m2)
                        mg = b.inc(v.reciprocal(out=A3[:, 707:708], in_=A3[:, 706:707]), 'vp', 1)
                        m3 = b.inc(v.tensor_scalar_mul(out=A3[:, 768:832], in0=PS1[0:128, 0:64],
                                                       scalar1=A3[:, 705:706]), 'vp', 1)
                        v.wait_ge(_vp, mg)
                        b.inc(v.tensor_scalar_mul(out=A3[:, 832:896], in0=PS3[0:128, 0:64],
                                                  scalar1=A3[:, 707:708]), 'vp', 1)
                        v.wait_ge(_vp, m3 + 1)
                        m4 = b.inc(v.tensor_sub(out=A3[:, 640:704], in0=A3[:, 832:896],
                                                in1=A3[:, 768:832]), 'vp', 1)
                        v.wait_ge(_vp, m4)
                        b.inc(v.scalar_tensor_tensor(out=A3[:, 640:704], in0=A3[:, 640:704],
                                                     scalar=gate_t_sb[:, 32 * i + h:32 * i + h + 1],
                                                     in1=A3[:, 768:832], op0=AT.mult, op1=AT.add),
                              'vp', 1)
                        v.wait_ge(_tp, t0 + 5)
                        b.inc(v.tensor_copy(out=A2[hb:hb + 64, i * 128:(i + 1) * 128],
                                            in_=PS4[0:64, 0:128]), 'vp', 1)
                    blk.vector(v_a)

                st_v = b.c['vp']

                def g_o(g, h=h, hb=hb, st_v=st_v):
                    g.wait_ge(_vp, st_v)
                    b.dma(g, scr_ao[64 * h:64 * h + 64, :], A2[hb:hb + 64, 0:TOK], 'gq')
                blk.gpsimd(g_o)
                b.barrier()

        def dst_qkv_a(fc):
            if fc < 16:
                return (scr_q, fc * 128)
            if fc < 32:
                return (ag1_in, (fc - 16) * 128)
            return (ag1_in, D + (fc - 32) * 128)

        emit_qkv_stage(wqkv_a, bqkv_a, 48, dst_qkv_a, x0, ln_here=True)
        emit_attn_a()
        dst1 = y if DEV_STOP == 1 else xres
        emit_proj_res(scr_ao, wo_a, bo_a, x0, dst1, want_stats=(DEV_STOP > 1))

        if DEV_STOP >= 2:
            dst2 = y if DEV_STOP == 2 else xres
            emit_ffn(xres, dst2, want_stats=(DEV_STOP > 2))

        if DEV_STOP >= 3:
            emit_attn_b()
            if DEV_SUB == 0:
                dst3 = y if DEV_STOP == 3 else xres
                emit_proj_res(scr_ao, wout, bout, xres, dst3, want_stats=(DEV_STOP > 3))

        if DEV_STOP >= 4 and DEV_SUB == 0:
            emit_ffn(xres, y, want_stats=False)

        def g_fin(g):
            g.wait_ge(_gq, b.c['gq'])
        blk.gpsimd(g_fin)

    return nc


_NC_CACHE = {}


def kernel(**inputs):
    in_maps = stage_inputs(inputs)
    key = (PHASE2, DEV_STOP, DEV_SUB)
    if key not in _NC_CACHE:
        _NC_CACHE[key] = build_program()
    nc = _NC_CACHE[key]
    res = run_bass_kernel_spmd(nc, in_maps, core_ids=list(range(8)))
    out = np.empty((Bsz, L, D), dtype=np.float32)
    for core in range(8):
        if not PHASE2 and core % 2 == 1:
            continue
        s = core // 2
        own_tiles = list(range(core % 2, 16, 2)) if PHASE2 else list(range(16))
        yc = res.results[core]['y']
        for ii, tt in enumerate(own_tiles):
            out[s, 128 * tt:128 * tt + 128, :] = yc[:, 128 * ii:128 * (ii + 1)].T
    return out


if __name__ == "__main__":
    pass


# version 03
# speedup vs baseline: 1.8017x; 1.8017x over previous
import numpy as np
import concourse.bass as bass
import concourse.mybir as mybir
from concourse.bass_utils import run_bass_kernel_spmd

F32 = mybir.dt.float32
F32R = mybir.dt.float32r
AT = mybir.AluOpType
AF = mybir.ActivationFunctionType

D = 2048
L = 2048
Bsz = 4
NH = 16
H2 = 32
M4 = 4
NL = 128
I5 = 5632
DC = 16
VELT = 4097.0
EPS_LN = 1e-5
SC_A = float(1.0 / np.sqrt(128.0))
SC_B = float(1.0 / np.sqrt(64.0))

PHASE2 = True
TOK = L // 2 if PHASE2 else L
QT = TOK // 128
OWNLM = NL // 2 if PHASE2 else NL
TC = 512
NTC = TOK // TC
ICH = I5 // 128
DEV_STOP = 4
DEV_DBG = False
DEV_SUB = 0


class Bld:
    def __init__(self, nc, blk, sems):
        self.nc, self.blk, self.sem = nc, blk, sems
        self.c = {k: 0 for k in sems}

    def inc(self, instr, s, n=1):
        instr.then_inc(self.sem[s], n)
        self.c[s] += n
        return self.c[s]

    def dma(self, eng, out, in_, q):
        return self.inc(eng.dma_start(out=out, in_=in_), q, 16)

    def chk(self, s, expected):
        assert self.c[s] == expected, f"sem {s}: emitted {self.c[s]} != planned {expected}"

    def barrier(self):
        tgt = self.c['bar'] + 5
        syncq = [(q, self.c[q]) for q in ('sq', 'w0', 'w1', 'w2', 'w3')]
        gpq = [(q, self.c[q]) for q in ('gq', 'g0', 'g1', 'g2', 'g3', 'g4', 'f0', 'f1', 'f2', 'f3')]
        bar = self.sem['bar']

        def t_f(t):
            t.wait_ge(self.sem['sq'], self.prolog_sq)
            self.inc(t.matmul(self.scratch_ps[96:97, 508:509],
                              self.ident[96:97, 96:97], self.ident[96:97, 96:97],
                              start=True, stop=True, tile_position=(96, 96)), 'bar', 1)
            t.wait_ge(bar, tgt)

        def v_f(v):
            self.inc(v.memset(self.scratch_sm[64:65, 3071:3072], 0.0), 'bar', 1)
            v.wait_ge(bar, tgt)

        def s_f(s):
            self.inc(s.memzero(self.scratch_sm[64:65, 3070:3071]), 'bar', 1)
            s.wait_ge(bar, tgt)

        def q_f(eng):
            for q, v in syncq:
                if v:
                    eng.wait_ge(self.sem[q], v)
            eng.sem_inc(bar, 1)
            self.c['bar'] += 1
            eng.wait_ge(bar, tgt)

        def g_f(eng):
            for q, v in gpq:
                if v:
                    eng.wait_ge(self.sem[q], v)
            eng.sem_inc(bar, 1)
            self.c['bar'] += 1
            eng.wait_ge(bar, tgt)
        self.blk.tensor(t_f)
        self.blk.vector(v_f)
        self.blk.scalar(s_f)
        self.blk.sync(q_f)
        self.blk.gpsimd(g_f)
        assert self.c['bar'] == tgt


def _rope_perm():
    perm = np.zeros(D, dtype=np.int64)
    for h in range(H2):
        for j in range(32):
            perm[32 * h + j] = 64 * h + 2 * j
            perm[1024 + 32 * h + j] = 64 * h + 2 * j + 1
    return perm


def stage_inputs(inputs):
    f32 = np.float32
    perm = _rope_perm()

    wqkv_a = np.ascontiguousarray(inputs['L0_attn_a_Wqkv'], dtype=f32)
    bqkv_a = np.ascontiguousarray(np.asarray(inputs['L0_attn_a_bqkv'], dtype=f32).reshape(48, 128).T)
    wo_a = np.ascontiguousarray(inputs['L0_attn_a_Wo'], dtype=f32)
    bo_a = np.ascontiguousarray(np.asarray(inputs['L0_attn_a_bo'], dtype=f32).reshape(16, 128).T)

    wqkv_b = np.array(inputs['L0_attn_b_w_qkv'], dtype=f32)
    bqkv_b = np.array(inputs['L0_attn_b_b_qkv'], dtype=f32)
    wqkv_b[:, 0:D] = wqkv_b[:, 0:D][:, perm]
    wqkv_b[:, D:2 * D] = wqkv_b[:, D:2 * D][:, perm]
    bqkv_b[0:D] = bqkv_b[0:D][perm]
    bqkv_b[D:2 * D] = bqkv_b[D:2 * D][perm]
    wqkv_b = np.ascontiguousarray(wqkv_b)
    bqkv_b = np.ascontiguousarray(bqkv_b.reshape(48, 128).T)

    pqT = np.ascontiguousarray(np.asarray(inputs['L0_attn_b_pool_q'], dtype=f32).T[perm, :])

    wlmqkv = np.ascontiguousarray(inputs['L0_attn_b_w_lm_qkv'], dtype=f32)
    blmqkv = np.ascontiguousarray(np.asarray(inputs['L0_attn_b_b_lm_qkv'], dtype=f32).reshape(48, 128).T)
    wlmout = np.ascontiguousarray(inputs['L0_attn_b_w_lm_out'], dtype=f32)
    blmout = np.ascontiguousarray(np.asarray(inputs['L0_attn_b_b_lm_out'], dtype=f32).reshape(16, 128).T)
    wl2tq = np.ascontiguousarray(np.asarray(inputs['L0_attn_b_w_l2t_q'], dtype=f32)[:, perm])
    bl2tq = np.ascontiguousarray(np.asarray(inputs['L0_attn_b_b_l2t_q'], dtype=f32)[perm].reshape(16, 128).T)
    wl2tout = np.ascontiguousarray(inputs['L0_attn_b_w_l2t_out'], dtype=f32)
    bl2tout = np.ascontiguousarray(np.asarray(inputs['L0_attn_b_b_l2t_out'], dtype=f32).reshape(16, 128).T)
    wlmkv = np.array(inputs['L0_attn_b_w_lm_kv'], dtype=f32)
    blmkv = np.array(inputs['L0_attn_b_b_lm_kv'], dtype=f32)
    wlmkv[:, 0:D] = wlmkv[:, 0:D][:, perm]
    blmkv[0:D] = blmkv[0:D][perm]
    wlmkv = np.ascontiguousarray(wlmkv)
    blmkv = np.ascontiguousarray(blmkv.reshape(32, 128).T)
    wout = np.ascontiguousarray(inputs['L0_attn_b_w_out'], dtype=f32)
    bout = np.ascontiguousarray(np.asarray(inputs['L0_attn_b_b_out'], dtype=f32).reshape(16, 128).T)
    wgate = np.ascontiguousarray(inputs['L0_attn_b_w_gate'], dtype=f32)
    bgate_v = np.asarray(inputs['L0_attn_b_b_gate'], dtype=f32)
    bgate = np.zeros((128, 1), dtype=f32)
    bgate[96:128, 0] = bgate_v

    gwT = np.ascontiguousarray(np.asarray(inputs['L0_ffn_gate_w'], dtype=f32).T)
    uwT = np.ascontiguousarray(np.asarray(inputs['L0_ffn_up_w'], dtype=f32).T)
    dwT = np.ascontiguousarray(np.asarray(inputs['L0_ffn_down_w'], dtype=f32).T)
    gsv = np.ascontiguousarray(np.asarray(inputs['L0_ffn_gate_s'], dtype=f32).reshape(44, 128).T)
    usv = np.ascontiguousarray(np.asarray(inputs['L0_ffn_up_s'], dtype=f32).reshape(44, 128).T)
    dsv = np.ascontiguousarray(np.asarray(inputs['L0_ffn_down_s'], dtype=f32).reshape(16, 128).T)

    ident = np.eye(128, dtype=f32)
    tri = (np.arange(128)[:, None] <= np.arange(128)[None, :]).astype(f32)
    si = np.arange(128)
    blktri = (((si[:, None] // 64) == (si[None, :] // 64)) &
              ((si[:, None] % 64) <= (si[None, :] % 64))).astype(f32)
    l2tm = np.ascontiguousarray(
        np.tile(((si[:, None] // 64) == (np.arange(8)[None, :] // 4)).astype(f32), (1, QT)))
    inv = 1.0 / 10000.0 ** (np.arange(0, 64, 2) / 64.0)

    x_full = np.asarray(inputs['L0_attn_a_x'], dtype=f32)

    in_maps = []
    for core in range(8):
        s = core // 2
        par = core % 2
        own_tiles = list(range(par, 16, 2)) if PHASE2 else list(range(16))
        own_tok = np.concatenate([np.arange(128 * t, 128 * t + 128) for t in own_tiles])
        own_blocks = [t for tt in own_tiles for t in (2 * tt, 2 * tt + 1)]

        xT = np.ascontiguousarray(x_full[s][own_tok].T)
        ang = own_tok[None, :].astype(np.float64) * inv[:, None]
        cos128 = np.ascontiguousarray(np.tile(np.cos(ang), (4, 1)).astype(f32))
        sin128 = np.ascontiguousarray(np.tile(np.sin(ang), (4, 1)).astype(f32))

        if PHASE2:
            ev = [t for tt in range(0, 16, 2) for t in (2 * tt, 2 * tt + 1)]
            od = [t for tt in range(1, 16, 2) for t in (2 * tt, 2 * tt + 1)]
            lm_blk = np.repeat(np.array(ev + od), M4)
        else:
            lm_blk = np.repeat(np.arange(32), M4)
        own_lm_blk = np.repeat(np.array(own_blocks), M4)
        lmask = (lm_blk[:, None] <= own_lm_blk[None, :]).astype(f32)
        gm = np.zeros((128, QT * 128), dtype=f32)
        for ii, tt in enumerate(own_tiles):
            q_blk = np.repeat(np.array([2 * tt, 2 * tt + 1]), 64)
            gm[:, ii * 128:(ii + 1) * 128] = (lm_blk[:, None] < q_blk[None, :]).astype(f32)

        if PHASE2:
            am = np.zeros((128, QT * 2 * 128), dtype=f32)
            for ii, tt in enumerate(own_tiles):
                for j, St in enumerate((2 * ii, 2 * ii + 1)):
                    if St < tt:
                        mblk = np.ones((128, 128), dtype=f32)
                    elif St == tt:
                        mblk = tri
                    else:
                        mblk = np.zeros((128, 128), dtype=f32)
                    am[:, (2 * ii + j) * 128:(2 * ii + j + 1) * 128] = mblk
            m_amask = am

        pqa = np.zeros((64, 128), dtype=f32)
        for h in range(H2):
            pqa[0:32, 4 * h:4 * h + 4] = pqT[32 * h:32 * h + 32, :]
            pqa[32:64, 4 * h:4 * h + 4] = pqT[1024 + 32 * h:1024 + 32 * h + 32, :]

        m = {
            'x0': xT,
            'wqkv_a': wqkv_a, 'bqkv_a': bqkv_a, 'wo_a': wo_a, 'bo_a': bo_a,
            'wqkv_b': wqkv_b, 'bqkv_b': bqkv_b, 'pqT': pqT,
            'wlmqkv': wlmqkv, 'blmqkv': blmqkv, 'wlmout': wlmout, 'blmout': blmout,
            'wl2tq': wl2tq, 'bl2tq': bl2tq, 'wl2tout': wl2tout, 'bl2tout': bl2tout,
            'wlmkv': wlmkv, 'blmkv': blmkv, 'wout': wout, 'bout': bout,
            'wgate': wgate, 'bgate': bgate,
            'gwT': gwT, 'uwT': uwT, 'dwT': dwT, 'gs': gsv, 'us': usv, 'ds': dsv,
            'c_ident': ident, 'c_tri': tri, 'c_blktri': blktri, 'c_l2tm': l2tm,
            'c_lmask': np.ascontiguousarray(lmask), 'c_gmask': np.ascontiguousarray(gm),
            'c_cos': cos128, 'c_sin': sin128, 'c_pqa': np.ascontiguousarray(pqa),
        }
        if PHASE2:
            m['c_amask'] = np.ascontiguousarray(m_amask)
        in_maps.append(m)
    return in_maps


def build_program():
    nc = bass.Bass("TRN2", target_bir_lowering=False)

    def din(name, shape, dt=F32):
        return nc.dram_tensor(name, shape, dt, kind="ExternalInput").ap()

    x0 = din('x0', [D, TOK])
    wqkv_a = din('wqkv_a', [D, 3 * D]); bqkv_a = din('bqkv_a', [128, 48])
    wo_a = din('wo_a', [D, D]); bo_a = din('bo_a', [128, 16])
    wqkv_b = din('wqkv_b', [D, 3 * D]); bqkv_b = din('bqkv_b', [128, 48])
    pqT = din('pqT', [D, M4])
    wlmqkv = din('wlmqkv', [D, 3 * D]); blmqkv = din('blmqkv', [128, 48])
    wlmout = din('wlmout', [D, D]); blmout = din('blmout', [128, 16])
    wl2tq = din('wl2tq', [D, D]); bl2tq = din('bl2tq', [128, 16])
    wl2tout = din('wl2tout', [D, D]); bl2tout = din('bl2tout', [128, 16])
    wlmkv = din('wlmkv', [D, 2 * D]); blmkv = din('blmkv', [128, 32])
    wout = din('wout', [D, D]); bout = din('bout', [128, 16])
    wgate = din('wgate', [D, H2]); bgate = din('bgate', [128, 1])
    gwT = din('gwT', [D, I5])
    uwT = din('uwT', [D, I5])
    dwT = din('dwT', [I5, D])
    gs = din('gs', [128, 44]); us = din('us', [128, 44]); ds = din('ds', [128, 16])
    c_ident = din('c_ident', [128, 128]); c_tri = din('c_tri', [128, 128])
    c_blktri = din('c_blktri', [128, 128]); c_l2tm = din('c_l2tm', [128, 8 * QT])
    c_lmask = din('c_lmask', [128, OWNLM]); c_gmask = din('c_gmask', [128, QT * 128])
    c_cos = din('c_cos', [128, TOK]); c_sin = din('c_sin', [128, TOK])
    c_pqa = din('c_pqa', [64, 128])
    c_amask = din('c_amask', [128, QT * 2 * 128]) if PHASE2 else None
    dbg = (nc.dram_tensor('dbg', [128, 4096], F32, kind="ExternalOutput").ap()
           if DEV_DBG else None)

    y = nc.dram_tensor('y', [D, TOK], F32, kind="ExternalOutput").ap()

    xres = nc.dram_tensor('xres', [D, TOK], F32).ap()
    scr_ao = nc.dram_tensor('scr_ao', [D, TOK], F32).ap()
    scr_q = nc.dram_tensor('scr_q', [D, TOK], F32).ap()
    ag1_in = nc.dram_tensor('ag1_in', [2 * D, TOK], F32).ap()
    ag1_out = nc.dram_tensor('ag1_out', [4 * D, TOK], F32).ap() if PHASE2 else None
    ag3_out = nc.dram_tensor('ag3_out', [4 * D, OWNLM], F32).ap() if PHASE2 else None
    ag4_out = nc.dram_tensor('ag4_out', [4 * D, OWNLM], F32).ap() if PHASE2 else None
    kvb = nc.dram_tensor('kvb', [2 * D, TOK], F32).ap()
    ag3_in = nc.dram_tensor('ag3_in', [2 * D, OWNLM], F32).ap()
    ag4_in = nc.dram_tensor('ag4_in', [2 * D, OWNLM], F32).ap()

    def d3(ap_, C):
        return ap_.rearrange("(c p) t -> p c t", p=128)

    def sb3(t_, C, w):
        return t_.rearrange("p (c t) -> p c t", c=C)

    import contextlib
    with contextlib.ExitStack() as _st:
        A1 = _st.enter_context(nc.sbuf_tensor('A1', [128, 8192], F32))
        A2 = _st.enter_context(nc.sbuf_tensor('A2', [128, 8192], F32))
        A3 = _st.enter_context(nc.sbuf_tensor('A3', [128, 3072], F32))
        A4 = _st.enter_context(nc.sbuf_tensor('A4', [128, 4096], F32))
        A5 = _st.enter_context(nc.sbuf_tensor('A5', [128, 8192], F32))
        A6 = _st.enter_context(nc.sbuf_tensor('A6', [128, 8192], F32))
        RW = _st.enter_context(nc.sbuf_tensor('RW', [128, 8192], mybir.dt.bfloat16))
        SM = _st.enter_context(nc.sbuf_tensor('SM', [128, 3072], F32))
        pqa_sb = _st.enter_context(nc.sbuf_tensor('pqa_sb', [64, 128], F32))
        gate_t_sb = _st.enter_context(nc.sbuf_tensor('gate_t_sb', [128, 32 * QT], F32))
        ident_sb = _st.enter_context(nc.sbuf_tensor('ident_sb', [128, 128], F32))
        tri_sb = _st.enter_context(nc.sbuf_tensor('tri_sb', [128, 128], F32))
        blktri_sb = _st.enter_context(nc.sbuf_tensor('blktri_sb', [128, 128], F32))
        l2tm_sb = _st.enter_context(nc.sbuf_tensor('l2tm_sb', [128, 8 * QT], F32))
        lmask_sb = _st.enter_context(nc.sbuf_tensor('lmask_sb', [128, OWNLM], F32))
        ones_c = _st.enter_context(nc.sbuf_tensor('ones_c', [128, 1], F32))
        ones_r = _st.enter_context(nc.sbuf_tensor('ones_r', [128, 128], F32))
        bias_sb = _st.enter_context(nc.sbuf_tensor('bias_sb', [128, 48], F32))
        bias2_sb = _st.enter_context(nc.sbuf_tensor('bias2_sb', [128, 1], F32))
        gsc_sb = _st.enter_context(nc.sbuf_tensor('gsc_sb', [128, 44], F32))
        usc_sb = _st.enter_context(nc.sbuf_tensor('usc_sb', [128, 44], F32))
        dsc_sb = _st.enter_context(nc.sbuf_tensor('dsc_sb', [128, 16], F32))
        PS0 = _st.enter_context(nc.psum_tensor('PS0', [128, 512], F32))
        PS1 = _st.enter_context(nc.psum_tensor('PS1', [128, 512], F32))
        PS2 = _st.enter_context(nc.psum_tensor('PS2', [128, 512], F32))
        PS3 = _st.enter_context(nc.psum_tensor('PS3', [128, 512], F32))
        PS4 = _st.enter_context(nc.psum_tensor('PS4', [128, 512], F32))
        PS5 = _st.enter_context(nc.psum_tensor('PS5', [128, 512], F32))
        PS6 = _st.enter_context(nc.psum_tensor('PS6', [128, 512], F32))
        PS7 = _st.enter_context(nc.psum_tensor('PS7', [128, 512], F32))
        _sq = _st.enter_context(nc.semaphore('sq'))
        _gq = _st.enter_context(nc.semaphore('gq'))
        _w0 = _st.enter_context(nc.semaphore('w0'))
        _w1 = _st.enter_context(nc.semaphore('w1'))
        _g0 = _st.enter_context(nc.semaphore('g0'))
        _g1 = _st.enter_context(nc.semaphore('g1'))
        _g2 = _st.enter_context(nc.semaphore('g2'))
        _g3 = _st.enter_context(nc.semaphore('g3'))
        _g4 = _st.enter_context(nc.semaphore('g4'))
        _w2 = _st.enter_context(nc.semaphore('w2'))
        _w3 = _st.enter_context(nc.semaphore('w3'))
        _f0 = _st.enter_context(nc.semaphore('f0'))
        _f1 = _st.enter_context(nc.semaphore('f1'))
        _f2 = _st.enter_context(nc.semaphore('f2'))
        _f3 = _st.enter_context(nc.semaphore('f3'))
        _tp = _st.enter_context(nc.semaphore('tp'))
        _vp = _st.enter_context(nc.semaphore('vp'))
        _sp = _st.enter_context(nc.semaphore('sp'))
        _cc = _st.enter_context(nc.semaphore('cc'))
        _bar = _st.enter_context(nc.semaphore('bar'))
        blk = _st.enter_context(nc.Block())
        b = Bld(nc, blk, {'sq': _sq, 'gq': _gq, 'tp': _tp, 'vp': _vp,
                          'sp': _sp, 'cc': _cc, 'bar': _bar,
                          'w0': _w0, 'w1': _w1, 'w2': _w2, 'w3': _w3,
                          'f0': _f0, 'f1': _f1, 'f2': _f2, 'f3': _f3,
                          'g0': _g0, 'g1': _g1, 'g2': _g2, 'g3': _g3, 'g4': _g4})
        WS2 = [_w2, _w3]
        wsn2 = ['w2', 'w3']
        FS = [_f0, _f1]
        fsn = ['f0', 'f1']
        FS2 = [_f2, _f3]
        fsn2 = ['f2', 'f3']
        b.scratch_ps = PS7
        b.scratch_sm = SM
        b.ones_c = ones_c
        b.ident = ident_sb
        WS = [_w0, _w1]
        GS = [_g0, _g1, _g2, _g3]
        wsn = ['w0', 'w1']
        gsn = ['g0', 'g1', 'g2', 'g3']
        st_m = SM[0:1, 0:TOK]
        tmp_a = SM[0:1, 2048:2048 + TC]
        tmp_b = SM[0:1, 2560:2560 + TC]
        st_r = SM[32:33, 0:TOK]
        cvec = SM[32:33, 2048:3072]
        rden = SM[64:65, 0:TOK]
        gate_sm = SM[96:128, 0:TOK]

        def prolog_q(q):
            b.dma(q, ident_sb[:], c_ident, 'sq')
            b.dma(q, tri_sb[:], c_tri, 'sq')
            b.dma(q, blktri_sb[:], c_blktri, 'sq')
            b.dma(q, l2tm_sb[:], c_l2tm, 'sq')
            b.dma(q, lmask_sb[:], c_lmask, 'sq')
        blk.sync(prolog_q)
        b.prolog_sq = b.c['sq']

        def prolog_v(v):
            v.memset(ones_c[:], 1.0)
            v.memset(ones_r[:], 1.0)
        blk.vector(prolog_v)
        b.barrier()

        def emit_ln_stats(xbuf, tc0):
            sp0, tp0, vp0 = b.c['sp'], b.c['tp'], b.c['vp']

            def s_body(s):
                for dc in range(DC):
                    if dc >= 2:
                        s.wait_ge(_tp, tp0 + dc - 1)
                    b.inc(s.activation(out=A3[:, (dc % 2) * TC:(dc % 2 + 1) * TC],
                                       in_=xbuf[:, dc * TC:(dc + 1) * TC],
                                       func=AF.Square), 'sp', 1)
            blk.scalar(s_body)

            def t_body(t):
                for dc in range(DC):
                    t.matmul(PS6[0:1, 0:TC], ones_c[:], xbuf[:, dc * TC:(dc + 1) * TC],
                             start=(dc == 0), stop=(dc == DC - 1))
                    t.wait_ge(_sp, sp0 + dc + 1)
                    b.inc(t.matmul(PS7[0:1, 0:TC], ones_c[:],
                                   A3[:, (dc % 2) * TC:(dc % 2 + 1) * TC],
                                   start=(dc == 0), stop=(dc == DC - 1)), 'tp', 1)
            blk.tensor(t_body)
            b.chk('tp', tp0 + DC)
            b.chk('sp', sp0 + DC)
            emit_stats_finish(tc0)

        def emit_stats_finish(tc0):
            tdone = b.c['tp']

            def v1(v):
                v.wait_ge(_tp, tdone)
                v.tensor_scalar_mul(out=st_m[:, tc0:tc0 + TC], in0=PS6[0:1, 0:TC], scalar1=1.0 / D)
                m1 = b.inc(v.tensor_scalar(out=tmp_a, in0=PS7[0:1, 0:TC], scalar1=1.0 / D,
                                           scalar2=EPS_LN, op0=AT.mult, op1=AT.add), 'vp', 1)
                v.wait_ge(_vp, m1)
                m2 = b.inc(v.tensor_mul(out=tmp_b, in0=st_m[:, tc0:tc0 + TC],
                                        in1=st_m[:, tc0:tc0 + TC]), 'vp', 1)
                v.wait_ge(_vp, m2)
                b.inc(v.tensor_sub(out=tmp_a, in0=tmp_a, in1=tmp_b), 'vp', 1)
            blk.vector(v1)
            vdone = b.c['vp']

            def s1(s):
                s.wait_ge(_vp, vdone)
                b.inc(s.activation(out=tmp_b, in_=tmp_a, func=AF.Sqrt), 'sp', 1)
            blk.scalar(s1)
            sdone = b.c['sp']

            def v2(v):
                v.wait_ge(_sp, sdone)
                b.inc(v.reciprocal(out=st_r[:, tc0:tc0 + TC], in_=tmp_b), 'vp', 1)
            blk.vector(v2)

        def emit_ln_apply(xbuf, xnbuf, tc0):
            vp0 = b.c['vp']

            def t1(t):
                t.wait_ge(_vp, vp0)
                t.matmul(PS4[:, 0:TC], ones_r[0:1, :], st_m[:, tc0:tc0 + TC], start=True, stop=True)
                b.inc(t.matmul(PS5[:, 0:TC], ones_r[32:33, :], st_r[:, tc0:tc0 + TC],
                               start=True, stop=True), 'tp', 1)
            blk.tensor(t1)
            tdone = b.c['tp']

            def v1(v):
                v.wait_ge(_tp, tdone)
                for dc in range(DC):
                    i = v.tensor_sub(out=xnbuf[:, dc * TC:(dc + 1) * TC],
                                     in0=xbuf[:, dc * TC:(dc + 1) * TC], in1=PS4[:, 0:TC])
                    if dc == DC - 1:
                        m1 = b.inc(i, 'vp', 1)
                        v.wait_ge(_vp, m1)
                for dc in range(DC):
                    i = v.tensor_mul(out=xnbuf[:, dc * TC:(dc + 1) * TC],
                                     in0=xnbuf[:, dc * TC:(dc + 1) * TC], in1=PS5[:, 0:TC])
                b.inc(i, 'vp', 1)
            blk.vector(v1)

        def emit_qkv_stage(W_ap, Bias_ap, nfc, dst, xsrc, ln_here, rope=False,
                           gate=False):
            def q_bias(q):
                b.dma(q, bias_sb[:, 0:nfc], Bias_ap, 'sq')
                if gate:
                    b.dma(q, bias2_sb[:, 0:1], bgate, 'sq')
            blk.sync(q_bias)

            rope_pairs = []
            rope_mem = {}
            if rope:
                for c in range(8):
                    rope_pairs.append((c, c + 8))
                for c in range(8):
                    rope_pairs.append((16 + c, 24 + c))
                for (fe, fo) in rope_pairs:
                    rope_mem[fe] = (fe, fo)
                    rope_mem[fo] = (fe, fo)
                order = [f for pr in rope_pairs for f in pr] + list(range(32, nfc))
            else:
                order = list(range(nfc))
            if gate:
                order = order + [nfc]

            total = nfc + (1 if gate else 0)
            for tc in range(NTC):
                tc0 = tc * TC

                def q_x(q):
                    b.dma(q, sb3(A1[:], DC, TC)[:, :, 0:TC], d3(xsrc, DC)[:, :, tc0:tc0 + TC], 'sq')
                blk.sync(q_x)
                xld = b.c['sq']

                def tw(t):
                    t.wait_ge(_sq, xld)
                blk.tensor(tw)

                def sw(s):
                    s.wait_ge(_sq, xld)
                blk.scalar(sw)
                if ln_here:
                    emit_ln_stats(A1, tc0)
                emit_ln_apply(A1, A2, tc0)
                xn_ready = b.c['vp']

                tp0 = b.c['tp']
                sp0 = b.c['sp']
                vp0 = b.c['vp']
                wb0 = [b.c['w0'], b.c['w1']]
                gb0 = [b.c['g0'], b.c['g1'], b.c['g2'], b.c['g3']]
                g40 = b.c['g4']

                def pair_idx(fc):
                    pr = rope_mem[fc]
                    return rope_pairs.index(pr)

                def w_mark(idx):
                    return wb0[idx % 2] + 16 * (idx // 2 + 1)

                pos = {fc: idx for idx, fc in enumerate(order)}

                slot_use = {0: [], 1: [], 2: [], 3: []}
                for idx, fc in enumerate(order):
                    if gate and fc == nfc:
                        continue
                    if rope and fc in rope_mem:
                        continue
                    slot_use[idx % 4].append(idx)

                def g_mark(idx):
                    sl = idx % 4
                    k = slot_use[sl].index(idx) + 1
                    return gb0[sl] + 16 * k

                def g4_mark(pi):
                    return g40 + 32 * (pi + 1)

                def q_w(q):
                    for idx, fc in enumerate(order):
                        bb = idx % 2
                        if idx >= 2:
                            q.wait_ge(_tp, tp0 + idx - 1)
                        if gate and fc == nfc:
                            b.dma(q, sb3(A4[:, bb * 2048:bb * 2048 + DC * 32], DC, 32),
                                  d3(wgate, DC), wsn[bb])
                        else:
                            b.dma(q, sb3(A4[:, bb * 2048:(bb + 1) * 2048], DC, 128),
                                  d3(W_ap, DC)[:, :, fc * 128:(fc + 1) * 128], wsn[bb])
                blk.sync(q_w)

                def t_mm(t):
                    t.wait_ge(_vp, xn_ready)
                    for idx, fc in enumerate(order):
                        bb = idx % 2
                        is_gate = gate and fc == nfc
                        wid = 32 if is_gate else 128
                        ps = PS0 if bb == 0 else PS1
                        t.wait_ge(WS[bb], w_mark(idx))
                        if idx >= 2:
                            t.wait_ge(_sp, sp0 + idx - 1)
                        for dc in range(DC):
                            i = t.matmul(ps[0:wid, 0:TC],
                                         A4[:, bb * 2048 + dc * wid:bb * 2048 + (dc + 1) * wid],
                                         A2[:, dc * TC:(dc + 1) * TC],
                                         start=(dc == 0), stop=(dc == DC - 1))
                        b.inc(i, 'tp', 1)
                blk.tensor(t_mm)
                b.chk('tp', tp0 + total)

                def s_evac(s):
                    for idx, fc in enumerate(order):
                        ps = PS0 if idx % 2 == 0 else PS1
                        slot = idx % 4
                        s.wait_ge(_tp, tp0 + idx + 1)
                        if idx >= 4:
                            pidx = idx - 4
                            pfc = order[pidx]
                            if rope and pfc in rope_mem:
                                s.wait_ge(_vp, vp0 + 2 * (pair_idx(pfc) + 1))
                            elif not (gate and pfc == nfc):
                                s.wait_ge(GS[pidx % 4], g_mark(pidx))
                        if gate and fc == nfc:
                            i = s.activation(out=gate_sm[:, tc0:tc0 + TC], in_=ps[0:32, 0:TC],
                                             func=AF.Sigmoid, bias=bias2_sb[96:128, 0:1])
                        else:
                            i = s.activation(out=A3[:, slot * TC:(slot + 1) * TC],
                                             in_=ps[0:128, 0:TC],
                                             func=AF.Identity, bias=bias_sb[:, fc:fc + 1])
                        b.inc(i, 'sp', 1)
                blk.scalar(s_evac)
                b.chk('sp', sp0 + total)

                if rope:
                    def v_rope(v):
                        for pi, (fe, fo) in enumerate(rope_pairs):
                            v.wait_ge(_sp, sp0 + max(pos[fe], pos[fo]) + 1)
                            if pi >= 1:
                                v.wait_ge(_g4, g4_mark(pi - 1))
                            xe = A3[:, (pos[fe] % 4) * TC:(pos[fe] % 4 + 1) * TC]
                            xo = A3[:, (pos[fo] % 4) * TC:(pos[fo] % 4 + 1) * TC]
                            oe = A5[:, 0:TC]
                            oo = A5[:, TC:2 * TC]
                            t1_ = A5[:, 2 * TC:3 * TC]
                            t2_ = A5[:, 3 * TC:4 * TC]
                            cosT = A6[:, tc0:tc0 + TC]
                            sinT = A6[:, 4096 + tc0:4096 + tc0 + TC]
                            v.tensor_mul(out=oe, in0=xe, in1=cosT)
                            v.tensor_mul(out=t1_, in0=xo, in1=sinT)
                            v.tensor_mul(out=oo, in0=xe, in1=sinT)
                            m = b.inc(v.tensor_mul(out=t2_, in0=xo, in1=cosT), 'vp', 1)
                            v.wait_ge(_vp, m)
                            v.tensor_sub(out=oe, in0=oe, in1=t1_)
                            b.inc(v.tensor_add(out=oo, in0=oo, in1=t2_), 'vp', 1)
                    blk.vector(v_rope)
                    b.chk('vp', vp0 + 2 * len(rope_pairs))

                def g_st(g):
                    for idx, fc in enumerate(order):
                        if gate and fc == nfc:
                            continue
                        if rope and fc in rope_mem:
                            pr = rope_mem[fc]
                            if fc != pr[1]:
                                continue
                            g.wait_ge(_vp, vp0 + 2 * (pair_idx(fc) + 1))
                            da_e, r0_e = dst(pr[0])
                            da_o, r0_o = dst(pr[1])
                            b.dma(g, d3(da_e, 0)[:, r0_e // 128, tc0:tc0 + TC], A5[:, 0:TC], 'g4')
                            b.dma(g, d3(da_o, 0)[:, r0_o // 128, tc0:tc0 + TC], A5[:, TC:2 * TC], 'g4')
                        else:
                            slot = idx % 4
                            g.wait_ge(_sp, sp0 + idx + 1)
                            da, r0 = dst(fc)
                            b.dma(g, d3(da, 0)[:, r0 // 128, tc0:tc0 + TC],
                                  A3[:, slot * TC:(slot + 1) * TC], gsn[slot])
                blk.gpsimd(g_st)
                b.barrier()

        def emit_attn_a():
            SEQ = L
            NS_T = SEQ // 128
            koff, voff, qoff2, vtoff2 = 0, SEQ, 2 * SEQ, 2 * SEQ + TOK

            def nS(i):
                return 2 * i + 2 if PHASE2 else i + 1

            def stcol(St):
                if PHASE2:
                    return (St // 2) * 128 + (St % 2) * 1024
                return St * 128

            if PHASE2:
                def q_am(q):
                    b.dma(q, A6[:, 4096:4096 + QT * 256], c_amask, 'sq')
                blk.sync(q_am)
                b.barrier()

            for h in range(NH):
                slot = A1 if h % 2 == 0 else A5

                def q_head(q, h=h, slot=slot):
                    if PHASE2:
                        q.wait_ge(_cc, 1)
                        b.dma(q, slot[:, koff:koff + TOK], ag1_out[h * 128:(h + 1) * 128, :], 'sq')
                        b.dma(q, slot[:, koff + TOK:koff + 2 * TOK],
                              ag1_out[2 * D + h * 128:2 * D + (h + 1) * 128, :], 'sq')
                        b.dma(q, slot[:, voff:voff + TOK],
                              ag1_out[D + h * 128:D + (h + 1) * 128, :], 'sq')
                        b.dma(q, slot[:, voff + TOK:voff + 2 * TOK],
                              ag1_out[3 * D + h * 128:3 * D + (h + 1) * 128, :], 'sq')
                    else:
                        b.dma(q, slot[:, koff:koff + SEQ], ag1_in[h * 128:(h + 1) * 128, :], 'sq')
                        b.dma(q, slot[:, voff:voff + SEQ], ag1_in[D + h * 128:D + (h + 1) * 128, :], 'sq')
                    b.dma(q, slot[:, qoff2:qoff2 + TOK], scr_q[h * 128:(h + 1) * 128, :], 'sq')
                blk.sync(q_head)
                ld_mark = b.c['sq']

                tp0 = b.c['tp']
                vp0 = b.c['vp']

                def t_vt(t, slot=slot, ld_mark=ld_mark, tp0=tp0, vp0=vp0):
                    t.wait_ge(_sq, ld_mark)
                    for St in range(NS_T):
                        ps = PS2 if St % 2 == 0 else PS3
                        if St >= 2:
                            t.wait_ge(_vp, vp0 + St - 1)
                        b.inc(t.transpose(ps[0:128, 0:128],
                                          slot[:, voff + stcol(St):voff + stcol(St) + 128],
                                          ident_sb[:]), 'tp', 1)
                blk.tensor(t_vt)

                def v_vt(v, slot=slot, tp0=tp0):
                    for St in range(NS_T):
                        ps = PS2 if St % 2 == 0 else PS3
                        v.wait_ge(_tp, tp0 + St + 1)
                        b.inc(v.tensor_copy(out=slot[:, vtoff2 + St * 128:vtoff2 + (St + 1) * 128],
                                            in_=ps[0:128, 0:128]), 'vp', 1)
                blk.vector(v_vt)
                b.chk('tp', tp0 + NS_T)
                b.chk('vp', vp0 + NS_T)
                vt_done = b.c['vp']

                T0 = b.c['tp']
                S0 = b.c['sp']
                V0 = b.c['vp']

                def t_off(i):
                    return T0 + sum(2 * nS(j) + 1 for j in range(i))

                def sc_mark(i, St):
                    return t_off(i) + 2 * St + 1

                def od_mark(i, St):
                    return t_off(i) + 2 * St + 2

                def bc_mark(i):
                    return t_off(i) + 2 * nS(i) + 1

                def s_off(i):
                    return S0 + sum(nS(j) + 1 for j in range(i))

                def exp_mark(i, St):
                    return s_off(i) + St + 1

                def cpy_mark(i):
                    return s_off(i) + nS(i) + 1

                MP = 2 if PHASE2 else 1

                def mask_mark(i, j=0):
                    return V0 + (MP + 2) * i + j + 1

                def rden_mark(i):
                    return V0 + (MP + 2) * i + MP + 1

                def outev_mark(i):
                    return V0 + (MP + 2) * i + MP + 2

                def masked_tiles(i):
                    if PHASE2:
                        return (nS(i) - 2, nS(i) - 1)
                    return (nS(i) - 1,)

                def t_all(t, h=h, slot=slot, vt_done=vt_done):
                    t.wait_ge(_vp, vt_done)
                    for i in range(QT):
                        n = nS(i)
                        out_ps = PS2 if i % 2 == 0 else PS3
                        den_ps = PS6 if i % 2 == 0 else PS7
                        if i >= 2:
                            t.wait_ge(_vp, outev_mark(i - 2))
                        for St in range(n):
                            ps = PS0 if St % 2 == 0 else PS1
                            if St >= 2:
                                t.wait_ge(_sp, exp_mark(i, St - 2))
                            b.inc(t.matmul(ps[0:128, 0:128],
                                           slot[:, koff + stcol(St):koff + stcol(St) + 128],
                                           slot[:, qoff2 + i * 128:qoff2 + (i + 1) * 128],
                                           start=True, stop=True), 'tp', 1)
                            if St in masked_tiles(i):
                                t.wait_ge(_vp, mask_mark(i, St - masked_tiles(i)[0]))
                            else:
                                t.wait_ge(_sp, exp_mark(i, St))
                            e_ap = A2[:, (St % 4) * 128:(St % 4 + 1) * 128]
                            t.matmul(out_ps[0:128, 0:128],
                                     slot[:, vtoff2 + St * 128:vtoff2 + (St + 1) * 128],
                                     e_ap, start=(St == 0), stop=(St == n - 1))
                            b.inc(t.matmul(den_ps[0:1, 0:128], ones_c[:], e_ap,
                                           start=(St == 0), stop=(St == n - 1)), 'tp', 1)
                        t.wait_ge(_vp, rden_mark(i))
                        if i >= 1:
                            t.wait_ge(_sp, cpy_mark(i - 1))
                        b.inc(t.matmul(PS4[:, 0:128], ones_r[64:65, :],
                                       rden[:, i * 128:(i + 1) * 128], start=True, stop=True),
                              'tp', 1)
                blk.tensor(t_all)

                def s_all(s):
                    for i in range(QT):
                        n = nS(i)
                        for St in range(n):
                            ps = PS0 if St % 2 == 0 else PS1
                            s.wait_ge(_tp, sc_mark(i, St))
                            if St >= 4:
                                s.wait_ge(_tp, od_mark(i, St - 4))
                            elif i >= 1:
                                s.wait_ge(_tp, od_mark(i - 1, nS(i - 1) - 1))
                            b.inc(s.activation(out=A2[:, (St % 4) * 128:(St % 4 + 1) * 128],
                                               in_=ps[0:128, 0:128], func=AF.Exp, scale=SC_A),
                                  'sp', 1)
                        s.wait_ge(_tp, bc_mark(i))
                        if i >= 2:
                            s.wait_ge(_vp, outev_mark(i - 2))
                        b.inc(s.activation(out=A3[:, (i % 2) * 128:(i % 2 + 1) * 128],
                                           in_=PS4[:, 0:128], func=AF.Copy), 'sp', 1)
                blk.scalar(s_all)

                def v_all(v, h=h):
                    for i in range(QT):
                        n = nS(i)
                        mts = masked_tiles(i)
                        for St in mts:
                            v.wait_ge(_sp, exp_mark(i, St))
                            if PHASE2:
                                mask_ap = A6[:, 4096 + (2 * i + (St - mts[0])) * 128:
                                             4096 + (2 * i + (St - mts[0]) + 1) * 128]
                            else:
                                mask_ap = tri_sb[:]
                            b.inc(v.tensor_mul(out=A2[:, (St % 4) * 128:(St % 4 + 1) * 128],
                                               in0=A2[:, (St % 4) * 128:(St % 4 + 1) * 128],
                                               in1=mask_ap), 'vp', 1)
                        v.wait_ge(_tp, od_mark(i, n - 1))
                        den_ps = PS6 if i % 2 == 0 else PS7
                        b.inc(v.reciprocal(out=rden[:, i * 128:(i + 1) * 128],
                                           in_=den_ps[0:1, 0:128]), 'vp', 1)
                        v.wait_ge(_sp, cpy_mark(i))
                        out_ps = PS2 if i % 2 == 0 else PS3
                        b.inc(v.tensor_mul(out=A6[:, i * 128:(i + 1) * 128],
                                           in0=out_ps[0:128, 0:128],
                                           in1=A3[:, (i % 2) * 128:(i % 2 + 1) * 128]), 'vp', 1)
                blk.vector(v_all)

                nmm = sum(2 * nS(j) + 1 for j in range(QT))
                b.chk('tp', T0 + nmm)
                b.chk('sp', S0 + sum(nS(j) + 1 for j in range(QT)))
                b.chk('vp', V0 + (MP + 2) * QT)

                st_mark = b.c['vp']

                def g_out(g, h=h, st_mark=st_mark):
                    g.wait_ge(_vp, st_mark)
                    b.dma(g, scr_ao[h * 128:(h + 1) * 128, :], A6[:, 0:TOK], 'gq')
                blk.gpsimd(g_out)
                b.barrier()

        def emit_proj_res(src_ap, W_ap, Bias_ap, xold_ap, dst_ap, want_stats):
            def q_bias(q):
                b.dma(q, bias_sb[:, 0:16], Bias_ap, 'sq')
            blk.sync(q_bias)

            for tc in range(NTC):
                tc0 = tc * TC

                def q_in(q):
                    b.dma(q, sb3(A1[:], DC, TC)[:, :, 0:TC], d3(src_ap, DC)[:, :, tc0:tc0 + TC], 'sq')
                    b.dma(q, sb3(A5[:], DC, TC)[:, :, 0:TC], d3(xold_ap, DC)[:, :, tc0:tc0 + TC], 'sq')
                blk.sync(q_in)
                xld = b.c['sq']

                K_ = 2 if want_stats else 1
                tp0 = b.c['tp']
                vp0 = b.c['vp']
                sp0 = b.c['sp']
                wb0 = [b.c['w0'], b.c['w1']]
                gb0 = [b.c['g0'], b.c['g1'], b.c['g2'], b.c['g3']]

                def q_w(q):
                    for fc in range(DC):
                        bb = fc % 2
                        if fc >= 2:
                            q.wait_ge(_tp, tp0 + K_ * (fc - 1))
                        b.dma(q, sb3(A4[:, bb * 2048:(bb + 1) * 2048], DC, 128),
                              d3(W_ap, DC)[:, :, fc * 128:(fc + 1) * 128], wsn[bb])
                blk.sync(q_w)

                def t_mm(t):
                    t.wait_ge(_sq, xld)
                    for fc in range(DC):
                        bb = fc % 2
                        ps = PS0 if bb == 0 else PS1
                        t.wait_ge(WS[bb], wb0[bb] + 16 * (fc // 2 + 1))
                        if fc >= 2:
                            t.wait_ge(_vp, vp0 + fc - 1)
                        for dc in range(DC):
                            i = t.matmul(ps[0:128, 0:TC],
                                         A4[:, bb * 2048 + dc * 128:bb * 2048 + (dc + 1) * 128],
                                         A1[:, dc * TC:(dc + 1) * TC],
                                         start=(dc == 0), stop=(dc == DC - 1))
                        b.inc(i, 'tp', 1)
                        if want_stats:
                            t.wait_ge(_vp, vp0 + fc + 1)
                            t.matmul(PS6[0:1, 0:TC], ones_c[:],
                                     A2[:, (fc % 4) * TC:(fc % 4 + 1) * TC],
                                     start=(fc == 0), stop=(fc == DC - 1))
                            t.wait_ge(_sp, sp0 + fc + 1)
                            b.inc(t.matmul(PS7[0:1, 0:TC], ones_c[:],
                                           A3[:, (fc % 2) * TC:(fc % 2 + 1) * TC],
                                           start=(fc == 0), stop=(fc == DC - 1)), 'tp', 1)
                blk.tensor(t_mm)

                def v_evac(v):
                    for fc in range(DC):
                        ps = PS0 if fc % 2 == 0 else PS1
                        v.wait_ge(_tp, tp0 + K_ * fc + 1)
                        if fc >= 4:
                            v.wait_ge(GS[fc % 4], gb0[fc % 4] + 16 * (fc // 4))
                            if want_stats:
                                v.wait_ge(_sp, sp0 + fc - 3)
                        b.inc(v.scalar_tensor_tensor(
                            out=A2[:, (fc % 4) * TC:(fc % 4 + 1) * TC],
                            in0=ps[0:128, 0:TC], scalar=bias_sb[:, fc:fc + 1],
                            in1=A5[:, fc * TC:(fc + 1) * TC], op0=AT.add, op1=AT.add), 'vp', 1)
                blk.vector(v_evac)

                if want_stats:
                    def s_sq(s):
                        for fc in range(DC):
                            s.wait_ge(_vp, vp0 + fc + 1)
                            if fc >= 2:
                                s.wait_ge(_tp, tp0 + 2 * fc)
                            b.inc(s.activation(out=A3[:, (fc % 2) * TC:(fc % 2 + 1) * TC],
                                               in_=A2[:, (fc % 4) * TC:(fc % 4 + 1) * TC],
                                               func=AF.Square), 'sp', 1)
                    blk.scalar(s_sq)

                def g_st(g):
                    for fc in range(DC):
                        g.wait_ge(_vp, vp0 + fc + 1)
                        b.dma(g, d3(dst_ap, DC)[:, fc, tc0:tc0 + TC],
                              A2[:, (fc % 4) * TC:(fc % 4 + 1) * TC], gsn[fc % 4])
                blk.gpsimd(g_st)

                b.chk('tp', tp0 + K_ * DC)
                b.chk('vp', vp0 + DC)
                if want_stats:
                    b.chk('sp', sp0 + DC)
                    emit_stats_finish(tc0)
                b.barrier()

        def emit_ffn(xold_ap, dst_ap, want_stats):
            GRP = 4
            NG = ICH // GRP
            BF16 = mybir.dt.bfloat16
            xh_bf = A1[:, 0:4096].bitcast(BF16)
            xl_bf = A1[:, 4096:8192].bitcast(BF16)
            hh_bf = A5[:, 0:1024].bitcast(BF16)
            hl_bf = A5[:, 1024:2048].bitcast(BF16)
            dw_bf = A4[:, 0:512].bitcast(BF16)

            def q_sc(q):
                b.dma(q, gsc_sb[:, 0:44], gs, 'sq')
                b.dma(q, usc_sb[:, 0:44], us, 'sq')
                b.dma(q, dsc_sb[:, 0:16], ds, 'sq')
            blk.sync(q_sc)

            for tc in range(NTC):
                tc0 = tc * TC

                def q_x(q):
                    b.dma(q, sb3(A6[:], DC, TC)[:, :, 0:TC], d3(xold_ap, DC)[:, :, tc0:tc0 + TC], 'sq')
                blk.sync(q_x)
                xld = b.c['sq']

                def tw(t):
                    t.wait_ge(_sq, xld)
                blk.tensor(tw)
                emit_ln_apply(A6, A2, tc0)
                xn_ready = b.c['vp']

                def v_split(v):
                    v.wait_ge(_vp, xn_ready)
                    for dc in range(DC):
                        m = b.inc(v.tensor_copy(out=xh_bf[:, dc * TC:(dc + 1) * TC],
                                                in_=A2[:, dc * TC:(dc + 1) * TC]), 'vp', 1)
                        v.wait_ge(_vp, m)
                        b.inc(v.tensor_sub(out=xl_bf[:, dc * TC:(dc + 1) * TC],
                                           in0=A2[:, dc * TC:(dc + 1) * TC],
                                           in1=xh_bf[:, dc * TC:(dc + 1) * TC]), 'vp', 1)
                blk.vector(v_split)
                split_done = b.c['vp']

                def xh_r(dc):
                    return xh_bf[:, dc * TC:(dc + 1) * TC]

                def xl_r(dc):
                    return xl_bf[:, dc * TC:(dc + 1) * TC]

                tp0 = b.c['tp']
                sp0 = b.c['sp']
                vp0 = b.c['vp']
                wb0 = [b.c['f0'], b.c['f1']]
                wb20 = [b.c['f2'], b.c['f3']]

                gu_mark = {}
                dn_tp_mark = {}
                tpc = tp0
                for ic in range(ICH):
                    tpc += 1; gu_mark[(ic, 0)] = tpc
                    tpc += 1; gu_mark[(ic, 1)] = tpc
                    if ic % GRP == GRP - 1:
                        grp = ic // GRP
                        for fcp in range(DC):
                            tpc += 1; dn_tp_mark[(grp, fcp)] = tpc
                tp_end = tpc

                hl_mark = {}
                dn_vp_mark = {}
                vpc = vp0
                for ic in range(ICH):
                    vpc += 5; hl_mark[ic] = vpc
                    if ic % GRP == GRP - 1:
                        grp = ic // GRP
                        for fcp in range(DC):
                            vpc += 1; dn_vp_mark[(grp, fcp)] = vpc
                vp_end = vpc

                def silu_mark(ic, which):
                    return sp0 + 2 * ic + which + 1

                def w_mark_gu(ic):
                    return wb0[ic % 2] + 32 * (ic // 2 + 1)

                def w_mark_dn(grp, fcp):
                    k = grp * DC + fcp
                    return wb20[k % 2] + 16 * (k // 2 + 1)

                def q_w(g):
                    for ic in range(ICH):
                        bb = ic % 2
                        if ic >= 2:
                            g.wait_ge(_tp, gu_mark[(ic - 2, 1)])
                        b.dma(g, sb3(RW[:, bb * 2048:(bb + 1) * 2048], DC, 128),
                              d3(gwT, DC)[:, :, ic * 128:(ic + 1) * 128], fsn[bb])
                        b.dma(g, sb3(RW[:, 4096 + bb * 2048:4096 + (bb + 1) * 2048], DC, 128),
                              d3(uwT, DC)[:, :, ic * 128:(ic + 1) * 128], fsn[bb])
                        if ic % GRP == GRP - 1:
                            grp = ic // GRP
                            for fcp in range(DC):
                                k = grp * DC + fcp
                                bb2 = k % 2
                                if k >= 2:
                                    kk = k - 2
                                    g.wait_ge(_tp, dn_tp_mark[(kk // DC, kk % DC)])
                                b.dma(g, sb3(dw_bf[:, bb2 * 512:(bb2 + 1) * 512], GRP, 128),
                                      dwT.rearrange("(g p) t -> p g t", p=128)[
                                          :, 4 * grp:4 * grp + GRP, fcp * 128:(fcp + 1) * 128],
                                      fsn2[bb2])
                blk.gpsimd(q_w)

                def t_mm(t):
                    t.wait_ge(_vp, split_done)
                    for ic in range(ICH):
                        bb = ic % 2
                        t.wait_ge(FS[bb], w_mark_gu(ic))
                        if ic >= 1:
                            t.wait_ge(_sp, silu_mark(ic - 1, 0))
                        for p in range(2):
                            for dc in range(DC):
                                i = t.matmul(PS0[0:128, 0:TC],
                                             RW[:, bb * 2048 + dc * 128:bb * 2048 + (dc + 1) * 128],
                                             xh_r(dc) if p == 0 else xl_r(dc),
                                             start=(p == 0 and dc == 0),
                                             stop=(p == 1 and dc == DC - 1))
                        b.inc(i, 'tp', 1)
                        if ic >= 1:
                            t.wait_ge(_sp, silu_mark(ic - 1, 1))
                        for p in range(2):
                            for dc in range(DC):
                                i = t.matmul(PS1[0:128, 0:TC],
                                             RW[:, 4096 + bb * 2048 + dc * 128:4096 + bb * 2048 + (dc + 1) * 128],
                                             xh_r(dc) if p == 0 else xl_r(dc),
                                             start=(p == 0 and dc == 0),
                                             stop=(p == 1 and dc == DC - 1))
                        b.inc(i, 'tp', 1)
                        if ic % GRP == GRP - 1:
                            grp = ic // GRP
                            t.wait_ge(_vp, hl_mark[ic])
                            for fcp in range(DC):
                                k = grp * DC + fcp
                                bb2 = k % 2
                                ps = PS2 if fcp % 2 == 0 else PS3
                                t.wait_ge(FS2[bb2], w_mark_dn(grp, fcp))
                                if k >= 2:
                                    kk = k - 2
                                    t.wait_ge(_vp, dn_vp_mark[(kk // DC, kk % DC)])
                                for p in range(2):
                                    for icg in range(GRP):
                                        hsrc = (hh_bf[:, icg * TC:(icg + 1) * TC] if p == 0
                                                else hl_bf[:, icg * TC:(icg + 1) * TC])
                                        i = t.matmul(ps[0:128, 0:TC],
                                                     dw_bf[:, bb2 * 512 + icg * 128:bb2 * 512 + (icg + 1) * 128],
                                                     hsrc,
                                                     start=(p == 0 and icg == 0),
                                                     stop=(p == 1 and icg == GRP - 1))
                                b.inc(i, 'tp', 1)
                blk.tensor(t_mm)
                b.chk('tp', tp_end)

                def s_ev(s):
                    for ic in range(ICH):
                        s.wait_ge(_tp, gu_mark[(ic, 0)])
                        if ic >= 1:
                            s.wait_ge(_vp, vp0 + 5 * (ic - 1) + 1 + 16 * ((ic - 1) // GRP))
                        b.inc(s.activation(out=A3[:, 4 * TC:5 * TC], in_=PS0[0:128, 0:TC],
                                           func=AF.Silu, scale=gsc_sb[:, ic:ic + 1]), 'sp', 1)
                        s.wait_ge(_tp, gu_mark[(ic, 1)])
                        b.inc(s.activation(out=A3[:, 5 * TC:6 * TC], in_=PS1[0:128, 0:TC],
                                           func=AF.Copy, scale=usc_sb[:, ic:ic + 1]), 'sp', 1)
                blk.scalar(s_ev)
                b.chk('sp', sp0 + 2 * ICH)

                def v_h(v):
                    for ic in range(ICH):
                        icg = ic % GRP
                        v.wait_ge(_sp, silu_mark(ic, 1))
                        if ic >= 1:
                            v.wait_ge(_vp, hl_mark[ic - 1])
                        if ic >= GRP:
                            v.wait_ge(_tp, dn_tp_mark[(ic // GRP - 1, DC - 1)])
                        hbuf = A3[:, 0:TC]
                        m = b.inc(v.tensor_mul(out=hbuf, in0=A3[:, 4 * TC:5 * TC],
                                               in1=A3[:, 5 * TC:6 * TC]), 'vp', 1)
                        v.wait_ge(_vp, m)
                        m = b.inc(v.tensor_copy(out=hh_bf[:, icg * TC:(icg + 1) * TC],
                                                in_=hbuf), 'vp', 1)
                        v.wait_ge(_vp, m)
                        m = b.inc(v.tensor_sub(out=hl_bf[:, icg * TC:(icg + 1) * TC],
                                               in0=hbuf,
                                               in1=hh_bf[:, icg * TC:(icg + 1) * TC]), 'vp', 1)
                        v.wait_ge(_vp, m)
                        b.inc(v.memset(SM[64:65, 3068:3069], 0.0), 'vp', 1)
                        b.inc(v.memset(SM[64:65, 3067:3068], 0.0), 'vp', 1)
                        if ic % GRP == GRP - 1:
                            grp = ic // GRP
                            for fcp in range(DC):
                                ps = PS2 if fcp % 2 == 0 else PS3
                                v.wait_ge(_tp, dn_tp_mark[(grp, fcp)])
                                if grp >= 1:
                                    v.wait_ge(_vp, dn_vp_mark[(grp - 1, fcp)])
                                if grp == 0:
                                    i = v.tensor_copy(out=A6[:, fcp * TC:(fcp + 1) * TC],
                                                      in_=ps[0:128, 0:TC])
                                else:
                                    i = v.tensor_add(out=A6[:, fcp * TC:(fcp + 1) * TC],
                                                     in0=A6[:, fcp * TC:(fcp + 1) * TC],
                                                     in1=ps[0:128, 0:TC])
                                b.inc(i, 'vp', 1)
                blk.vector(v_h)
                b.chk('vp', vp_end)
                acc_done = b.c['vp']

                def q_xold(q):
                    q.wait_ge(_vp, acc_done)
                    b.dma(q, sb3(A5[:], DC, TC)[:, :, 0:TC], d3(xold_ap, DC)[:, :, tc0:tc0 + TC], 'sq')
                blk.sync(q_xold)
                xold_ld = b.c['sq']

                fin_vp0 = b.c['vp']
                fin_sp0 = b.c['sp']
                fin_tp0 = b.c['tp']
                gb0 = [b.c['g0'], b.c['g1'], b.c['g2'], b.c['g3']]

                def v_fin(v):
                    v.wait_ge(_sq, xold_ld)
                    v.wait_ge(_vp, acc_done)
                    for fcp in range(DC):
                        if fcp >= 4:
                            v.wait_ge(GS[fcp % 4], gb0[fcp % 4] + 16 * (fcp // 4))
                            if want_stats:
                                v.wait_ge(_sp, fin_sp0 + fcp - 3)
                                v.wait_ge(_tp, fin_tp0 + fcp - 3)
                        b.inc(v.scalar_tensor_tensor(
                            out=A2[:, (fcp % 4) * TC:(fcp % 4 + 1) * TC],
                            in0=A6[:, fcp * TC:(fcp + 1) * TC], scalar=dsc_sb[:, fcp:fcp + 1],
                            in1=A5[:, fcp * TC:(fcp + 1) * TC], op0=AT.mult, op1=AT.add), 'vp', 1)
                blk.vector(v_fin)

                def g_fin2(g):
                    for fcp in range(DC):
                        g.wait_ge(_vp, fin_vp0 + fcp + 1)
                        b.dma(g, d3(dst_ap, DC)[:, fcp, tc0:tc0 + TC],
                              A2[:, (fcp % 4) * TC:(fcp % 4 + 1) * TC], gsn[fcp % 4])
                blk.gpsimd(g_fin2)

                if want_stats:
                    def s_sq2(s):
                        for fcp in range(DC):
                            s.wait_ge(_vp, fin_vp0 + fcp + 1)
                            if fcp >= 2:
                                s.wait_ge(_tp, fin_tp0 + fcp - 1)
                            b.inc(s.activation(out=A3[:, (fcp % 2) * TC:(fcp % 2 + 1) * TC],
                                               in_=A2[:, (fcp % 4) * TC:(fcp % 4 + 1) * TC],
                                               func=AF.Square), 'sp', 1)
                    blk.scalar(s_sq2)

                    def t_st(t):
                        for fcp in range(DC):
                            t.wait_ge(_vp, fin_vp0 + fcp + 1)
                            t.matmul(PS6[0:1, 0:TC], ones_c[:],
                                     A2[:, (fcp % 4) * TC:(fcp % 4 + 1) * TC],
                                     start=(fcp == 0), stop=(fcp == DC - 1))
                            t.wait_ge(_sp, fin_sp0 + fcp + 1)
                            b.inc(t.matmul(PS7[0:1, 0:TC], ones_c[:],
                                           A3[:, (fcp % 2) * TC:(fcp % 2 + 1) * TC],
                                           start=(fcp == 0), stop=(fcp == DC - 1)), 'tp', 1)
                    blk.tensor(t_st)
                    emit_stats_finish(tc0)
                b.barrier()

        def emit_small_proj(W_ap, Bias_ap, nfc, rhs, dstf, mode='copy', base=None,
                            store=None, bias_n=None):
            bias_n = bias_n or nfc

            def q_bias(q):
                b.dma(q, bias_sb[:, 0:bias_n], Bias_ap, 'sq')
            blk.sync(q_bias)
            bld = b.c['sq']

            tp0 = b.c['tp']
            sp0 = b.c['sp']
            vp0 = b.c['vp']
            wb0 = [b.c['w0'], b.c['w1']]
            gb0 = b.c['g0']

            def q_w(q):
                for fc in range(nfc):
                    bb = fc % 2
                    if fc >= 2:
                        q.wait_ge(_tp, tp0 + fc - 1)
                    b.dma(q, sb3(A4[:, bb * 2048:(bb + 1) * 2048], DC, 128),
                          d3(W_ap, DC)[:, :, fc * 128:(fc + 1) * 128], wsn[bb])
            blk.sync(q_w)

            def t_mm(t):
                t.wait_ge(_sq, bld)
                for fc in range(nfc):
                    bb = fc % 2
                    ps = PS0 if bb == 0 else PS1
                    t.wait_ge(WS[bb], wb0[bb] + 16 * (fc // 2 + 1))
                    if fc >= 2:
                        if mode == 'copy':
                            t.wait_ge(_sp, sp0 + fc - 1)
                        else:
                            t.wait_ge(_vp, vp0 + fc - 1)
                    for dc in range(DC):
                        i = t.matmul(ps[0:128, 0:OWNLM],
                                     A4[:, bb * 2048 + dc * 128:bb * 2048 + (dc + 1) * 128],
                                     rhs(dc), start=(dc == 0), stop=(dc == DC - 1))
                    b.inc(i, 'tp', 1)
            blk.tensor(t_mm)

            if mode == 'copy':
                def s_ev(s):
                    for fc in range(nfc):
                        ps = PS0 if fc % 2 == 0 else PS1
                        s.wait_ge(_tp, tp0 + fc + 1)
                        b.inc(s.activation(out=dstf(fc), in_=ps[0:128, 0:OWNLM],
                                           func=AF.Identity, bias=bias_sb[:, fc:fc + 1]), 'sp', 1)
                blk.scalar(s_ev)
            else:
                def v_ev(v):
                    for fc in range(nfc):
                        ps = PS0 if fc % 2 == 0 else PS1
                        v.wait_ge(_tp, tp0 + fc + 1)
                        b.inc(v.scalar_tensor_tensor(out=dstf(fc), in0=ps[0:128, 0:OWNLM],
                                                     scalar=bias_sb[:, fc:fc + 1],
                                                     in1=base(fc), op0=AT.add, op1=AT.add),
                              'vp', 1)
                blk.vector(v_ev)

            if store is not None:
                da, row0f = store

                def g_s(g):
                    for fc in range(nfc):
                        if mode == 'copy':
                            g.wait_ge(_sp, sp0 + fc + 1)
                        else:
                            g.wait_ge(_vp, vp0 + fc + 1)
                        b.dma(g, d3(da, 0)[:, row0f(fc) // 128, :], dstf(fc), 'g0')
                blk.gpsimd(g_s)

        def emit_attn_b():
            def q_cs(q):
                b.dma(q, A6[:, 0:TOK], c_cos, 'sq')
                b.dma(q, A6[:, 4096:4096 + TOK], c_sin, 'sq')
                b.dma(q, pqa_sb[:], c_pqa, 'sq')
            blk.sync(q_cs)

            def dst_qkv_b(fc):
                if fc < 16:
                    return (scr_q, fc * 128)
                if fc < 32:
                    return (kvb, (fc - 16) * 128)
                return (kvb, D + (fc - 32) * 128)

            emit_qkv_stage(wqkv_b, bqkv_b, 48, dst_qkv_b, xres, ln_here=False,
                           rope=True, gate=True)

            gt_tp0 = b.c['tp']
            gt_vp0 = b.c['vp']

            def t_gt(t):
                for i in range(QT):
                    ps = PS2 if i % 2 == 0 else PS3
                    if i >= 2:
                        t.wait_ge(_vp, gt_vp0 + i - 1)
                    b.inc(t.transpose(ps[0:128, 0:32],
                                      gate_sm[:, i * 128:(i + 1) * 128],
                                      ident_sb[96:128, 96:128],
                                      tile_position=(96, 0)), 'tp', 1)
            blk.tensor(t_gt)

            def v_gt(v):
                for i in range(QT):
                    ps = PS2 if i % 2 == 0 else PS3
                    v.wait_ge(_tp, gt_tp0 + i + 1)
                    b.inc(v.tensor_copy(out=gate_t_sb[:, i * 32:(i + 1) * 32],
                                        in_=ps[0:128, 0:32]), 'vp', 1)
            blk.vector(v_gt)
            b.barrier()

            def kv_load(q, h, coloff, with_q=False):
                b.dma(q, A1[0:32, coloff:coloff + TOK], kvb[32 * h:32 * h + 32, :], wsn[h % 2])
                b.dma(q, A1[32:64, coloff:coloff + TOK],
                      kvb[1024 + 32 * h:1024 + 32 * h + 32, :], wsn[h % 2])
                b.dma(q, A1[64:128, coloff:coloff + TOK],
                      kvb[2048 + 64 * h:2048 + 64 * h + 64, :], wsn[h % 2])
                if with_q:
                    qoff = coloff + 2 * TOK
                    b.dma(q, A1[0:32, qoff:qoff + TOK], scr_q[32 * h:32 * h + 32, :], wsn[h % 2])
                    b.dma(q, A1[32:64, qoff:qoff + TOK],
                          scr_q[1024 + 32 * h:1024 + 32 * h + 32, :], wsn[h % 2])

            def emit_vt(t, coloff, vtoff, tpb, vpb):
                for i in range(QT):
                    ps = PS2 if i % 2 == 0 else PS3
                    if i >= 2:
                        t.wait_ge(_vp, vpb + i - 1)
                    b.inc(t.transpose(ps[0:128, 0:64],
                                      A1[64:128, coloff + i * 128:coloff + (i + 1) * 128],
                                      ident_sb[64:128, 64:128], tile_position=(64, 0)), 'tp', 1)

            def emit_vt_ev(v, vtoff, tpb):
                for i in range(QT):
                    ps = PS2 if i % 2 == 0 else PS3
                    v.wait_ge(_tp, tpb + i + 1)
                    b.inc(v.tensor_copy(out=A5[:, vtoff + 66 * i:vtoff + 66 * i + 64],
                                        in_=ps[0:128, 0:64]), 'vp', 1)

            def v_ones(v):
                v.memset(A5[:, 64:66 * QT:66], 1.0)
                b.inc(v.memset(A5[:, 2112 + 64:2112 + 66 * QT:66], 1.0), 'vp', 1)
            blk.vector(v_ones)
            b.barrier()

            for h in range(H2):
                coloff = (h % 2) * TOK
                vtoff = (h % 2) * 2112
                hb = 64 * (h % 2)
                hc = h // 2

                def q_h(q, h=h, coloff=coloff):
                    kv_load(q, h, coloff)
                blk.sync(q_h)
                ld_mark = b.c[wsn[h % 2]]

                tpb = b.c['tp']
                vpb = b.c['vp']

                def t_p(t, coloff=coloff, vtoff=vtoff, ld_mark=ld_mark,
                        tpb=tpb, vpb=vpb, h=h):
                    t.wait_ge(WS[h % 2], ld_mark)
                    emit_vt(t, coloff, vtoff, tpb, vpb)
                blk.tensor(t_p)

                def v_p(v, vtoff=vtoff, tpb=tpb):
                    emit_vt_ev(v, vtoff, tpb)
                blk.vector(v_p)
                vt_done = b.c['vp']

                def t_sc(t, h=h, coloff=coloff, vt_done=vt_done):
                    t.wait_ge(_vp, vt_done)
                    for i in range(QT):
                        b.inc(t.matmul(PS0[0:128, 4 * i:4 * i + 4],
                                       A1[0:64, coloff + i * 128:coloff + (i + 1) * 128],
                                       pqa_sb[0:64, 4 * h:4 * h + 4],
                                       start=True, stop=True), 'tp', 1)
                blk.tensor(t_sc)
                sc_done = b.c['tp']

                def s_e(s, sc_done=sc_done):
                    s.wait_ge(_tp, sc_done)
                    b.inc(s.activation(out=A3[:, 0:4 * QT], in_=PS0[0:128, 0:4 * QT],
                                       func=AF.Exp, scale=SC_B), 'sp', 1)
                blk.scalar(s_e)
                e_done = b.c['sp']

                def t_lm(t, vtoff=vtoff, e_done=e_done):
                    t.wait_ge(_sp, e_done)
                    for i in range(QT):
                        for bb_ in range(2):
                            psB = PS1 if bb_ == 0 else PS5
                            b.inc(t.matmul(psB[0:65, 4 * i:4 * i + 4],
                                           A5[64 * bb_:64 * bb_ + 64,
                                              vtoff + 66 * i:vtoff + 66 * i + 65],
                                           A3[64 * bb_:64 * bb_ + 64, 4 * i:4 * i + 4],
                                           start=True, stop=True,
                                           tile_position=(64 * bb_, 0)), 'tp', 1)
                blk.tensor(t_lm)
                lm_done = b.c['tp']

                def v_r(v, lm_done=lm_done):
                    v.wait_ge(_tp, lm_done)
                    m = b.inc(v.reciprocal(out=rden[:, 0:4 * QT], in_=PS1[64:65, 0:4 * QT]), 'vp', 1)
                    b.inc(v.reciprocal(out=rden[:, 256:256 + 4 * QT],
                                       in_=PS5[64:65, 0:4 * QT]), 'vp', 1)
                blk.vector(v_r)
                r_done = b.c['vp']

                def t_bc(t, r_done=r_done):
                    t.wait_ge(_vp, r_done)
                    t.matmul(PS4[0:64, 0:4 * QT], ones_r[64:65, 0:64],
                             rden[:, 0:4 * QT], start=True, stop=True,
                             tile_position=(64, 0))
                    b.inc(t.matmul(PS4[0:64, 256:256 + 4 * QT], ones_r[64:65, 0:64],
                                   rden[:, 256:256 + 4 * QT], start=True, stop=True,
                                   tile_position=(64, 0)), 'tp', 1)
                blk.tensor(t_bc)
                bc_done = b.c['tp']

                def s_c(s, bc_done=bc_done):
                    s.wait_ge(_tp, bc_done)
                    s.activation(out=A3[0:64, 256:256 + 4 * QT], in_=PS4[0:64, 0:4 * QT],
                                 func=AF.Copy)
                    b.inc(s.activation(out=A3[0:64, 384:384 + 4 * QT],
                                       in_=PS4[0:64, 256:256 + 4 * QT], func=AF.Copy), 'sp', 1)
                blk.scalar(s_c)
                c_done = b.c['sp']

                def v_out(v, hb=hb, hc=hc, c_done=c_done):
                    v.wait_ge(_sp, c_done)
                    lmf = A2[hb:hb + 64, hc * OWNLM:(hc + 1) * OWNLM]
                    lmf3 = lmf.rearrange("p (i m) -> p i m", m=8)
                    v.tensor_mul(out=lmf3[:, :, 0:4],
                                 in0=PS1[0:64, 0:4 * QT].rearrange("p (i m) -> p i m", m=4),
                                 in1=A3[0:64, 256:256 + 4 * QT].rearrange("p (i m) -> p i m", m=4))
                    b.inc(v.tensor_mul(out=lmf3[:, :, 4:8],
                                       in0=PS5[0:64, 0:4 * QT].rearrange("p (i m) -> p i m", m=4),
                                       in1=A3[0:64, 384:384 + 4 * QT].rearrange("p (i m) -> p i m", m=4)),
                          'vp', 1)
                blk.vector(v_out)
                b.barrier()


            if DEV_SUB == 31:
                def g_sub31(g):
                    b.dma(g, y[0:128, 0:2048], A2[:, 0:2048], 'gq')
                    g.wait_ge(_gq, b.c['gq'])
                blk.gpsimd(g_sub31)
                b.barrier()
                return

            def rhs_lmf0(dc):
                return A2[:, dc * OWNLM:(dc + 1) * OWNLM]

            def dst_lqkv(fc):
                return A5[:, fc * OWNLM:(fc + 1) * OWNLM]

            emit_small_proj(wlmqkv, blmqkv, 48, rhs_lmf0, dst_lqkv, mode='copy',
                            store=None, bias_n=48)
            st_sp = b.c['sp']

            def g_lkv(g):
                g.wait_ge(_sp, st_sp)
                for fc in range(16, 48):
                    b.dma(g, d3(ag3_in, 0)[:, fc - 16, :], dst_lqkv(fc), 'gq')
            blk.gpsimd(g_lkv)
            b.barrier()
            if PHASE2:
                emit_ag(ag3_in, ag3_out)

            def q_lkv(q):
                if PHASE2:
                    q.wait_ge(_cc, b.c['cc'])
                    a6v = A6[:, 0:32 * NL].rearrange("p (c l) -> p c l", l=NL)
                    b.dma(q, a6v[:, 0:16, 0:OWNLM], d3(ag3_out[0:2048, :], 16), 'sq')
                    b.dma(q, a6v[:, 0:16, OWNLM:NL], d3(ag3_out[4096:6144, :], 16), 'sq')
                    b.dma(q, a6v[:, 16:32, 0:OWNLM], d3(ag3_out[2048:4096, :], 16), 'sq')
                    b.dma(q, a6v[:, 16:32, OWNLM:NL], d3(ag3_out[6144:8192, :], 16), 'sq')
                else:
                    b.dma(q, sb3(A6[:, 0:32 * NL], 32, NL), d3(ag3_in, 32), 'sq')
            blk.sync(q_lkv)
            lkv_ld = b.c['sq']

            def v_ones2(v):
                b.inc(v.memset(A3[:, 576:577], 1.0), 'vp', 1)
            blk.vector(v_ones2)

            for h in range(H2):
                hb = 64 * (h % 2)
                hc = h // 2
                tp_l0 = b.c['tp']
                vp_l0 = b.c['vp']
                sp_l0 = b.c['sp']

                def t_l(t, h=h, hb=hb, hc=hc, tp_l0=tp_l0, vp_l0=vp_l0):
                    t.wait_ge(_sq, lkv_ld)
                    b.inc(t.matmul(PS0[0:128, 0:OWNLM],
                                   A6[hb:hb + 64, hc * NL:(hc + 1) * NL],
                                   A5[hb:hb + 64, (h // 2) * OWNLM + 0:(h // 2) * OWNLM + OWNLM]
                                   if False else
                                   A5[hb:hb + 64, hc * OWNLM:(hc + 1) * OWNLM],
                                   start=True, stop=True, tile_position=(hb, 0)), 'tp', 1)
                    b.inc(t.transpose(PS2[0:128, 0:64],
                                      A6[hb:hb + 64, (16 + hc) * NL:(16 + hc) * NL + 128],
                                      ident_sb[hb:hb + 64, hb:hb + 64],
                                      tile_position=(hb, 0)), 'tp', 1)
                    t.wait_ge(_vp, vp_l0 + 2)
                    b.inc(t.matmul(PS1[0:65, 0:OWNLM], A3[:, 512:577],
                                   A3[:, 0:OWNLM], start=True, stop=True), 'tp', 1)
                    t.wait_ge(_vp, vp_l0 + 4)
                    b.inc(t.matmul(PS4[0:64, 0:OWNLM], ones_r[64:65, 0:64],
                                   rden[:, 0:OWNLM], start=True, stop=True,
                                   tile_position=(64, 0)), 'tp', 1)
                blk.tensor(t_l)

                def s_l(s, tp_l0=tp_l0):
                    s.wait_ge(_tp, tp_l0 + 1)
                    b.inc(s.activation(out=A3[:, 0:OWNLM], in_=PS0[0:128, 0:OWNLM],
                                       func=AF.Exp, scale=SC_B), 'sp', 1)
                    s.wait_ge(_tp, tp_l0 + 4)
                    b.inc(s.activation(out=A3[0:64, 768:768 + OWNLM], in_=PS4[0:64, 0:OWNLM],
                                       func=AF.Copy), 'sp', 1)
                blk.scalar(s_l)

                def v_l(v, hb=hb, hc=hc, tp_l0=tp_l0, sp_l0=sp_l0):
                    v.wait_ge(_sp, sp_l0 + 1)
                    b.inc(v.tensor_mul(out=A3[:, 0:OWNLM], in0=A3[:, 0:OWNLM],
                                       in1=lmask_sb[:]), 'vp', 1)
                    v.wait_ge(_tp, tp_l0 + 2)
                    b.inc(v.tensor_copy(out=A3[:, 512:576], in_=PS2[0:128, 0:64]), 'vp', 1)
                    v.wait_ge(_tp, tp_l0 + 3)
                    m = b.inc(v.tensor_scalar_max(out=rden[:, 0:OWNLM],
                                                  in0=PS1[64:65, 0:OWNLM], scalar1=1e-9), 'vp', 1)
                    v.wait_ge(_vp, m)
                    b.inc(v.reciprocal(out=rden[:, 0:OWNLM], in_=rden[:, 0:OWNLM]), 'vp', 1)
                    v.wait_ge(_sp, sp_l0 + 2)
                    b.inc(v.tensor_mul(out=A2[hb:hb + 64, 2048 + hc * OWNLM:2048 + (hc + 1) * OWNLM],
                                       in0=PS1[0:64, 0:OWNLM],
                                       in1=A3[0:64, 768:768 + OWNLM]), 'vp', 1)
                blk.vector(v_l)
                b.barrier()

            if DEV_SUB == 32:
                def g_sub32(g):
                    b.dma(g, y[0:128, 0:2048], A2[:, 2048:4096], 'gq')
                    g.wait_ge(_gq, b.c['gq'])
                blk.gpsimd(g_sub32)
                b.barrier()
                return

            def rhs_lo(dc):
                return A2[:, 2048 + dc * OWNLM:2048 + (dc + 1) * OWNLM]

            def dst_lmf1(fc):
                return A2[:, 4096 + fc * OWNLM:4096 + (fc + 1) * OWNLM]

            def base_lmf0(fc):
                return A2[:, fc * OWNLM:(fc + 1) * OWNLM]

            emit_small_proj(wlmout, blmout, 16, rhs_lo, dst_lmf1, mode='addres',
                            base=base_lmf0, bias_n=16)
            b.barrier()

            def rhs_lmf1(dc):
                return A2[:, 4096 + dc * OWNLM:4096 + (dc + 1) * OWNLM]

            def dst_l2q(fc):
                return A2[:, 6144 + fc * OWNLM:6144 + (fc + 1) * OWNLM]

            emit_small_proj(wl2tq, bl2tq, 16, rhs_lmf1, dst_l2q, mode='copy', bias_n=16)
            b.barrier()

            def v_ones_r(v):
                v.memset(A5[:, 64:66 * QT:66], 1.0)
                b.inc(v.memset(A5[:, 2112 + 64:2112 + 66 * QT:66], 1.0), 'vp', 1)
            blk.vector(v_ones_r)
            b.barrier()

            for h in range(H2):
                coloff = (h % 2) * TOK
                vtoff = (h % 2) * 2112
                hb = 64 * (h % 2)
                hc = h // 2
                xeb = (32 * h) % 128
                xec = (32 * h) // 128
                xob = (1024 + 32 * h) % 128
                xoc = (1024 + 32 * h) // 128

                def q_h2(q, h=h, coloff=coloff):
                    kv_load(q, h, coloff)
                blk.sync(q_h2)
                ld_mark = b.c[wsn[h % 2]]

                tpb = b.c['tp']
                vpb = b.c['vp']

                def t_v2(t, coloff=coloff, vtoff=vtoff, ld_mark=ld_mark,
                         tpb=tpb, vpb=vpb, h=h):
                    t.wait_ge(WS[h % 2], ld_mark)
                    emit_vt(t, coloff, vtoff, tpb, vpb)
                blk.tensor(t_v2)

                def v_v2(v, vtoff=vtoff, tpb=tpb):
                    emit_vt_ev(v, vtoff, tpb)
                blk.vector(v_v2)
                vt_done = b.c['vp']

                def v_cq(v, xeb=xeb, xec=xec, xob=xob, xoc=xoc, vt_done=vt_done):
                    v.wait_ge(_vp, vt_done)
                    v.tensor_copy(out=A3[0:32, 896:896 + OWNLM],
                                  in_=A2[xeb:xeb + 32, 6144 + xec * OWNLM:6144 + (xec + 1) * OWNLM])
                    b.inc(v.tensor_copy(out=A3[32:64, 896:896 + OWNLM],
                                        in_=A2[xob:xob + 32,
                                               6144 + xoc * OWNLM:6144 + (xoc + 1) * OWNLM]),
                          'vp', 1)
                blk.vector(v_cq)
                cq_done = b.c['vp']

                def t_r(t, coloff=coloff, cq_done=cq_done):
                    t.wait_ge(_vp, cq_done)
                    for i in range(QT):
                        b.inc(t.matmul(PS0[0:128, 8 * i:8 * i + 8],
                                       A1[0:64, coloff + i * 128:coloff + (i + 1) * 128],
                                       A3[0:64, 896 + 8 * i:896 + 8 * i + 8],
                                       start=True, stop=True), 'tp', 1)
                blk.tensor(t_r)
                rsc_done = b.c['tp']

                def s_r(s, rsc_done=rsc_done):
                    s.wait_ge(_tp, rsc_done)
                    b.inc(s.activation(out=A3[:, 0:8 * QT], in_=PS0[0:128, 0:8 * QT],
                                       func=AF.Exp, scale=SC_B), 'sp', 1)
                blk.scalar(s_r)
                er_done = b.c['sp']

                def v_m(v, er_done=er_done):
                    v.wait_ge(_sp, er_done)
                    b.inc(v.tensor_mul(out=A3[:, 0:8 * QT], in0=A3[:, 0:8 * QT],
                                       in1=l2tm_sb[:]), 'vp', 1)
                blk.vector(v_m)
                vm_done = b.c['vp']

                def t_ro(t, vtoff=vtoff, vm_done=vm_done):
                    t.wait_ge(_vp, vm_done)
                    for i in range(QT):
                        for bb_ in range(2):
                            psB = PS1 if bb_ == 0 else PS5
                            b.inc(t.matmul(psB[0:65, 4 * i:4 * i + 4],
                                           A5[64 * bb_:64 * bb_ + 64,
                                              vtoff + 66 * i:vtoff + 66 * i + 65],
                                           A3[64 * bb_:64 * bb_ + 64,
                                              8 * i + 4 * bb_:8 * i + 4 * bb_ + 4],
                                           start=True, stop=True,
                                           tile_position=(64 * bb_, 0)), 'tp', 1)
                blk.tensor(t_ro)
                ro_done = b.c['tp']

                def v_r2(v, ro_done=ro_done):
                    v.wait_ge(_tp, ro_done)
                    v.reciprocal(out=rden[:, 0:4 * QT], in_=PS1[64:65, 0:4 * QT])
                    b.inc(v.reciprocal(out=rden[:, 256:256 + 4 * QT],
                                       in_=PS5[64:65, 0:4 * QT]), 'vp', 1)
                blk.vector(v_r2)
                r2_done = b.c['vp']

                def t_bc2(t, r2_done=r2_done):
                    t.wait_ge(_vp, r2_done)
                    t.matmul(PS4[0:64, 0:4 * QT], ones_r[64:65, 0:64],
                             rden[:, 0:4 * QT], start=True, stop=True,
                             tile_position=(64, 0))
                    b.inc(t.matmul(PS4[0:64, 256:256 + 4 * QT], ones_r[64:65, 0:64],
                                   rden[:, 256:256 + 4 * QT], start=True, stop=True,
                                   tile_position=(64, 0)), 'tp', 1)
                blk.tensor(t_bc2)
                bc2_done = b.c['tp']

                def s_c2(s, bc2_done=bc2_done):
                    s.wait_ge(_tp, bc2_done)
                    s.activation(out=A3[0:64, 256:256 + 4 * QT], in_=PS4[0:64, 0:4 * QT],
                                 func=AF.Copy)
                    b.inc(s.activation(out=A3[0:64, 384:384 + 4 * QT],
                                       in_=PS4[0:64, 256:256 + 4 * QT], func=AF.Copy), 'sp', 1)
                blk.scalar(s_c2)
                c2_done = b.c['sp']

                def v_o2(v, hb=hb, hc=hc, c2_done=c2_done):
                    v.wait_ge(_sp, c2_done)
                    ro_ = A2[hb:hb + 64, 2048 + hc * OWNLM:2048 + (hc + 1) * OWNLM]
                    ro3 = ro_.rearrange("p (i m) -> p i m", m=8)
                    v.tensor_mul(out=ro3[:, :, 0:4],
                                 in0=PS1[0:64, 0:4 * QT].rearrange("p (i m) -> p i m", m=4),
                                 in1=A3[0:64, 256:256 + 4 * QT].rearrange("p (i m) -> p i m", m=4))
                    b.inc(v.tensor_mul(out=ro3[:, :, 4:8],
                                       in0=PS5[0:64, 0:4 * QT].rearrange("p (i m) -> p i m", m=4),
                                       in1=A3[0:64, 384:384 + 4 * QT].rearrange("p (i m) -> p i m", m=4)),
                          'vp', 1)
                blk.vector(v_o2)
                b.barrier()

            if DEV_SUB == 33:
                def g_sub33(g):
                    b.dma(g, y[0:128, 0:2048], A2[:, 2048:4096], 'gq')
                    g.wait_ge(_gq, b.c['gq'])
                blk.gpsimd(g_sub33)
                b.barrier()
                return

            def dst_lmf2(fc):
                return A2[:, fc * OWNLM:(fc + 1) * OWNLM]

            emit_small_proj(wl2tout, bl2tout, 16, rhs_lo, dst_lmf2, mode='addres',
                            base=rhs_lmf1, bias_n=16)
            b.barrier()

            def rhs_lmf2(dc):
                return A2[:, dc * OWNLM:(dc + 1) * OWNLM]

            def dst_gkv(fc):
                return A5[:, fc * OWNLM:(fc + 1) * OWNLM]

            emit_small_proj(wlmkv, blmkv, 32, rhs_lmf2, dst_gkv, mode='copy',
                            store=(ag4_in, lambda fc: fc * 128), bias_n=32)
            b.barrier()
            if PHASE2:
                emit_ag(ag4_in, ag4_out)

            if DEV_DBG:
                def g_dbg2(g):
                    b.dma(g, dbg[:, 0:2048], A2[:, 2048:4096], 'gq')
                    b.dma(g, dbg[:, 2048:4096], A2[:, 0:2048], 'gq')
                    g.wait_ge(_gq, b.c['gq'])
                blk.gpsimd(g_dbg2)
                b.barrier()

            if DEV_SUB == 34:
                def g_sub34(g):
                    b.dma(g, y[0:2048, 0:128], ag4_in[0:2048, :], 'gq')
                    g.wait_ge(_gq, b.c['gq'])
                blk.gpsimd(g_sub34)
                b.barrier()
                return

            def q_gm(q):
                b.dma(q, A6[:, 0:QT * 128], c_gmask, 'sq')
            blk.sync(q_gm)
            gmask_sb = A6
            b.barrier()
            def v_ones_r2(v):
                v.memset(A5[:, 64:66 * QT:66], 1.0)
                b.inc(v.memset(A5[:, 2112 + 64:2112 + 66 * QT:66], 1.0), 'vp', 1)
            blk.vector(v_ones_r2)
            b.barrier()

            for h in range(H2):
                coloff = (h % 2) * TOK
                qoff = coloff + 2 * TOK
                vtoff = (h % 2) * 2112
                hb = 64 * (h % 2)

                def q_h3(q, h=h, coloff=coloff):
                    if PHASE2 and h == 0:
                        q.wait_ge(_cc, b.c['cc'])
                    kv_load(q, h, coloff, with_q=True)
                    if PHASE2:
                        b.dma(q, A3[0:32, 896:896 + OWNLM], ag4_out[32 * h:32 * h + 32, :], wsn[h % 2])
                        b.dma(q, A3[0:32, 896 + OWNLM:896 + NL],
                              ag4_out[4096 + 32 * h:4096 + 32 * h + 32, :], wsn[h % 2])
                        b.dma(q, A3[32:64, 896:896 + OWNLM],
                              ag4_out[1024 + 32 * h:1024 + 32 * h + 32, :], wsn[h % 2])
                        b.dma(q, A3[32:64, 896 + OWNLM:896 + NL],
                              ag4_out[4096 + 1024 + 32 * h:4096 + 1024 + 32 * h + 32, :], wsn[h % 2])
                        b.dma(q, A3[64:128, 1024:1024 + OWNLM],
                              ag4_out[2048 + 64 * h:2048 + 64 * h + 64, :], wsn[h % 2])
                        b.dma(q, A3[64:128, 1024 + OWNLM:1024 + NL],
                              ag4_out[4096 + 2048 + 64 * h:4096 + 2048 + 64 * h + 64, :], wsn[h % 2])
                    else:
                        b.dma(q, A3[0:32, 896:896 + NL], ag4_in[32 * h:32 * h + 32, :], wsn[h % 2])
                        b.dma(q, A3[32:64, 896:896 + NL],
                              ag4_in[1024 + 32 * h:1024 + 32 * h + 32, :], wsn[h % 2])
                        b.dma(q, A3[64:128, 1024:1024 + NL],
                              ag4_in[2048 + 64 * h:2048 + 64 * h + 64, :], wsn[h % 2])
                blk.sync(q_h3)
                ld_mark = b.c[wsn[h % 2]]

                tpb = b.c['tp']
                vpb = b.c['vp']

                def t_v3(t, coloff=coloff, vtoff=vtoff, ld_mark=ld_mark,
                         tpb=tpb, vpb=vpb, h=h):
                    t.wait_ge(WS[h % 2], ld_mark)
                    emit_vt(t, coloff, vtoff, tpb, vpb)
                    b.inc(t.transpose(PS5[0:128, 0:64], A3[64:128, 1024:1024 + NL],
                                      ident_sb[64:128, 64:128], tile_position=(64, 0)), 'tp', 1)
                blk.tensor(t_v3)
                gvt_tp = b.c['tp']

                def v_v3(v, vtoff=vtoff, tpb=tpb, gvt_tp=gvt_tp):
                    emit_vt_ev(v, vtoff, tpb)
                    v.wait_ge(_tp, gvt_tp)
                    b.inc(v.tensor_copy(out=A3[:, 512:576], in_=PS5[0:128, 0:64]), 'vp', 1)
                blk.vector(v_v3)
                vt_done = b.c['vp']

                for i in range(QT):
                    t0 = b.c['tp']
                    s0 = b.c['sp']
                    v0 = b.c['vp']

                    def t_a(t, h=h, i=i, coloff=coloff, qoff=qoff, vtoff=vtoff,
                            vt_done=vt_done, t0=t0, s0=s0, v0=v0):
                        t.wait_ge(_vp, vt_done)
                        if i >= 1:
                            t.wait_ge(_sp, s0)
                        b.inc(t.matmul(PS0[0:128, 0:128],
                                       A1[0:64, coloff + i * 128:coloff + (i + 1) * 128],
                                       A1[0:64, qoff + i * 128:qoff + (i + 1) * 128],
                                       start=True, stop=True), 'tp', 1)
                        b.inc(t.matmul(PS2[0:128, 0:128],
                                       A3[0:64, 896:896 + NL],
                                       A1[0:64, qoff + i * 128:qoff + (i + 1) * 128],
                                       start=True, stop=True), 'tp', 1)
                        t.wait_ge(_vp, v0 + 1)
                        b.inc(t.matmul(PS1[0:128, 0:65], A3[:, 0:128],
                                       A5[:, vtoff + 66 * i:vtoff + 66 * i + 65],
                                       start=True, stop=True), 'tp', 1)
                        t.wait_ge(_vp, v0 + 2)
                        b.inc(t.matmul(PS3[0:128, 0:65], A3[:, 128:256],
                                       A3[:, 512:577], start=True, stop=True), 'tp', 1)
                        t.wait_ge(_vp, v0 + 10)
                        b.inc(t.transpose(PS4[0:64, 0:128], A3[:, 640:704],
                                          ident_sb[:]), 'tp', 1)
                    blk.tensor(t_a)

                    def s_a(s, t0=t0):
                        s.wait_ge(_tp, t0 + 1)
                        b.inc(s.activation(out=A3[:, 0:128], in_=PS0[0:128, 0:128],
                                           func=AF.Exp, scale=SC_B), 'sp', 1)
                        s.wait_ge(_tp, t0 + 2)
                        b.inc(s.activation(out=A3[:, 128:256], in_=PS2[0:128, 0:128],
                                           func=AF.Exp, scale=SC_B), 'sp', 1)
                    blk.scalar(s_a)

                    def v_a(v, h=h, i=i, hb=hb, t0=t0, s0=s0, v0=v0):
                        v.wait_ge(_sp, s0 + 1)
                        b.inc(v.tensor_mul(out=A3[:, 0:128], in0=A3[:, 0:128],
                                           in1=blktri_sb[:]), 'vp', 1)
                        v.wait_ge(_sp, s0 + 2)
                        b.inc(v.tensor_mul(out=A3[:, 128:256], in0=A3[:, 128:256],
                                           in1=gmask_sb[:, i * 128:(i + 1) * 128]), 'vp', 1)
                        v.wait_ge(_tp, t0 + 3)
                        m = b.inc(v.tensor_scalar_max(out=A3[:, 704:705],
                                                      in0=PS1[0:128, 64:65], scalar1=1e-9), 'vp', 1)
                        v.wait_ge(_vp, m)
                        m = b.inc(v.reciprocal(out=A3[:, 705:706], in_=A3[:, 704:705]), 'vp', 1)
                        v.wait_ge(_tp, t0 + 4)
                        m2 = b.inc(v.tensor_scalar_max(out=A3[:, 706:707],
                                                       in0=PS3[0:128, 64:65], scalar1=1e-9), 'vp', 1)
                        v.wait_ge(_vp, m2)
                        mg = b.inc(v.reciprocal(out=A3[:, 707:708], in_=A3[:, 706:707]), 'vp', 1)
                        m3 = b.inc(v.tensor_scalar_mul(out=A3[:, 768:832], in0=PS1[0:128, 0:64],
                                                       scalar1=A3[:, 705:706]), 'vp', 1)
                        v.wait_ge(_vp, mg)
                        b.inc(v.tensor_scalar_mul(out=A3[:, 832:896], in0=PS3[0:128, 0:64],
                                                  scalar1=A3[:, 707:708]), 'vp', 1)
                        v.wait_ge(_vp, m3 + 1)
                        m4 = b.inc(v.tensor_sub(out=A3[:, 640:704], in0=A3[:, 832:896],
                                                in1=A3[:, 768:832]), 'vp', 1)
                        v.wait_ge(_vp, m4)
                        b.inc(v.scalar_tensor_tensor(out=A3[:, 640:704], in0=A3[:, 640:704],
                                                     scalar=gate_t_sb[:, 32 * i + h:32 * i + h + 1],
                                                     in1=A3[:, 768:832], op0=AT.mult, op1=AT.add),
                              'vp', 1)
                        v.wait_ge(_tp, t0 + 5)
                        b.inc(v.tensor_copy(out=A2[hb:hb + 64, i * 128:(i + 1) * 128],
                                            in_=PS4[0:64, 0:128]), 'vp', 1)
                    blk.vector(v_a)

                st_v = b.c['vp']

                def g_o(g, h=h, hb=hb, st_v=st_v):
                    g.wait_ge(_vp, st_v)
                    b.dma(g, scr_ao[64 * h:64 * h + 64, :], A2[hb:hb + 64, 0:TOK], 'gq')
                blk.gpsimd(g_o)
                b.barrier()

        def dst_qkv_a(fc):
            if fc < 16:
                return (scr_q, fc * 128)
            if fc < 32:
                return (ag1_in, (fc - 16) * 128)
            return (ag1_in, D + (fc - 32) * 128)

        def emit_ag(in_ap, out_ap):
            def g_ag(g):
                b.inc(g.collective_compute(
                    "AllGather", AT.bypass,
                    replica_groups=[[0, 1], [2, 3], [4, 5], [6, 7]],
                    ins=[in_ap.opt()], outs=[out_ap.opt()]), 'cc', 1)
            blk.gpsimd(g_ag)

        emit_qkv_stage(wqkv_a, bqkv_a, 48, dst_qkv_a, x0, ln_here=True)
        if PHASE2:
            emit_ag(ag1_in, ag1_out)
        emit_attn_a()
        dst1 = y if DEV_STOP == 1 else xres
        emit_proj_res(scr_ao, wo_a, bo_a, x0, dst1, want_stats=(DEV_STOP > 1))

        if DEV_STOP >= 2:
            dst2 = y if DEV_STOP == 2 else xres
            emit_ffn(xres, dst2, want_stats=(DEV_STOP > 2))

        if DEV_STOP >= 3:
            emit_attn_b()
            if DEV_SUB == 0:
                dst3 = y if DEV_STOP == 3 else xres
                emit_proj_res(scr_ao, wout, bout, xres, dst3, want_stats=(DEV_STOP > 3))

        if DEV_STOP >= 4 and DEV_SUB == 0:
            emit_ffn(xres, y, want_stats=False)

        def g_fin(g):
            g.wait_ge(_gq, b.c['gq'])
        blk.gpsimd(g_fin)

    return nc


_NC_CACHE = {}


def kernel(**inputs):
    in_maps = stage_inputs(inputs)
    key = (PHASE2, DEV_STOP, DEV_SUB)
    if key not in _NC_CACHE:
        _NC_CACHE[key] = build_program()
    nc = _NC_CACHE[key]
    res = run_bass_kernel_spmd(nc, in_maps, core_ids=list(range(8)))
    out = np.empty((Bsz, L, D), dtype=np.float32)
    for core in range(8):
        if not PHASE2 and core % 2 == 1:
            continue
        s = core // 2
        own_tiles = list(range(core % 2, 16, 2)) if PHASE2 else list(range(16))
        yc = res.results[core]['y']
        for ii, tt in enumerate(own_tiles):
            out[s, 128 * tt:128 * tt + 128, :] = yc[:, 128 * ii:128 * (ii + 1)].T
    return out


if __name__ == "__main__":
    pass


# version 04
# speedup vs baseline: 1.9467x; 1.0805x over previous
import numpy as np
import concourse.bass as bass
import concourse.mybir as mybir
from concourse.bass_utils import run_bass_kernel_spmd

F32 = mybir.dt.float32
F32R = mybir.dt.float32r
AT = mybir.AluOpType
AF = mybir.ActivationFunctionType

D = 2048
L = 2048
Bsz = 4
NH = 16
H2 = 32
M4 = 4
NL = 128
I5 = 5632
DC = 16
VELT = 4097.0
EPS_LN = 1e-5
SC_A = float(1.0 / np.sqrt(128.0))
SC_B = float(1.0 / np.sqrt(64.0))

PHASE2 = True
TOK = L // 2 if PHASE2 else L
QT = TOK // 128
OWNLM = NL // 2 if PHASE2 else NL
TC = 512
NTC = TOK // TC
ICH = I5 // 128
DEV_STOP = 4
DEV_DBG = False
DEV_SUB = 0


class Bld:
    def __init__(self, nc, blk, sems):
        self.nc, self.blk, self.sem = nc, blk, sems
        self.c = {k: 0 for k in sems}

    def inc(self, instr, s, n=1):
        instr.then_inc(self.sem[s], n)
        self.c[s] += n
        return self.c[s]

    def dma(self, eng, out, in_, q):
        return self.inc(eng.dma_start(out=out, in_=in_), q, 16)

    def chk(self, s, expected):
        assert self.c[s] == expected, f"sem {s}: emitted {self.c[s]} != planned {expected}"

    def barrier(self):
        tgt = self.c['bar'] + 5
        syncq = [(q, self.c[q]) for q in ('sq', 'w0', 'w1', 'w2', 'w3')]
        gpq = [(q, self.c[q]) for q in ('gq', 'g0', 'g1', 'g2', 'g3', 'g4', 'f0', 'f1', 'f2', 'f3')]
        bar = self.sem['bar']

        def t_f(t):
            t.wait_ge(self.sem['sq'], self.prolog_sq)
            self.inc(t.matmul(self.scratch_ps[96:97, 508:509],
                              self.ident[96:97, 96:97], self.ident[96:97, 96:97],
                              start=True, stop=True, tile_position=(96, 96)), 'bar', 1)
            t.wait_ge(bar, tgt)

        def v_f(v):
            self.inc(v.memset(self.scratch_sm[64:65, 3071:3072], 0.0), 'bar', 1)
            v.wait_ge(bar, tgt)

        def s_f(s):
            self.inc(s.memzero(self.scratch_sm[64:65, 3070:3071]), 'bar', 1)
            s.wait_ge(bar, tgt)

        def q_f(eng):
            for q, v in syncq:
                if v:
                    eng.wait_ge(self.sem[q], v)
            eng.sem_inc(bar, 1)
            self.c['bar'] += 1
            eng.wait_ge(bar, tgt)

        def g_f(eng):
            for q, v in gpq:
                if v:
                    eng.wait_ge(self.sem[q], v)
            eng.sem_inc(bar, 1)
            self.c['bar'] += 1
            eng.wait_ge(bar, tgt)
        self.blk.tensor(t_f)
        self.blk.vector(v_f)
        self.blk.scalar(s_f)
        self.blk.sync(q_f)
        self.blk.gpsimd(g_f)
        assert self.c['bar'] == tgt


def _rope_perm():
    perm = np.zeros(D, dtype=np.int64)
    for h in range(H2):
        for j in range(32):
            perm[32 * h + j] = 64 * h + 2 * j
            perm[1024 + 32 * h + j] = 64 * h + 2 * j + 1
    return perm


def stage_inputs(inputs):
    f32 = np.float32
    perm = _rope_perm()

    wqkv_a = np.ascontiguousarray(inputs['L0_attn_a_Wqkv'], dtype=f32)
    bqkv_a = np.ascontiguousarray(np.asarray(inputs['L0_attn_a_bqkv'], dtype=f32).reshape(48, 128).T)
    wo_a = np.ascontiguousarray(inputs['L0_attn_a_Wo'], dtype=f32)
    bo_a = np.ascontiguousarray(np.asarray(inputs['L0_attn_a_bo'], dtype=f32).reshape(16, 128).T)

    wqkv_b = np.array(inputs['L0_attn_b_w_qkv'], dtype=f32)
    bqkv_b = np.array(inputs['L0_attn_b_b_qkv'], dtype=f32)
    wqkv_b[:, 0:D] = wqkv_b[:, 0:D][:, perm]
    wqkv_b[:, D:2 * D] = wqkv_b[:, D:2 * D][:, perm]
    bqkv_b[0:D] = bqkv_b[0:D][perm]
    bqkv_b[D:2 * D] = bqkv_b[D:2 * D][perm]
    wqkv_b = np.ascontiguousarray(wqkv_b)
    bqkv_b = np.ascontiguousarray(bqkv_b.reshape(48, 128).T)

    pqT = np.ascontiguousarray(np.asarray(inputs['L0_attn_b_pool_q'], dtype=f32).T[perm, :])

    wlmqkv = np.ascontiguousarray(inputs['L0_attn_b_w_lm_qkv'], dtype=f32)
    blmqkv = np.ascontiguousarray(np.asarray(inputs['L0_attn_b_b_lm_qkv'], dtype=f32).reshape(48, 128).T)
    wlmout = np.ascontiguousarray(inputs['L0_attn_b_w_lm_out'], dtype=f32)
    blmout = np.ascontiguousarray(np.asarray(inputs['L0_attn_b_b_lm_out'], dtype=f32).reshape(16, 128).T)
    wl2tq = np.ascontiguousarray(np.asarray(inputs['L0_attn_b_w_l2t_q'], dtype=f32)[:, perm])
    bl2tq = np.ascontiguousarray(np.asarray(inputs['L0_attn_b_b_l2t_q'], dtype=f32)[perm].reshape(16, 128).T)
    wl2tout = np.ascontiguousarray(inputs['L0_attn_b_w_l2t_out'], dtype=f32)
    bl2tout = np.ascontiguousarray(np.asarray(inputs['L0_attn_b_b_l2t_out'], dtype=f32).reshape(16, 128).T)
    wlmkv = np.array(inputs['L0_attn_b_w_lm_kv'], dtype=f32)
    blmkv = np.array(inputs['L0_attn_b_b_lm_kv'], dtype=f32)
    wlmkv[:, 0:D] = wlmkv[:, 0:D][:, perm]
    blmkv[0:D] = blmkv[0:D][perm]
    wlmkv = np.ascontiguousarray(wlmkv)
    blmkv = np.ascontiguousarray(blmkv.reshape(32, 128).T)
    wout = np.ascontiguousarray(inputs['L0_attn_b_w_out'], dtype=f32)
    bout = np.ascontiguousarray(np.asarray(inputs['L0_attn_b_b_out'], dtype=f32).reshape(16, 128).T)
    wgate = np.ascontiguousarray(inputs['L0_attn_b_w_gate'], dtype=f32)
    bgate_v = np.asarray(inputs['L0_attn_b_b_gate'], dtype=f32)
    bgate = np.zeros((128, 1), dtype=f32)
    bgate[96:128, 0] = bgate_v

    gwT = np.ascontiguousarray(np.asarray(inputs['L0_ffn_gate_w'], dtype=f32).T)
    uwT = np.ascontiguousarray(np.asarray(inputs['L0_ffn_up_w'], dtype=f32).T)
    dwT = np.ascontiguousarray(np.asarray(inputs['L0_ffn_down_w'], dtype=f32).T)
    gsv = np.ascontiguousarray(np.asarray(inputs['L0_ffn_gate_s'], dtype=f32).reshape(44, 128).T)
    usv = np.ascontiguousarray(np.asarray(inputs['L0_ffn_up_s'], dtype=f32).reshape(44, 128).T)
    dsv = np.ascontiguousarray(np.asarray(inputs['L0_ffn_down_s'], dtype=f32).reshape(16, 128).T)

    ident = np.eye(128, dtype=f32)
    tri = (np.arange(128)[:, None] <= np.arange(128)[None, :]).astype(f32)
    si = np.arange(128)
    blktri = (((si[:, None] // 64) == (si[None, :] // 64)) &
              ((si[:, None] % 64) <= (si[None, :] % 64))).astype(f32)
    l2tm = np.ascontiguousarray(
        np.tile(((si[:, None] // 64) == (np.arange(8)[None, :] // 4)).astype(f32), (1, QT)))
    inv = 1.0 / 10000.0 ** (np.arange(0, 64, 2) / 64.0)

    x_full = np.asarray(inputs['L0_attn_a_x'], dtype=f32)

    in_maps = []
    for core in range(8):
        s = core // 2
        par = core % 2
        own_tiles = list(range(par, 16, 2)) if PHASE2 else list(range(16))
        own_tok = np.concatenate([np.arange(128 * t, 128 * t + 128) for t in own_tiles])
        own_blocks = [t for tt in own_tiles for t in (2 * tt, 2 * tt + 1)]

        xT = np.ascontiguousarray(x_full[s][own_tok].T)
        ang = own_tok[None, :].astype(np.float64) * inv[:, None]
        cos128 = np.ascontiguousarray(np.tile(np.cos(ang), (4, 1)).astype(f32))
        sin128 = np.ascontiguousarray(np.tile(np.sin(ang), (4, 1)).astype(f32))

        if PHASE2:
            ev = [t for tt in range(0, 16, 2) for t in (2 * tt, 2 * tt + 1)]
            od = [t for tt in range(1, 16, 2) for t in (2 * tt, 2 * tt + 1)]
            lm_blk = np.repeat(np.array(ev + od), M4)
        else:
            lm_blk = np.repeat(np.arange(32), M4)
        own_lm_blk = np.repeat(np.array(own_blocks), M4)
        lmask = (lm_blk[:, None] <= own_lm_blk[None, :]).astype(f32)
        gm = np.zeros((128, QT * 128), dtype=f32)
        for ii, tt in enumerate(own_tiles):
            q_blk = np.repeat(np.array([2 * tt, 2 * tt + 1]), 64)
            gm[:, ii * 128:(ii + 1) * 128] = (lm_blk[:, None] < q_blk[None, :]).astype(f32)

        if PHASE2:
            am = np.zeros((128, QT * 2 * 128), dtype=f32)
            for ii, tt in enumerate(own_tiles):
                for j, St in enumerate((2 * ii, 2 * ii + 1)):
                    if St < tt:
                        mblk = np.ones((128, 128), dtype=f32)
                    elif St == tt:
                        mblk = tri
                    else:
                        mblk = np.zeros((128, 128), dtype=f32)
                    am[:, (2 * ii + j) * 128:(2 * ii + j + 1) * 128] = mblk
            m_amask = am

        pqa = np.zeros((64, 128), dtype=f32)
        for h in range(H2):
            pqa[0:32, 4 * h:4 * h + 4] = pqT[32 * h:32 * h + 32, :]
            pqa[32:64, 4 * h:4 * h + 4] = pqT[1024 + 32 * h:1024 + 32 * h + 32, :]

        m = {
            'x0': xT,
            'wqkv_a': wqkv_a, 'bqkv_a': bqkv_a, 'wo_a': wo_a, 'bo_a': bo_a,
            'wqkv_b': wqkv_b, 'bqkv_b': bqkv_b, 'pqT': pqT,
            'wlmqkv': wlmqkv, 'blmqkv': blmqkv, 'wlmout': wlmout, 'blmout': blmout,
            'wl2tq': wl2tq, 'bl2tq': bl2tq, 'wl2tout': wl2tout, 'bl2tout': bl2tout,
            'wlmkv': wlmkv, 'blmkv': blmkv, 'wout': wout, 'bout': bout,
            'wgate': wgate, 'bgate': bgate,
            'gwT': gwT, 'uwT': uwT, 'dwT': dwT, 'gs': gsv, 'us': usv, 'ds': dsv,
            'c_ident': ident, 'c_tri': tri, 'c_blktri': blktri, 'c_l2tm': l2tm,
            'c_lmask': np.ascontiguousarray(lmask), 'c_gmask': np.ascontiguousarray(gm),
            'c_cos': cos128, 'c_sin': sin128, 'c_pqa': np.ascontiguousarray(pqa),
        }
        if PHASE2:
            m['c_amask'] = np.ascontiguousarray(m_amask)
        in_maps.append(m)
    return in_maps


def build_program():
    nc = bass.Bass("TRN2", target_bir_lowering=False)

    def din(name, shape, dt=F32):
        return nc.dram_tensor(name, shape, dt, kind="ExternalInput").ap()

    x0 = din('x0', [D, TOK])
    wqkv_a = din('wqkv_a', [D, 3 * D]); bqkv_a = din('bqkv_a', [128, 48])
    wo_a = din('wo_a', [D, D]); bo_a = din('bo_a', [128, 16])
    wqkv_b = din('wqkv_b', [D, 3 * D]); bqkv_b = din('bqkv_b', [128, 48])
    pqT = din('pqT', [D, M4])
    wlmqkv = din('wlmqkv', [D, 3 * D]); blmqkv = din('blmqkv', [128, 48])
    wlmout = din('wlmout', [D, D]); blmout = din('blmout', [128, 16])
    wl2tq = din('wl2tq', [D, D]); bl2tq = din('bl2tq', [128, 16])
    wl2tout = din('wl2tout', [D, D]); bl2tout = din('bl2tout', [128, 16])
    wlmkv = din('wlmkv', [D, 2 * D]); blmkv = din('blmkv', [128, 32])
    wout = din('wout', [D, D]); bout = din('bout', [128, 16])
    wgate = din('wgate', [D, H2]); bgate = din('bgate', [128, 1])
    gwT = din('gwT', [D, I5])
    uwT = din('uwT', [D, I5])
    dwT = din('dwT', [I5, D])
    gs = din('gs', [128, 44]); us = din('us', [128, 44]); ds = din('ds', [128, 16])
    c_ident = din('c_ident', [128, 128]); c_tri = din('c_tri', [128, 128])
    c_blktri = din('c_blktri', [128, 128]); c_l2tm = din('c_l2tm', [128, 8 * QT])
    c_lmask = din('c_lmask', [128, OWNLM]); c_gmask = din('c_gmask', [128, QT * 128])
    c_cos = din('c_cos', [128, TOK]); c_sin = din('c_sin', [128, TOK])
    c_pqa = din('c_pqa', [64, 128])
    c_amask = din('c_amask', [128, QT * 2 * 128]) if PHASE2 else None
    dbg = (nc.dram_tensor('dbg', [128, 4096], F32, kind="ExternalOutput").ap()
           if DEV_DBG else None)

    y = nc.dram_tensor('y', [D, TOK], F32, kind="ExternalOutput").ap()

    xres = nc.dram_tensor('xres', [D, TOK], F32).ap()
    scr_ao = nc.dram_tensor('scr_ao', [D, TOK], F32).ap()
    scr_q = nc.dram_tensor('scr_q', [D, TOK], F32).ap()
    ag1_in = nc.dram_tensor('ag1_in', [2 * D, TOK], F32).ap()
    ag1_out = nc.dram_tensor('ag1_out', [4 * D, TOK], F32).ap() if PHASE2 else None
    ag3_out = nc.dram_tensor('ag3_out', [4 * D, OWNLM], F32).ap() if PHASE2 else None
    ag4_out = nc.dram_tensor('ag4_out', [4 * D, OWNLM], F32).ap() if PHASE2 else None
    kvb = nc.dram_tensor('kvb', [2 * D, TOK], F32).ap()
    ag3_in = nc.dram_tensor('ag3_in', [2 * D, OWNLM], F32).ap()
    ag4_in = nc.dram_tensor('ag4_in', [2 * D, OWNLM], F32).ap()

    def d3(ap_, C):
        return ap_.rearrange("(c p) t -> p c t", p=128)

    def sb3(t_, C, w):
        return t_.rearrange("p (c t) -> p c t", c=C)

    import contextlib
    with contextlib.ExitStack() as _st:
        A1 = _st.enter_context(nc.sbuf_tensor('A1', [128, 8192], F32))
        A2 = _st.enter_context(nc.sbuf_tensor('A2', [128, 8192], F32))
        A3 = _st.enter_context(nc.sbuf_tensor('A3', [128, 3072], F32))
        A4 = _st.enter_context(nc.sbuf_tensor('A4', [128, 4096], F32))
        A5 = _st.enter_context(nc.sbuf_tensor('A5', [128, 8192], F32))
        A6 = _st.enter_context(nc.sbuf_tensor('A6', [128, 8192], F32))
        RW = _st.enter_context(nc.sbuf_tensor('RW', [128, 8192], mybir.dt.bfloat16))
        SM = _st.enter_context(nc.sbuf_tensor('SM', [128, 3072], F32))
        pqa_sb = _st.enter_context(nc.sbuf_tensor('pqa_sb', [64, 128], F32))
        gate_t_sb = _st.enter_context(nc.sbuf_tensor('gate_t_sb', [128, 32 * QT], F32))
        ident_sb = _st.enter_context(nc.sbuf_tensor('ident_sb', [128, 128], F32))
        tri_sb = _st.enter_context(nc.sbuf_tensor('tri_sb', [128, 128], F32))
        blktri_sb = _st.enter_context(nc.sbuf_tensor('blktri_sb', [128, 128], F32))
        l2tm_sb = _st.enter_context(nc.sbuf_tensor('l2tm_sb', [128, 8 * QT], F32))
        lmask_sb = _st.enter_context(nc.sbuf_tensor('lmask_sb', [128, OWNLM], F32))
        ones_c = _st.enter_context(nc.sbuf_tensor('ones_c', [128, 1], F32))
        ones_r = _st.enter_context(nc.sbuf_tensor('ones_r', [128, 128], F32))
        bias_sb = _st.enter_context(nc.sbuf_tensor('bias_sb', [128, 48], F32))
        bias2_sb = _st.enter_context(nc.sbuf_tensor('bias2_sb', [128, 1], F32))
        gsc_sb = _st.enter_context(nc.sbuf_tensor('gsc_sb', [128, 44], F32))
        usc_sb = _st.enter_context(nc.sbuf_tensor('usc_sb', [128, 44], F32))
        dsc_sb = _st.enter_context(nc.sbuf_tensor('dsc_sb', [128, 16], F32))
        PS0 = _st.enter_context(nc.psum_tensor('PS0', [128, 512], F32))
        PS1 = _st.enter_context(nc.psum_tensor('PS1', [128, 512], F32))
        PS2 = _st.enter_context(nc.psum_tensor('PS2', [128, 512], F32))
        PS3 = _st.enter_context(nc.psum_tensor('PS3', [128, 512], F32))
        PS4 = _st.enter_context(nc.psum_tensor('PS4', [128, 512], F32))
        PS5 = _st.enter_context(nc.psum_tensor('PS5', [128, 512], F32))
        PS6 = _st.enter_context(nc.psum_tensor('PS6', [128, 512], F32))
        PS7 = _st.enter_context(nc.psum_tensor('PS7', [128, 512], F32))
        _sq = _st.enter_context(nc.semaphore('sq'))
        _gq = _st.enter_context(nc.semaphore('gq'))
        _w0 = _st.enter_context(nc.semaphore('w0'))
        _w1 = _st.enter_context(nc.semaphore('w1'))
        _g0 = _st.enter_context(nc.semaphore('g0'))
        _g1 = _st.enter_context(nc.semaphore('g1'))
        _g2 = _st.enter_context(nc.semaphore('g2'))
        _g3 = _st.enter_context(nc.semaphore('g3'))
        _g4 = _st.enter_context(nc.semaphore('g4'))
        _w2 = _st.enter_context(nc.semaphore('w2'))
        _w3 = _st.enter_context(nc.semaphore('w3'))
        _f0 = _st.enter_context(nc.semaphore('f0'))
        _f1 = _st.enter_context(nc.semaphore('f1'))
        _f2 = _st.enter_context(nc.semaphore('f2'))
        _f3 = _st.enter_context(nc.semaphore('f3'))
        _tp = _st.enter_context(nc.semaphore('tp'))
        _vp = _st.enter_context(nc.semaphore('vp'))
        _sp = _st.enter_context(nc.semaphore('sp'))
        _cc = _st.enter_context(nc.semaphore('cc'))
        _bar = _st.enter_context(nc.semaphore('bar'))
        blk = _st.enter_context(nc.Block())
        b = Bld(nc, blk, {'sq': _sq, 'gq': _gq, 'tp': _tp, 'vp': _vp,
                          'sp': _sp, 'cc': _cc, 'bar': _bar,
                          'w0': _w0, 'w1': _w1, 'w2': _w2, 'w3': _w3,
                          'f0': _f0, 'f1': _f1, 'f2': _f2, 'f3': _f3,
                          'g0': _g0, 'g1': _g1, 'g2': _g2, 'g3': _g3, 'g4': _g4})
        WS2 = [_w2, _w3]
        wsn2 = ['w2', 'w3']
        FS = [_f0, _f1]
        fsn = ['f0', 'f1']
        FS2 = [_f2, _f3]
        fsn2 = ['f2', 'f3']
        b.scratch_ps = PS7
        b.scratch_sm = SM
        b.ones_c = ones_c
        b.ident = ident_sb
        WS = [_w0, _w1]
        GS = [_g0, _g1, _g2, _g3]
        wsn = ['w0', 'w1']
        gsn = ['g0', 'g1', 'g2', 'g3']
        st_m = SM[0:1, 0:TOK]
        tmp_a = SM[0:1, 2048:2048 + TC]
        tmp_b = SM[0:1, 2560:2560 + TC]
        st_r = SM[32:33, 0:TOK]
        cvec = SM[32:33, 2048:3072]
        rden = SM[64:65, 0:TOK]
        gate_sm = SM[96:128, 0:TOK]

        def prolog_q(q):
            b.dma(q, ident_sb[:], c_ident, 'sq')
            b.dma(q, tri_sb[:], c_tri, 'sq')
            b.dma(q, blktri_sb[:], c_blktri, 'sq')
            b.dma(q, l2tm_sb[:], c_l2tm, 'sq')
            b.dma(q, lmask_sb[:], c_lmask, 'sq')
        blk.sync(prolog_q)
        b.prolog_sq = b.c['sq']

        def prolog_v(v):
            v.memset(ones_c[:], 1.0)
            v.memset(ones_r[:], 1.0)
        blk.vector(prolog_v)
        b.barrier()

        def emit_ln_stats(xbuf, tc0):
            sp0, tp0, vp0 = b.c['sp'], b.c['tp'], b.c['vp']

            def s_body(s):
                for dc in range(DC):
                    if dc >= 2:
                        s.wait_ge(_tp, tp0 + dc - 1)
                    b.inc(s.activation(out=A3[:, (dc % 2) * TC:(dc % 2 + 1) * TC],
                                       in_=xbuf[:, dc * TC:(dc + 1) * TC],
                                       func=AF.Square), 'sp', 1)
            blk.scalar(s_body)

            def t_body(t):
                for dc in range(DC):
                    t.matmul(PS6[0:1, 0:TC], ones_c[:], xbuf[:, dc * TC:(dc + 1) * TC],
                             start=(dc == 0), stop=(dc == DC - 1))
                    t.wait_ge(_sp, sp0 + dc + 1)
                    b.inc(t.matmul(PS7[0:1, 0:TC], ones_c[:],
                                   A3[:, (dc % 2) * TC:(dc % 2 + 1) * TC],
                                   start=(dc == 0), stop=(dc == DC - 1)), 'tp', 1)
            blk.tensor(t_body)
            b.chk('tp', tp0 + DC)
            b.chk('sp', sp0 + DC)
            emit_stats_finish(tc0)

        def emit_stats_finish(tc0):
            tdone = b.c['tp']

            def v1(v):
                v.wait_ge(_tp, tdone)
                v.tensor_scalar_mul(out=st_m[:, tc0:tc0 + TC], in0=PS6[0:1, 0:TC], scalar1=1.0 / D)
                m1 = b.inc(v.tensor_scalar(out=tmp_a, in0=PS7[0:1, 0:TC], scalar1=1.0 / D,
                                           scalar2=EPS_LN, op0=AT.mult, op1=AT.add), 'vp', 1)
                v.wait_ge(_vp, m1)
                m2 = b.inc(v.tensor_mul(out=tmp_b, in0=st_m[:, tc0:tc0 + TC],
                                        in1=st_m[:, tc0:tc0 + TC]), 'vp', 1)
                v.wait_ge(_vp, m2)
                b.inc(v.tensor_sub(out=tmp_a, in0=tmp_a, in1=tmp_b), 'vp', 1)
            blk.vector(v1)
            vdone = b.c['vp']

            def s1(s):
                s.wait_ge(_vp, vdone)
                b.inc(s.activation(out=tmp_b, in_=tmp_a, func=AF.Sqrt), 'sp', 1)
            blk.scalar(s1)
            sdone = b.c['sp']

            def v2(v):
                v.wait_ge(_sp, sdone)
                b.inc(v.reciprocal(out=st_r[:, tc0:tc0 + TC], in_=tmp_b), 'vp', 1)
            blk.vector(v2)

        def emit_ln_apply(xbuf, xnbuf, tc0):
            vp0 = b.c['vp']

            def t1(t):
                t.wait_ge(_vp, vp0)
                t.matmul(PS4[:, 0:TC], ones_r[0:1, :], st_m[:, tc0:tc0 + TC], start=True, stop=True)
                b.inc(t.matmul(PS5[:, 0:TC], ones_r[32:33, :], st_r[:, tc0:tc0 + TC],
                               start=True, stop=True), 'tp', 1)
            blk.tensor(t1)
            tdone = b.c['tp']

            def v1(v):
                v.wait_ge(_tp, tdone)
                for dc in range(DC):
                    i = v.tensor_sub(out=xnbuf[:, dc * TC:(dc + 1) * TC],
                                     in0=xbuf[:, dc * TC:(dc + 1) * TC], in1=PS4[:, 0:TC])
                    if dc == DC - 1:
                        m1 = b.inc(i, 'vp', 1)
                        v.wait_ge(_vp, m1)
                for dc in range(DC):
                    i = v.tensor_mul(out=xnbuf[:, dc * TC:(dc + 1) * TC],
                                     in0=xnbuf[:, dc * TC:(dc + 1) * TC], in1=PS5[:, 0:TC])
                b.inc(i, 'vp', 1)
            blk.vector(v1)

        def emit_qkv_stage(W_ap, Bias_ap, nfc, dst, xsrc, ln_here, rope=False,
                           gate=False):
            def q_bias(q):
                b.dma(q, bias_sb[:, 0:nfc], Bias_ap, 'sq')
                if gate:
                    b.dma(q, bias2_sb[:, 0:1], bgate, 'sq')
            blk.sync(q_bias)

            rope_pairs = []
            rope_mem = {}
            if rope:
                for c in range(8):
                    rope_pairs.append((c, c + 8))
                for c in range(8):
                    rope_pairs.append((16 + c, 24 + c))
                for (fe, fo) in rope_pairs:
                    rope_mem[fe] = (fe, fo)
                    rope_mem[fo] = (fe, fo)
                order = [f for pr in rope_pairs for f in pr] + list(range(32, nfc))
            else:
                order = list(range(nfc))
            if gate:
                order = order + [nfc]

            total = nfc + (1 if gate else 0)
            for tc in range(NTC):
                tc0 = tc * TC

                def q_x(q):
                    b.dma(q, sb3(A1[:], DC, TC)[:, :, 0:TC], d3(xsrc, DC)[:, :, tc0:tc0 + TC], 'sq')
                blk.sync(q_x)
                xld = b.c['sq']

                def tw(t):
                    t.wait_ge(_sq, xld)
                blk.tensor(tw)

                def sw(s):
                    s.wait_ge(_sq, xld)
                blk.scalar(sw)
                if ln_here:
                    emit_ln_stats(A1, tc0)
                emit_ln_apply(A1, A2, tc0)
                xn_ready = b.c['vp']

                tp0 = b.c['tp']
                sp0 = b.c['sp']
                vp0 = b.c['vp']
                wb0 = [b.c['w0'], b.c['w1']]
                gb0 = [b.c['g0'], b.c['g1'], b.c['g2'], b.c['g3']]
                g40 = b.c['g4']

                def pair_idx(fc):
                    pr = rope_mem[fc]
                    return rope_pairs.index(pr)

                def w_mark(idx):
                    return wb0[idx % 2] + 16 * (idx // 2 + 1)

                pos = {fc: idx for idx, fc in enumerate(order)}

                slot_use = {0: [], 1: [], 2: [], 3: []}
                for idx, fc in enumerate(order):
                    if gate and fc == nfc:
                        continue
                    if rope and fc in rope_mem:
                        continue
                    slot_use[idx % 4].append(idx)

                def g_mark(idx):
                    sl = idx % 4
                    k = slot_use[sl].index(idx) + 1
                    return gb0[sl] + 16 * k

                def g4_mark(pi):
                    return g40 + 32 * (pi + 1)

                def q_w(q):
                    for idx, fc in enumerate(order):
                        bb = idx % 2
                        if idx >= 2:
                            q.wait_ge(_tp, tp0 + idx - 1)
                        if gate and fc == nfc:
                            b.dma(q, sb3(A4[:, bb * 2048:bb * 2048 + DC * 32], DC, 32),
                                  d3(wgate, DC), wsn[bb])
                        else:
                            b.dma(q, sb3(A4[:, bb * 2048:(bb + 1) * 2048], DC, 128),
                                  d3(W_ap, DC)[:, :, fc * 128:(fc + 1) * 128], wsn[bb])
                blk.sync(q_w)

                def t_mm(t):
                    t.wait_ge(_vp, xn_ready)
                    for idx, fc in enumerate(order):
                        bb = idx % 2
                        is_gate = gate and fc == nfc
                        wid = 32 if is_gate else 128
                        ps = PS0 if bb == 0 else PS1
                        t.wait_ge(WS[bb], w_mark(idx))
                        if idx >= 2:
                            t.wait_ge(_sp, sp0 + idx - 1)
                        for dc in range(DC):
                            i = t.matmul(ps[0:wid, 0:TC],
                                         A4[:, bb * 2048 + dc * wid:bb * 2048 + (dc + 1) * wid],
                                         A2[:, dc * TC:(dc + 1) * TC],
                                         start=(dc == 0), stop=(dc == DC - 1))
                        b.inc(i, 'tp', 1)
                blk.tensor(t_mm)
                b.chk('tp', tp0 + total)

                def s_evac(s):
                    for idx, fc in enumerate(order):
                        ps = PS0 if idx % 2 == 0 else PS1
                        slot = idx % 4
                        s.wait_ge(_tp, tp0 + idx + 1)
                        if idx >= 4:
                            pidx = idx - 4
                            pfc = order[pidx]
                            if rope and pfc in rope_mem:
                                s.wait_ge(_vp, vp0 + 2 * (pair_idx(pfc) + 1))
                            elif not (gate and pfc == nfc):
                                s.wait_ge(GS[pidx % 4], g_mark(pidx))
                        if gate and fc == nfc:
                            i = s.activation(out=gate_sm[:, tc0:tc0 + TC], in_=ps[0:32, 0:TC],
                                             func=AF.Sigmoid, bias=bias2_sb[96:128, 0:1])
                        else:
                            i = s.activation(out=A3[:, slot * TC:(slot + 1) * TC],
                                             in_=ps[0:128, 0:TC],
                                             func=AF.Identity, bias=bias_sb[:, fc:fc + 1])
                        b.inc(i, 'sp', 1)
                blk.scalar(s_evac)
                b.chk('sp', sp0 + total)

                if rope:
                    def v_rope(v):
                        for pi, (fe, fo) in enumerate(rope_pairs):
                            v.wait_ge(_sp, sp0 + max(pos[fe], pos[fo]) + 1)
                            if pi >= 1:
                                v.wait_ge(_g4, g4_mark(pi - 1))
                            xe = A3[:, (pos[fe] % 4) * TC:(pos[fe] % 4 + 1) * TC]
                            xo = A3[:, (pos[fo] % 4) * TC:(pos[fo] % 4 + 1) * TC]
                            oe = A5[:, 0:TC]
                            oo = A5[:, TC:2 * TC]
                            t1_ = A5[:, 2 * TC:3 * TC]
                            t2_ = A5[:, 3 * TC:4 * TC]
                            cosT = A6[:, tc0:tc0 + TC]
                            sinT = A6[:, 4096 + tc0:4096 + tc0 + TC]
                            v.tensor_mul(out=oe, in0=xe, in1=cosT)
                            v.tensor_mul(out=t1_, in0=xo, in1=sinT)
                            v.tensor_mul(out=oo, in0=xe, in1=sinT)
                            m = b.inc(v.tensor_mul(out=t2_, in0=xo, in1=cosT), 'vp', 1)
                            v.wait_ge(_vp, m)
                            v.tensor_sub(out=oe, in0=oe, in1=t1_)
                            b.inc(v.tensor_add(out=oo, in0=oo, in1=t2_), 'vp', 1)
                    blk.vector(v_rope)
                    b.chk('vp', vp0 + 2 * len(rope_pairs))

                def g_st(g):
                    for idx, fc in enumerate(order):
                        if gate and fc == nfc:
                            continue
                        if rope and fc in rope_mem:
                            pr = rope_mem[fc]
                            if fc != pr[1]:
                                continue
                            g.wait_ge(_vp, vp0 + 2 * (pair_idx(fc) + 1))
                            da_e, r0_e = dst(pr[0])
                            da_o, r0_o = dst(pr[1])
                            b.dma(g, d3(da_e, 0)[:, r0_e // 128, tc0:tc0 + TC], A5[:, 0:TC], 'g4')
                            b.dma(g, d3(da_o, 0)[:, r0_o // 128, tc0:tc0 + TC], A5[:, TC:2 * TC], 'g4')
                        else:
                            slot = idx % 4
                            g.wait_ge(_sp, sp0 + idx + 1)
                            da, r0 = dst(fc)
                            b.dma(g, d3(da, 0)[:, r0 // 128, tc0:tc0 + TC],
                                  A3[:, slot * TC:(slot + 1) * TC], gsn[slot])
                blk.gpsimd(g_st)
                b.barrier()

        def emit_attn_a():
            SEQ = L
            NS_T = SEQ // 128
            koff, voff, qoff2, vtoff2 = 0, SEQ, 2 * SEQ, 2 * SEQ + TOK

            def nS(i):
                return 2 * i + 2 if PHASE2 else i + 1

            def stcol(St):
                if PHASE2:
                    return (St // 2) * 128 + (St % 2) * 1024
                return St * 128

            if PHASE2:
                def q_am(q):
                    b.dma(q, A6[:, 4096:4096 + QT * 256], c_amask, 'sq')
                blk.sync(q_am)
                b.barrier()

            h_tend = {}
            h_vend = {}
            h_send = {}
            h_gend = {}
            for h in range(NH):
                slot = A1 if h % 2 == 0 else A5

                def q_head(q, h=h, slot=slot):
                    if PHASE2:
                        q.wait_ge(_cc, 1)
                    if h >= 2:
                        q.wait_ge(_tp, h_tend[h - 2])
                    sl = wsn[h % 2]
                    if PHASE2:
                        b.dma(q, slot[:, koff:koff + TOK], ag1_out[h * 128:(h + 1) * 128, :], sl)
                        b.dma(q, slot[:, koff + TOK:koff + 2 * TOK],
                              ag1_out[2 * D + h * 128:2 * D + (h + 1) * 128, :], sl)
                        b.dma(q, slot[:, voff:voff + TOK],
                              ag1_out[D + h * 128:D + (h + 1) * 128, :], sl)
                        b.dma(q, slot[:, voff + TOK:voff + 2 * TOK],
                              ag1_out[3 * D + h * 128:3 * D + (h + 1) * 128, :], sl)
                    else:
                        b.dma(q, slot[:, koff:koff + SEQ], ag1_in[h * 128:(h + 1) * 128, :], sl)
                        b.dma(q, slot[:, voff:voff + SEQ], ag1_in[D + h * 128:D + (h + 1) * 128, :], sl)
                    b.dma(q, slot[:, qoff2:qoff2 + TOK], scr_q[h * 128:(h + 1) * 128, :], sl)
                blk.sync(q_head)
                ld_mark = b.c[wsn[h % 2]]

                tp0 = b.c['tp']
                vp0 = b.c['vp']

                def t_vt(t, slot=slot, ld_mark=ld_mark, tp0=tp0, vp0=vp0, h=h):
                    t.wait_ge(WS[h % 2], ld_mark)
                    if h >= 1:
                        t.wait_ge(_vp, h_vend[h - 1])
                        t.wait_ge(_sp, h_send[h - 1])
                    for St in range(NS_T):
                        ps = PS2 if St % 2 == 0 else PS3
                        if St >= 2:
                            t.wait_ge(_vp, vp0 + St - 1)
                        b.inc(t.transpose(ps[0:128, 0:128],
                                          slot[:, voff + stcol(St):voff + stcol(St) + 128],
                                          ident_sb[:]), 'tp', 1)
                blk.tensor(t_vt)

                def v_vt(v, slot=slot, tp0=tp0):
                    for St in range(NS_T):
                        ps = PS2 if St % 2 == 0 else PS3
                        v.wait_ge(_tp, tp0 + St + 1)
                        b.inc(v.tensor_copy(out=slot[:, vtoff2 + St * 128:vtoff2 + (St + 1) * 128],
                                            in_=ps[0:128, 0:128]), 'vp', 1)
                blk.vector(v_vt)
                b.chk('tp', tp0 + NS_T)
                b.chk('vp', vp0 + NS_T)
                vt_done = b.c['vp']

                T0 = b.c['tp']
                S0 = b.c['sp']
                V0 = b.c['vp']

                def t_off(i):
                    return T0 + sum(2 * nS(j) + 1 for j in range(i))

                def sc_mark(i, St):
                    return t_off(i) + 2 * St + 1

                def od_mark(i, St):
                    return t_off(i) + 2 * St + 2

                def bc_mark(i):
                    return t_off(i) + 2 * nS(i) + 1

                def s_off(i):
                    return S0 + sum(nS(j) + 1 for j in range(i))

                def exp_mark(i, St):
                    return s_off(i) + St + 1

                def cpy_mark(i):
                    return s_off(i) + nS(i) + 1

                MP = 2 if PHASE2 else 1

                def mask_mark(i, j=0):
                    return V0 + (MP + 2) * i + j + 1

                def rden_mark(i):
                    return V0 + (MP + 2) * i + MP + 1

                def outev_mark(i):
                    return V0 + (MP + 2) * i + MP + 2

                def masked_tiles(i):
                    if PHASE2:
                        return (nS(i) - 2, nS(i) - 1)
                    return (nS(i) - 1,)

                def t_all(t, h=h, slot=slot, vt_done=vt_done):
                    t.wait_ge(_vp, vt_done)
                    for i in range(QT):
                        n = nS(i)
                        out_ps = PS2 if i % 2 == 0 else PS3
                        den_ps = PS6 if i % 2 == 0 else PS7
                        if i >= 2:
                            t.wait_ge(_vp, outev_mark(i - 2))
                        for St in range(n):
                            ps = PS0 if St % 2 == 0 else PS1
                            if St >= 2:
                                t.wait_ge(_sp, exp_mark(i, St - 2))
                            b.inc(t.matmul(ps[0:128, 0:128],
                                           slot[:, koff + stcol(St):koff + stcol(St) + 128],
                                           slot[:, qoff2 + i * 128:qoff2 + (i + 1) * 128],
                                           start=True, stop=True), 'tp', 1)
                            if St in masked_tiles(i):
                                t.wait_ge(_vp, mask_mark(i, St - masked_tiles(i)[0]))
                            else:
                                t.wait_ge(_sp, exp_mark(i, St))
                            e_ap = A2[:, (St % 4) * 128:(St % 4 + 1) * 128]
                            t.matmul(out_ps[0:128, 0:128],
                                     slot[:, vtoff2 + St * 128:vtoff2 + (St + 1) * 128],
                                     e_ap, start=(St == 0), stop=(St == n - 1))
                            b.inc(t.matmul(den_ps[0:1, 0:128], ones_c[:], e_ap,
                                           start=(St == 0), stop=(St == n - 1)), 'tp', 1)
                        t.wait_ge(_vp, rden_mark(i))
                        if i >= 1:
                            t.wait_ge(_sp, cpy_mark(i - 1))
                        b.inc(t.matmul(PS4[:, 0:128], ones_r[64:65, :],
                                       rden[:, i * 128:(i + 1) * 128], start=True, stop=True),
                              'tp', 1)
                blk.tensor(t_all)

                def s_all(s, h=h):
                    if h >= 1:
                        s.wait_ge(_tp, h_tend[h - 1])
                    for i in range(QT):
                        n = nS(i)
                        for St in range(n):
                            ps = PS0 if St % 2 == 0 else PS1
                            s.wait_ge(_tp, sc_mark(i, St))
                            if St >= 4:
                                s.wait_ge(_tp, od_mark(i, St - 4))
                            elif i >= 1:
                                s.wait_ge(_tp, od_mark(i - 1, nS(i - 1) - 1))
                            b.inc(s.activation(out=A2[:, (St % 4) * 128:(St % 4 + 1) * 128],
                                               in_=ps[0:128, 0:128], func=AF.Exp, scale=SC_A),
                                  'sp', 1)
                        s.wait_ge(_tp, bc_mark(i))
                        if i >= 2:
                            s.wait_ge(_vp, outev_mark(i - 2))
                        b.inc(s.activation(out=A3[:, (i % 2) * 128:(i % 2 + 1) * 128],
                                           in_=PS4[:, 0:128], func=AF.Copy), 'sp', 1)
                blk.scalar(s_all)

                def v_all(v, h=h):
                    if h >= 1:
                        v.wait_ge(_g4, h_gend[h - 1])
                    for i in range(QT):
                        n = nS(i)
                        mts = masked_tiles(i)
                        for St in mts:
                            v.wait_ge(_sp, exp_mark(i, St))
                            if PHASE2:
                                mask_ap = A6[:, 4096 + (2 * i + (St - mts[0])) * 128:
                                             4096 + (2 * i + (St - mts[0]) + 1) * 128]
                            else:
                                mask_ap = tri_sb[:]
                            b.inc(v.tensor_mul(out=A2[:, (St % 4) * 128:(St % 4 + 1) * 128],
                                               in0=A2[:, (St % 4) * 128:(St % 4 + 1) * 128],
                                               in1=mask_ap), 'vp', 1)
                        v.wait_ge(_tp, od_mark(i, n - 1))
                        den_ps = PS6 if i % 2 == 0 else PS7
                        b.inc(v.reciprocal(out=rden[:, i * 128:(i + 1) * 128],
                                           in_=den_ps[0:1, 0:128]), 'vp', 1)
                        v.wait_ge(_sp, cpy_mark(i))
                        out_ps = PS2 if i % 2 == 0 else PS3
                        b.inc(v.tensor_mul(out=A6[:, i * 128:(i + 1) * 128],
                                           in0=out_ps[0:128, 0:128],
                                           in1=A3[:, (i % 2) * 128:(i % 2 + 1) * 128]), 'vp', 1)
                blk.vector(v_all)

                nmm = sum(2 * nS(j) + 1 for j in range(QT))
                b.chk('tp', T0 + nmm)
                b.chk('sp', S0 + sum(nS(j) + 1 for j in range(QT)))
                b.chk('vp', V0 + (MP + 2) * QT)

                h_tend[h] = b.c['tp']
                h_vend[h] = b.c['vp']
                h_send[h] = b.c['sp']
                st_mark = b.c['vp']

                def g_out(g, h=h, st_mark=st_mark):
                    g.wait_ge(_vp, st_mark)
                    if h >= 1:
                        g.wait_ge(_g4, h_gend[h - 1])
                    b.dma(g, scr_ao[h * 128:(h + 1) * 128, :], A6[:, 0:TOK], 'g4')
                blk.gpsimd(g_out)
                h_gend[h] = b.c['g4']
            b.barrier()

        def emit_proj_res(src_ap, W_ap, Bias_ap, xold_ap, dst_ap, want_stats):
            def q_bias(q):
                b.dma(q, bias_sb[:, 0:16], Bias_ap, 'sq')
            blk.sync(q_bias)

            for tc in range(NTC):
                tc0 = tc * TC

                def q_in(q):
                    b.dma(q, sb3(A1[:], DC, TC)[:, :, 0:TC], d3(src_ap, DC)[:, :, tc0:tc0 + TC], 'sq')
                    b.dma(q, sb3(A5[:], DC, TC)[:, :, 0:TC], d3(xold_ap, DC)[:, :, tc0:tc0 + TC], 'sq')
                blk.sync(q_in)
                xld = b.c['sq']

                K_ = 2 if want_stats else 1
                tp0 = b.c['tp']
                vp0 = b.c['vp']
                sp0 = b.c['sp']
                wb0 = [b.c['w0'], b.c['w1']]
                gb0 = [b.c['g0'], b.c['g1'], b.c['g2'], b.c['g3']]

                def q_w(q):
                    for fc in range(DC):
                        bb = fc % 2
                        if fc >= 2:
                            q.wait_ge(_tp, tp0 + K_ * (fc - 1))
                        b.dma(q, sb3(A4[:, bb * 2048:(bb + 1) * 2048], DC, 128),
                              d3(W_ap, DC)[:, :, fc * 128:(fc + 1) * 128], wsn[bb])
                blk.sync(q_w)

                def t_mm(t):
                    t.wait_ge(_sq, xld)
                    for fc in range(DC):
                        bb = fc % 2
                        ps = PS0 if bb == 0 else PS1
                        t.wait_ge(WS[bb], wb0[bb] + 16 * (fc // 2 + 1))
                        if fc >= 2:
                            t.wait_ge(_vp, vp0 + fc - 1)
                        for dc in range(DC):
                            i = t.matmul(ps[0:128, 0:TC],
                                         A4[:, bb * 2048 + dc * 128:bb * 2048 + (dc + 1) * 128],
                                         A1[:, dc * TC:(dc + 1) * TC],
                                         start=(dc == 0), stop=(dc == DC - 1))
                        b.inc(i, 'tp', 1)
                        if want_stats:
                            t.wait_ge(_vp, vp0 + fc + 1)
                            t.matmul(PS6[0:1, 0:TC], ones_c[:],
                                     A2[:, (fc % 4) * TC:(fc % 4 + 1) * TC],
                                     start=(fc == 0), stop=(fc == DC - 1))
                            t.wait_ge(_sp, sp0 + fc + 1)
                            b.inc(t.matmul(PS7[0:1, 0:TC], ones_c[:],
                                           A3[:, (fc % 2) * TC:(fc % 2 + 1) * TC],
                                           start=(fc == 0), stop=(fc == DC - 1)), 'tp', 1)
                blk.tensor(t_mm)

                def v_evac(v):
                    for fc in range(DC):
                        ps = PS0 if fc % 2 == 0 else PS1
                        v.wait_ge(_tp, tp0 + K_ * fc + 1)
                        if fc >= 4:
                            v.wait_ge(GS[fc % 4], gb0[fc % 4] + 16 * (fc // 4))
                            if want_stats:
                                v.wait_ge(_sp, sp0 + fc - 3)
                        b.inc(v.scalar_tensor_tensor(
                            out=A2[:, (fc % 4) * TC:(fc % 4 + 1) * TC],
                            in0=ps[0:128, 0:TC], scalar=bias_sb[:, fc:fc + 1],
                            in1=A5[:, fc * TC:(fc + 1) * TC], op0=AT.add, op1=AT.add), 'vp', 1)
                blk.vector(v_evac)

                if want_stats:
                    def s_sq(s):
                        for fc in range(DC):
                            s.wait_ge(_vp, vp0 + fc + 1)
                            if fc >= 2:
                                s.wait_ge(_tp, tp0 + 2 * fc)
                            b.inc(s.activation(out=A3[:, (fc % 2) * TC:(fc % 2 + 1) * TC],
                                               in_=A2[:, (fc % 4) * TC:(fc % 4 + 1) * TC],
                                               func=AF.Square), 'sp', 1)
                    blk.scalar(s_sq)

                def g_st(g):
                    for fc in range(DC):
                        g.wait_ge(_vp, vp0 + fc + 1)
                        b.dma(g, d3(dst_ap, DC)[:, fc, tc0:tc0 + TC],
                              A2[:, (fc % 4) * TC:(fc % 4 + 1) * TC], gsn[fc % 4])
                blk.gpsimd(g_st)

                b.chk('tp', tp0 + K_ * DC)
                b.chk('vp', vp0 + DC)
                if want_stats:
                    b.chk('sp', sp0 + DC)
                    emit_stats_finish(tc0)
                b.barrier()

        def emit_ffn(xold_ap, dst_ap, want_stats):
            GRP = 4
            NG = ICH // GRP
            BF16 = mybir.dt.bfloat16
            xh_bf = A1[:, 0:4096].bitcast(BF16)
            xl_bf = A1[:, 4096:8192].bitcast(BF16)
            hh_bf = A5[:, 0:1024].bitcast(BF16)
            hl_bf = A5[:, 1024:2048].bitcast(BF16)
            dw_bf = A4[:, 0:512].bitcast(BF16)

            def q_sc(q):
                b.dma(q, gsc_sb[:, 0:44], gs, 'sq')
                b.dma(q, usc_sb[:, 0:44], us, 'sq')
                b.dma(q, dsc_sb[:, 0:16], ds, 'sq')
            blk.sync(q_sc)

            for tc in range(NTC):
                tc0 = tc * TC

                def q_x(q):
                    b.dma(q, sb3(A6[:], DC, TC)[:, :, 0:TC], d3(xold_ap, DC)[:, :, tc0:tc0 + TC], 'sq')
                blk.sync(q_x)
                xld = b.c['sq']

                def tw(t):
                    t.wait_ge(_sq, xld)
                blk.tensor(tw)
                emit_ln_apply(A6, A2, tc0)
                xn_ready = b.c['vp']

                def v_split(v):
                    v.wait_ge(_vp, xn_ready)
                    for dc in range(DC):
                        m = b.inc(v.tensor_copy(out=xh_bf[:, dc * TC:(dc + 1) * TC],
                                                in_=A2[:, dc * TC:(dc + 1) * TC]), 'vp', 1)
                        v.wait_ge(_vp, m)
                        b.inc(v.tensor_sub(out=xl_bf[:, dc * TC:(dc + 1) * TC],
                                           in0=A2[:, dc * TC:(dc + 1) * TC],
                                           in1=xh_bf[:, dc * TC:(dc + 1) * TC]), 'vp', 1)
                blk.vector(v_split)
                split_done = b.c['vp']

                def xh_r(dc):
                    return xh_bf[:, dc * TC:(dc + 1) * TC]

                def xl_r(dc):
                    return xl_bf[:, dc * TC:(dc + 1) * TC]

                tp0 = b.c['tp']
                sp0 = b.c['sp']
                vp0 = b.c['vp']
                wb0 = [b.c['f0'], b.c['f1']]
                wb20 = [b.c['f2'], b.c['f3']]

                gu_mark = {}
                dn_tp_mark = {}
                tpc = tp0
                for ic in range(ICH):
                    tpc += 1; gu_mark[(ic, 0)] = tpc
                    tpc += 1; gu_mark[(ic, 1)] = tpc
                    if ic % GRP == GRP - 1:
                        grp = ic // GRP
                        for fcp in range(DC):
                            tpc += 1; dn_tp_mark[(grp, fcp)] = tpc
                tp_end = tpc

                hl_mark = {}
                dn_vp_mark = {}
                vpc = vp0
                for ic in range(ICH):
                    vpc += 5; hl_mark[ic] = vpc
                    if ic % GRP == GRP - 1:
                        grp = ic // GRP
                        for fcp in range(DC):
                            vpc += 1; dn_vp_mark[(grp, fcp)] = vpc
                vp_end = vpc

                def silu_mark(ic, which):
                    return sp0 + 2 * ic + which + 1

                def w_mark_gu(ic):
                    return wb0[ic % 2] + 32 * (ic // 2 + 1)

                def w_mark_dn(grp, fcp):
                    k = grp * DC + fcp
                    return wb20[k % 2] + 16 * (k // 2 + 1)

                def q_w(g):
                    for ic in range(ICH):
                        bb = ic % 2
                        if ic >= 2:
                            g.wait_ge(_tp, gu_mark[(ic - 2, 1)])
                        b.dma(g, sb3(RW[:, bb * 2048:(bb + 1) * 2048], DC, 128),
                              d3(gwT, DC)[:, :, ic * 128:(ic + 1) * 128], fsn[bb])
                        b.dma(g, sb3(RW[:, 4096 + bb * 2048:4096 + (bb + 1) * 2048], DC, 128),
                              d3(uwT, DC)[:, :, ic * 128:(ic + 1) * 128], fsn[bb])
                        if ic % GRP == GRP - 1:
                            grp = ic // GRP
                            for fcp in range(DC):
                                k = grp * DC + fcp
                                bb2 = k % 2
                                if k >= 2:
                                    kk = k - 2
                                    g.wait_ge(_tp, dn_tp_mark[(kk // DC, kk % DC)])
                                b.dma(g, sb3(dw_bf[:, bb2 * 512:(bb2 + 1) * 512], GRP, 128),
                                      dwT.rearrange("(g p) t -> p g t", p=128)[
                                          :, 4 * grp:4 * grp + GRP, fcp * 128:(fcp + 1) * 128],
                                      fsn2[bb2])
                blk.gpsimd(q_w)

                def t_mm(t):
                    t.wait_ge(_vp, split_done)
                    for ic in range(ICH):
                        bb = ic % 2
                        t.wait_ge(FS[bb], w_mark_gu(ic))
                        if ic >= 1:
                            t.wait_ge(_sp, silu_mark(ic - 1, 0))
                        for p in range(2):
                            for dc in range(DC):
                                i = t.matmul(PS0[0:128, 0:TC],
                                             RW[:, bb * 2048 + dc * 128:bb * 2048 + (dc + 1) * 128],
                                             xh_r(dc) if p == 0 else xl_r(dc),
                                             start=(p == 0 and dc == 0),
                                             stop=(p == 1 and dc == DC - 1))
                        b.inc(i, 'tp', 1)
                        if ic >= 1:
                            t.wait_ge(_sp, silu_mark(ic - 1, 1))
                        for p in range(2):
                            for dc in range(DC):
                                i = t.matmul(PS1[0:128, 0:TC],
                                             RW[:, 4096 + bb * 2048 + dc * 128:4096 + bb * 2048 + (dc + 1) * 128],
                                             xh_r(dc) if p == 0 else xl_r(dc),
                                             start=(p == 0 and dc == 0),
                                             stop=(p == 1 and dc == DC - 1))
                        b.inc(i, 'tp', 1)
                        if ic % GRP == GRP - 1:
                            grp = ic // GRP
                            t.wait_ge(_vp, hl_mark[ic])
                            for fcp in range(DC):
                                k = grp * DC + fcp
                                bb2 = k % 2
                                ps = PS2 if fcp % 2 == 0 else PS3
                                t.wait_ge(FS2[bb2], w_mark_dn(grp, fcp))
                                if k >= 2:
                                    kk = k - 2
                                    t.wait_ge(_vp, dn_vp_mark[(kk // DC, kk % DC)])
                                for p in range(2):
                                    for icg in range(GRP):
                                        hsrc = (hh_bf[:, icg * TC:(icg + 1) * TC] if p == 0
                                                else hl_bf[:, icg * TC:(icg + 1) * TC])
                                        i = t.matmul(ps[0:128, 0:TC],
                                                     dw_bf[:, bb2 * 512 + icg * 128:bb2 * 512 + (icg + 1) * 128],
                                                     hsrc,
                                                     start=(p == 0 and icg == 0),
                                                     stop=(p == 1 and icg == GRP - 1))
                                b.inc(i, 'tp', 1)
                blk.tensor(t_mm)
                b.chk('tp', tp_end)

                def s_ev(s):
                    for ic in range(ICH):
                        s.wait_ge(_tp, gu_mark[(ic, 0)])
                        if ic >= 1:
                            s.wait_ge(_vp, vp0 + 5 * (ic - 1) + 1 + 16 * ((ic - 1) // GRP))
                        b.inc(s.activation(out=A3[:, 4 * TC:5 * TC], in_=PS0[0:128, 0:TC],
                                           func=AF.Silu, scale=gsc_sb[:, ic:ic + 1]), 'sp', 1)
                        s.wait_ge(_tp, gu_mark[(ic, 1)])
                        b.inc(s.activation(out=A3[:, 5 * TC:6 * TC], in_=PS1[0:128, 0:TC],
                                           func=AF.Copy, scale=usc_sb[:, ic:ic + 1]), 'sp', 1)
                blk.scalar(s_ev)
                b.chk('sp', sp0 + 2 * ICH)

                def v_h(v):
                    for ic in range(ICH):
                        icg = ic % GRP
                        v.wait_ge(_sp, silu_mark(ic, 1))
                        if ic >= 1:
                            v.wait_ge(_vp, hl_mark[ic - 1])
                        if ic >= GRP:
                            v.wait_ge(_tp, dn_tp_mark[(ic // GRP - 1, DC - 1)])
                        hbuf = A3[:, 0:TC]
                        m = b.inc(v.tensor_mul(out=hbuf, in0=A3[:, 4 * TC:5 * TC],
                                               in1=A3[:, 5 * TC:6 * TC]), 'vp', 1)
                        v.wait_ge(_vp, m)
                        m = b.inc(v.tensor_copy(out=hh_bf[:, icg * TC:(icg + 1) * TC],
                                                in_=hbuf), 'vp', 1)
                        v.wait_ge(_vp, m)
                        m = b.inc(v.tensor_sub(out=hl_bf[:, icg * TC:(icg + 1) * TC],
                                               in0=hbuf,
                                               in1=hh_bf[:, icg * TC:(icg + 1) * TC]), 'vp', 1)
                        v.wait_ge(_vp, m)
                        b.inc(v.memset(SM[64:65, 3068:3069], 0.0), 'vp', 1)
                        b.inc(v.memset(SM[64:65, 3067:3068], 0.0), 'vp', 1)
                        if ic % GRP == GRP - 1:
                            grp = ic // GRP
                            for fcp in range(DC):
                                ps = PS2 if fcp % 2 == 0 else PS3
                                v.wait_ge(_tp, dn_tp_mark[(grp, fcp)])
                                if grp >= 1:
                                    v.wait_ge(_vp, dn_vp_mark[(grp - 1, fcp)])
                                if grp == 0:
                                    i = v.tensor_copy(out=A6[:, fcp * TC:(fcp + 1) * TC],
                                                      in_=ps[0:128, 0:TC])
                                else:
                                    i = v.tensor_add(out=A6[:, fcp * TC:(fcp + 1) * TC],
                                                     in0=A6[:, fcp * TC:(fcp + 1) * TC],
                                                     in1=ps[0:128, 0:TC])
                                b.inc(i, 'vp', 1)
                blk.vector(v_h)
                b.chk('vp', vp_end)
                acc_done = b.c['vp']

                def q_xold(q):
                    q.wait_ge(_vp, acc_done)
                    b.dma(q, sb3(A5[:], DC, TC)[:, :, 0:TC], d3(xold_ap, DC)[:, :, tc0:tc0 + TC], 'sq')
                blk.sync(q_xold)
                xold_ld = b.c['sq']

                fin_vp0 = b.c['vp']
                fin_sp0 = b.c['sp']
                fin_tp0 = b.c['tp']
                gb0 = [b.c['g0'], b.c['g1'], b.c['g2'], b.c['g3']]

                def v_fin(v):
                    v.wait_ge(_sq, xold_ld)
                    v.wait_ge(_vp, acc_done)
                    for fcp in range(DC):
                        if fcp >= 4:
                            v.wait_ge(GS[fcp % 4], gb0[fcp % 4] + 16 * (fcp // 4))
                            if want_stats:
                                v.wait_ge(_sp, fin_sp0 + fcp - 3)
                                v.wait_ge(_tp, fin_tp0 + fcp - 3)
                        b.inc(v.scalar_tensor_tensor(
                            out=A2[:, (fcp % 4) * TC:(fcp % 4 + 1) * TC],
                            in0=A6[:, fcp * TC:(fcp + 1) * TC], scalar=dsc_sb[:, fcp:fcp + 1],
                            in1=A5[:, fcp * TC:(fcp + 1) * TC], op0=AT.mult, op1=AT.add), 'vp', 1)
                blk.vector(v_fin)

                def g_fin2(g):
                    for fcp in range(DC):
                        g.wait_ge(_vp, fin_vp0 + fcp + 1)
                        b.dma(g, d3(dst_ap, DC)[:, fcp, tc0:tc0 + TC],
                              A2[:, (fcp % 4) * TC:(fcp % 4 + 1) * TC], gsn[fcp % 4])
                blk.gpsimd(g_fin2)

                if want_stats:
                    def s_sq2(s):
                        for fcp in range(DC):
                            s.wait_ge(_vp, fin_vp0 + fcp + 1)
                            if fcp >= 2:
                                s.wait_ge(_tp, fin_tp0 + fcp - 1)
                            b.inc(s.activation(out=A3[:, (fcp % 2) * TC:(fcp % 2 + 1) * TC],
                                               in_=A2[:, (fcp % 4) * TC:(fcp % 4 + 1) * TC],
                                               func=AF.Square), 'sp', 1)
                    blk.scalar(s_sq2)

                    def t_st(t):
                        for fcp in range(DC):
                            t.wait_ge(_vp, fin_vp0 + fcp + 1)
                            t.matmul(PS6[0:1, 0:TC], ones_c[:],
                                     A2[:, (fcp % 4) * TC:(fcp % 4 + 1) * TC],
                                     start=(fcp == 0), stop=(fcp == DC - 1))
                            t.wait_ge(_sp, fin_sp0 + fcp + 1)
                            b.inc(t.matmul(PS7[0:1, 0:TC], ones_c[:],
                                           A3[:, (fcp % 2) * TC:(fcp % 2 + 1) * TC],
                                           start=(fcp == 0), stop=(fcp == DC - 1)), 'tp', 1)
                    blk.tensor(t_st)
                    emit_stats_finish(tc0)
                b.barrier()

        def emit_small_proj(W_ap, Bias_ap, nfc, rhs, dstf, mode='copy', base=None,
                            store=None, bias_n=None):
            bias_n = bias_n or nfc

            def q_bias(q):
                b.dma(q, bias_sb[:, 0:bias_n], Bias_ap, 'sq')
            blk.sync(q_bias)
            bld = b.c['sq']

            tp0 = b.c['tp']
            sp0 = b.c['sp']
            vp0 = b.c['vp']
            wb0 = [b.c['w0'], b.c['w1']]
            gb0 = b.c['g0']

            def q_w(q):
                for fc in range(nfc):
                    bb = fc % 2
                    if fc >= 2:
                        q.wait_ge(_tp, tp0 + fc - 1)
                    b.dma(q, sb3(A4[:, bb * 2048:(bb + 1) * 2048], DC, 128),
                          d3(W_ap, DC)[:, :, fc * 128:(fc + 1) * 128], wsn[bb])
            blk.sync(q_w)

            def t_mm(t):
                t.wait_ge(_sq, bld)
                for fc in range(nfc):
                    bb = fc % 2
                    ps = PS0 if bb == 0 else PS1
                    t.wait_ge(WS[bb], wb0[bb] + 16 * (fc // 2 + 1))
                    if fc >= 2:
                        if mode == 'copy':
                            t.wait_ge(_sp, sp0 + fc - 1)
                        else:
                            t.wait_ge(_vp, vp0 + fc - 1)
                    for dc in range(DC):
                        i = t.matmul(ps[0:128, 0:OWNLM],
                                     A4[:, bb * 2048 + dc * 128:bb * 2048 + (dc + 1) * 128],
                                     rhs(dc), start=(dc == 0), stop=(dc == DC - 1))
                    b.inc(i, 'tp', 1)
            blk.tensor(t_mm)

            if mode == 'copy':
                def s_ev(s):
                    for fc in range(nfc):
                        ps = PS0 if fc % 2 == 0 else PS1
                        s.wait_ge(_tp, tp0 + fc + 1)
                        b.inc(s.activation(out=dstf(fc), in_=ps[0:128, 0:OWNLM],
                                           func=AF.Identity, bias=bias_sb[:, fc:fc + 1]), 'sp', 1)
                blk.scalar(s_ev)
            else:
                def v_ev(v):
                    for fc in range(nfc):
                        ps = PS0 if fc % 2 == 0 else PS1
                        v.wait_ge(_tp, tp0 + fc + 1)
                        b.inc(v.scalar_tensor_tensor(out=dstf(fc), in0=ps[0:128, 0:OWNLM],
                                                     scalar=bias_sb[:, fc:fc + 1],
                                                     in1=base(fc), op0=AT.add, op1=AT.add),
                              'vp', 1)
                blk.vector(v_ev)

            if store is not None:
                da, row0f = store

                def g_s(g):
                    for fc in range(nfc):
                        if mode == 'copy':
                            g.wait_ge(_sp, sp0 + fc + 1)
                        else:
                            g.wait_ge(_vp, vp0 + fc + 1)
                        b.dma(g, d3(da, 0)[:, row0f(fc) // 128, :], dstf(fc), 'g0')
                blk.gpsimd(g_s)

        def emit_attn_b():
            def q_cs(q):
                b.dma(q, A6[:, 0:TOK], c_cos, 'sq')
                b.dma(q, A6[:, 4096:4096 + TOK], c_sin, 'sq')
                b.dma(q, pqa_sb[:], c_pqa, 'sq')
            blk.sync(q_cs)

            def dst_qkv_b(fc):
                if fc < 16:
                    return (scr_q, fc * 128)
                if fc < 32:
                    return (kvb, (fc - 16) * 128)
                return (kvb, D + (fc - 32) * 128)

            emit_qkv_stage(wqkv_b, bqkv_b, 48, dst_qkv_b, xres, ln_here=False,
                           rope=True, gate=True)

            gt_tp0 = b.c['tp']
            gt_vp0 = b.c['vp']

            def t_gt(t):
                for i in range(QT):
                    ps = PS2 if i % 2 == 0 else PS3
                    if i >= 2:
                        t.wait_ge(_vp, gt_vp0 + i - 1)
                    b.inc(t.transpose(ps[0:128, 0:32],
                                      gate_sm[:, i * 128:(i + 1) * 128],
                                      ident_sb[96:128, 96:128],
                                      tile_position=(96, 0)), 'tp', 1)
            blk.tensor(t_gt)

            def v_gt(v):
                for i in range(QT):
                    ps = PS2 if i % 2 == 0 else PS3
                    v.wait_ge(_tp, gt_tp0 + i + 1)
                    b.inc(v.tensor_copy(out=gate_t_sb[:, i * 32:(i + 1) * 32],
                                        in_=ps[0:128, 0:32]), 'vp', 1)
            blk.vector(v_gt)
            b.barrier()

            def kv_load(q, h, coloff, with_q=False):
                b.dma(q, A1[0:32, coloff:coloff + TOK], kvb[32 * h:32 * h + 32, :], wsn[h % 2])
                b.dma(q, A1[32:64, coloff:coloff + TOK],
                      kvb[1024 + 32 * h:1024 + 32 * h + 32, :], wsn[h % 2])
                b.dma(q, A1[64:128, coloff:coloff + TOK],
                      kvb[2048 + 64 * h:2048 + 64 * h + 64, :], wsn[h % 2])
                if with_q:
                    qoff = coloff + 2 * TOK
                    b.dma(q, A1[0:32, qoff:qoff + TOK], scr_q[32 * h:32 * h + 32, :], wsn[h % 2])
                    b.dma(q, A1[32:64, qoff:qoff + TOK],
                          scr_q[1024 + 32 * h:1024 + 32 * h + 32, :], wsn[h % 2])

            def emit_vt(t, coloff, vtoff, tpb, vpb):
                for i in range(QT):
                    ps = PS2 if i % 2 == 0 else PS3
                    if i >= 2:
                        t.wait_ge(_vp, vpb + i - 1)
                    b.inc(t.transpose(ps[0:128, 0:64],
                                      A1[64:128, coloff + i * 128:coloff + (i + 1) * 128],
                                      ident_sb[64:128, 64:128], tile_position=(64, 0)), 'tp', 1)

            def emit_vt_ev(v, vtoff, tpb):
                for i in range(QT):
                    ps = PS2 if i % 2 == 0 else PS3
                    v.wait_ge(_tp, tpb + i + 1)
                    b.inc(v.tensor_copy(out=A5[:, vtoff + 66 * i:vtoff + 66 * i + 64],
                                        in_=ps[0:128, 0:64]), 'vp', 1)

            def v_ones(v):
                v.memset(A5[:, 64:66 * QT:66], 1.0)
                b.inc(v.memset(A5[:, 2112 + 64:2112 + 66 * QT:66], 1.0), 'vp', 1)
            blk.vector(v_ones)
            b.barrier()

            p_tend = {}
            p_vend = {}
            p_send = {}
            for h in range(H2):
                coloff = (h % 2) * TOK
                vtoff = (h % 2) * 2112
                hb = 64 * (h % 2)
                hc = h // 2

                def q_h(q, h=h, coloff=coloff):
                    if h >= 2:
                        q.wait_ge(_tp, p_tend[h - 2])
                    kv_load(q, h, coloff)
                blk.sync(q_h)
                ld_mark = b.c[wsn[h % 2]]

                tpb = b.c['tp']
                vpb = b.c['vp']

                def t_p(t, coloff=coloff, vtoff=vtoff, ld_mark=ld_mark,
                        tpb=tpb, vpb=vpb, h=h):
                    t.wait_ge(WS[h % 2], ld_mark)
                    if h >= 1:
                        t.wait_ge(_vp, p_vend[h - 1])
                        t.wait_ge(_sp, p_send[h - 1])
                    emit_vt(t, coloff, vtoff, tpb, vpb)
                blk.tensor(t_p)

                def v_p(v, vtoff=vtoff, tpb=tpb):
                    emit_vt_ev(v, vtoff, tpb)
                blk.vector(v_p)
                vt_done = b.c['vp']

                def t_sc(t, h=h, coloff=coloff, vt_done=vt_done):
                    t.wait_ge(_vp, vt_done)
                    for i in range(QT):
                        b.inc(t.matmul(PS0[0:128, 4 * i:4 * i + 4],
                                       A1[0:64, coloff + i * 128:coloff + (i + 1) * 128],
                                       pqa_sb[0:64, 4 * h:4 * h + 4],
                                       start=True, stop=True), 'tp', 1)
                blk.tensor(t_sc)
                sc_done = b.c['tp']

                def s_e(s, sc_done=sc_done):
                    s.wait_ge(_tp, sc_done)
                    b.inc(s.activation(out=A3[:, 0:4 * QT], in_=PS0[0:128, 0:4 * QT],
                                       func=AF.Exp, scale=SC_B), 'sp', 1)
                blk.scalar(s_e)
                e_done = b.c['sp']

                def t_lm(t, vtoff=vtoff, e_done=e_done):
                    t.wait_ge(_sp, e_done)
                    for i in range(QT):
                        for bb_ in range(2):
                            psB = PS1 if bb_ == 0 else PS5
                            b.inc(t.matmul(psB[0:65, 4 * i:4 * i + 4],
                                           A5[64 * bb_:64 * bb_ + 64,
                                              vtoff + 66 * i:vtoff + 66 * i + 65],
                                           A3[64 * bb_:64 * bb_ + 64, 4 * i:4 * i + 4],
                                           start=True, stop=True,
                                           tile_position=(64 * bb_, 0)), 'tp', 1)
                blk.tensor(t_lm)
                lm_done = b.c['tp']

                def v_r(v, lm_done=lm_done):
                    v.wait_ge(_tp, lm_done)
                    m = b.inc(v.reciprocal(out=rden[:, 0:4 * QT], in_=PS1[64:65, 0:4 * QT]), 'vp', 1)
                    b.inc(v.reciprocal(out=rden[:, 256:256 + 4 * QT],
                                       in_=PS5[64:65, 0:4 * QT]), 'vp', 1)
                blk.vector(v_r)
                r_done = b.c['vp']

                def t_bc(t, r_done=r_done):
                    t.wait_ge(_vp, r_done)
                    t.matmul(PS4[0:64, 0:4 * QT], ones_r[64:65, 0:64],
                             rden[:, 0:4 * QT], start=True, stop=True,
                             tile_position=(64, 0))
                    b.inc(t.matmul(PS4[0:64, 256:256 + 4 * QT], ones_r[64:65, 0:64],
                                   rden[:, 256:256 + 4 * QT], start=True, stop=True,
                                   tile_position=(64, 0)), 'tp', 1)
                blk.tensor(t_bc)
                bc_done = b.c['tp']

                def s_c(s, bc_done=bc_done):
                    s.wait_ge(_tp, bc_done)
                    s.activation(out=A3[0:64, 256:256 + 4 * QT], in_=PS4[0:64, 0:4 * QT],
                                 func=AF.Copy)
                    b.inc(s.activation(out=A3[0:64, 384:384 + 4 * QT],
                                       in_=PS4[0:64, 256:256 + 4 * QT], func=AF.Copy), 'sp', 1)
                blk.scalar(s_c)
                c_done = b.c['sp']

                def v_out(v, hb=hb, hc=hc, c_done=c_done):
                    v.wait_ge(_sp, c_done)
                    lmf = A2[hb:hb + 64, hc * OWNLM:(hc + 1) * OWNLM]
                    lmf3 = lmf.rearrange("p (i m) -> p i m", m=8)
                    v.tensor_mul(out=lmf3[:, :, 0:4],
                                 in0=PS1[0:64, 0:4 * QT].rearrange("p (i m) -> p i m", m=4),
                                 in1=A3[0:64, 256:256 + 4 * QT].rearrange("p (i m) -> p i m", m=4))
                    b.inc(v.tensor_mul(out=lmf3[:, :, 4:8],
                                       in0=PS5[0:64, 0:4 * QT].rearrange("p (i m) -> p i m", m=4),
                                       in1=A3[0:64, 384:384 + 4 * QT].rearrange("p (i m) -> p i m", m=4)),
                          'vp', 1)
                blk.vector(v_out)
                p_tend[h] = b.c['tp']
                p_vend[h] = b.c['vp']
                p_send[h] = b.c['sp']
            b.barrier()


            if DEV_SUB == 31:
                def g_sub31(g):
                    b.dma(g, y[0:128, 0:2048], A2[:, 0:2048], 'gq')
                    g.wait_ge(_gq, b.c['gq'])
                blk.gpsimd(g_sub31)
                b.barrier()
                return

            def rhs_lmf0(dc):
                return A2[:, dc * OWNLM:(dc + 1) * OWNLM]

            def dst_lqkv(fc):
                return A5[:, fc * OWNLM:(fc + 1) * OWNLM]

            emit_small_proj(wlmqkv, blmqkv, 48, rhs_lmf0, dst_lqkv, mode='copy',
                            store=None, bias_n=48)
            st_sp = b.c['sp']

            def g_lkv(g):
                g.wait_ge(_sp, st_sp)
                for fc in range(16, 48):
                    b.dma(g, d3(ag3_in, 0)[:, fc - 16, :], dst_lqkv(fc), 'gq')
            blk.gpsimd(g_lkv)
            b.barrier()
            if PHASE2:
                emit_ag(ag3_in, ag3_out)

            def q_lkv(q):
                if PHASE2:
                    q.wait_ge(_cc, b.c['cc'])
                    a6v = A6[:, 0:32 * NL].rearrange("p (c l) -> p c l", l=NL)
                    b.dma(q, a6v[:, 0:16, 0:OWNLM], d3(ag3_out[0:2048, :], 16), 'sq')
                    b.dma(q, a6v[:, 0:16, OWNLM:NL], d3(ag3_out[4096:6144, :], 16), 'sq')
                    b.dma(q, a6v[:, 16:32, 0:OWNLM], d3(ag3_out[2048:4096, :], 16), 'sq')
                    b.dma(q, a6v[:, 16:32, OWNLM:NL], d3(ag3_out[6144:8192, :], 16), 'sq')
                else:
                    b.dma(q, sb3(A6[:, 0:32 * NL], 32, NL), d3(ag3_in, 32), 'sq')
            blk.sync(q_lkv)
            lkv_ld = b.c['sq']

            def v_ones2(v):
                b.inc(v.memset(A3[:, 576:577], 1.0), 'vp', 1)
            blk.vector(v_ones2)

            l_vend = {}
            l_send = {}
            for h in range(H2):
                hb = 64 * (h % 2)
                hc = h // 2
                tp_l0 = b.c['tp']
                vp_l0 = b.c['vp']
                sp_l0 = b.c['sp']

                def t_l(t, h=h, hb=hb, hc=hc, tp_l0=tp_l0, vp_l0=vp_l0):
                    t.wait_ge(_sq, lkv_ld)
                    if h >= 1:
                        t.wait_ge(_vp, l_vend[h - 1])
                        t.wait_ge(_sp, l_send[h - 1])
                    b.inc(t.matmul(PS0[0:128, 0:OWNLM],
                                   A6[hb:hb + 64, hc * NL:(hc + 1) * NL],
                                   A5[hb:hb + 64, (h // 2) * OWNLM + 0:(h // 2) * OWNLM + OWNLM]
                                   if False else
                                   A5[hb:hb + 64, hc * OWNLM:(hc + 1) * OWNLM],
                                   start=True, stop=True, tile_position=(hb, 0)), 'tp', 1)
                    b.inc(t.transpose(PS2[0:128, 0:64],
                                      A6[hb:hb + 64, (16 + hc) * NL:(16 + hc) * NL + 128],
                                      ident_sb[hb:hb + 64, hb:hb + 64],
                                      tile_position=(hb, 0)), 'tp', 1)
                    t.wait_ge(_vp, vp_l0 + 2)
                    b.inc(t.matmul(PS1[0:65, 0:OWNLM], A3[:, 512:577],
                                   A3[:, 0:OWNLM], start=True, stop=True), 'tp', 1)
                    t.wait_ge(_vp, vp_l0 + 4)
                    b.inc(t.matmul(PS4[0:64, 0:OWNLM], ones_r[64:65, 0:64],
                                   rden[:, 0:OWNLM], start=True, stop=True,
                                   tile_position=(64, 0)), 'tp', 1)
                blk.tensor(t_l)

                def s_l(s, tp_l0=tp_l0):
                    s.wait_ge(_tp, tp_l0 + 1)
                    b.inc(s.activation(out=A3[:, 0:OWNLM], in_=PS0[0:128, 0:OWNLM],
                                       func=AF.Exp, scale=SC_B), 'sp', 1)
                    s.wait_ge(_tp, tp_l0 + 4)
                    b.inc(s.activation(out=A3[0:64, 768:768 + OWNLM], in_=PS4[0:64, 0:OWNLM],
                                       func=AF.Copy), 'sp', 1)
                blk.scalar(s_l)

                def v_l(v, hb=hb, hc=hc, tp_l0=tp_l0, sp_l0=sp_l0):
                    v.wait_ge(_sp, sp_l0 + 1)
                    b.inc(v.tensor_mul(out=A3[:, 0:OWNLM], in0=A3[:, 0:OWNLM],
                                       in1=lmask_sb[:]), 'vp', 1)
                    v.wait_ge(_tp, tp_l0 + 2)
                    b.inc(v.tensor_copy(out=A3[:, 512:576], in_=PS2[0:128, 0:64]), 'vp', 1)
                    v.wait_ge(_tp, tp_l0 + 3)
                    m = b.inc(v.tensor_scalar_max(out=rden[:, 0:OWNLM],
                                                  in0=PS1[64:65, 0:OWNLM], scalar1=1e-9), 'vp', 1)
                    v.wait_ge(_vp, m)
                    b.inc(v.reciprocal(out=rden[:, 0:OWNLM], in_=rden[:, 0:OWNLM]), 'vp', 1)
                    v.wait_ge(_sp, sp_l0 + 2)
                    b.inc(v.tensor_mul(out=A2[hb:hb + 64, 2048 + hc * OWNLM:2048 + (hc + 1) * OWNLM],
                                       in0=PS1[0:64, 0:OWNLM],
                                       in1=A3[0:64, 768:768 + OWNLM]), 'vp', 1)
                blk.vector(v_l)
                l_vend[h] = b.c['vp']
                l_send[h] = b.c['sp']
            b.barrier()

            if DEV_SUB == 32:
                def g_sub32(g):
                    b.dma(g, y[0:128, 0:2048], A2[:, 2048:4096], 'gq')
                    g.wait_ge(_gq, b.c['gq'])
                blk.gpsimd(g_sub32)
                b.barrier()
                return

            def rhs_lo(dc):
                return A2[:, 2048 + dc * OWNLM:2048 + (dc + 1) * OWNLM]

            def dst_lmf1(fc):
                return A2[:, 4096 + fc * OWNLM:4096 + (fc + 1) * OWNLM]

            def base_lmf0(fc):
                return A2[:, fc * OWNLM:(fc + 1) * OWNLM]

            emit_small_proj(wlmout, blmout, 16, rhs_lo, dst_lmf1, mode='addres',
                            base=base_lmf0, bias_n=16)
            b.barrier()

            def rhs_lmf1(dc):
                return A2[:, 4096 + dc * OWNLM:4096 + (dc + 1) * OWNLM]

            def dst_l2q(fc):
                return A2[:, 6144 + fc * OWNLM:6144 + (fc + 1) * OWNLM]

            emit_small_proj(wl2tq, bl2tq, 16, rhs_lmf1, dst_l2q, mode='copy', bias_n=16)
            b.barrier()

            def v_ones_r(v):
                v.memset(A5[:, 64:66 * QT:66], 1.0)
                b.inc(v.memset(A5[:, 2112 + 64:2112 + 66 * QT:66], 1.0), 'vp', 1)
            blk.vector(v_ones_r)
            b.barrier()

            r_tend = {}
            r_vend = {}
            r_send = {}
            for h in range(H2):
                coloff = (h % 2) * TOK
                vtoff = (h % 2) * 2112
                hb = 64 * (h % 2)
                hc = h // 2
                xeb = (32 * h) % 128
                xec = (32 * h) // 128
                xob = (1024 + 32 * h) % 128
                xoc = (1024 + 32 * h) // 128

                def q_h2(q, h=h, coloff=coloff):
                    if h >= 2:
                        q.wait_ge(_tp, r_tend[h - 2])
                    kv_load(q, h, coloff)
                blk.sync(q_h2)
                ld_mark = b.c[wsn[h % 2]]

                tpb = b.c['tp']
                vpb = b.c['vp']

                def t_v2(t, coloff=coloff, vtoff=vtoff, ld_mark=ld_mark,
                         tpb=tpb, vpb=vpb, h=h):
                    t.wait_ge(WS[h % 2], ld_mark)
                    if h >= 1:
                        t.wait_ge(_vp, r_vend[h - 1])
                        t.wait_ge(_sp, r_send[h - 1])
                    emit_vt(t, coloff, vtoff, tpb, vpb)
                blk.tensor(t_v2)

                def v_v2(v, vtoff=vtoff, tpb=tpb):
                    emit_vt_ev(v, vtoff, tpb)
                blk.vector(v_v2)
                vt_done = b.c['vp']

                def v_cq(v, xeb=xeb, xec=xec, xob=xob, xoc=xoc, vt_done=vt_done):
                    v.wait_ge(_vp, vt_done)
                    v.tensor_copy(out=A3[0:32, 896:896 + OWNLM],
                                  in_=A2[xeb:xeb + 32, 6144 + xec * OWNLM:6144 + (xec + 1) * OWNLM])
                    b.inc(v.tensor_copy(out=A3[32:64, 896:896 + OWNLM],
                                        in_=A2[xob:xob + 32,
                                               6144 + xoc * OWNLM:6144 + (xoc + 1) * OWNLM]),
                          'vp', 1)
                blk.vector(v_cq)
                cq_done = b.c['vp']

                def t_r(t, coloff=coloff, cq_done=cq_done):
                    t.wait_ge(_vp, cq_done)
                    for i in range(QT):
                        b.inc(t.matmul(PS0[0:128, 8 * i:8 * i + 8],
                                       A1[0:64, coloff + i * 128:coloff + (i + 1) * 128],
                                       A3[0:64, 896 + 8 * i:896 + 8 * i + 8],
                                       start=True, stop=True), 'tp', 1)
                blk.tensor(t_r)
                rsc_done = b.c['tp']

                def s_r(s, rsc_done=rsc_done):
                    s.wait_ge(_tp, rsc_done)
                    b.inc(s.activation(out=A3[:, 0:8 * QT], in_=PS0[0:128, 0:8 * QT],
                                       func=AF.Exp, scale=SC_B), 'sp', 1)
                blk.scalar(s_r)
                er_done = b.c['sp']

                def v_m(v, er_done=er_done):
                    v.wait_ge(_sp, er_done)
                    b.inc(v.tensor_mul(out=A3[:, 0:8 * QT], in0=A3[:, 0:8 * QT],
                                       in1=l2tm_sb[:]), 'vp', 1)
                blk.vector(v_m)
                vm_done = b.c['vp']

                def t_ro(t, vtoff=vtoff, vm_done=vm_done):
                    t.wait_ge(_vp, vm_done)
                    for i in range(QT):
                        for bb_ in range(2):
                            psB = PS1 if bb_ == 0 else PS5
                            b.inc(t.matmul(psB[0:65, 4 * i:4 * i + 4],
                                           A5[64 * bb_:64 * bb_ + 64,
                                              vtoff + 66 * i:vtoff + 66 * i + 65],
                                           A3[64 * bb_:64 * bb_ + 64,
                                              8 * i + 4 * bb_:8 * i + 4 * bb_ + 4],
                                           start=True, stop=True,
                                           tile_position=(64 * bb_, 0)), 'tp', 1)
                blk.tensor(t_ro)
                ro_done = b.c['tp']

                def v_r2(v, ro_done=ro_done):
                    v.wait_ge(_tp, ro_done)
                    v.reciprocal(out=rden[:, 0:4 * QT], in_=PS1[64:65, 0:4 * QT])
                    b.inc(v.reciprocal(out=rden[:, 256:256 + 4 * QT],
                                       in_=PS5[64:65, 0:4 * QT]), 'vp', 1)
                blk.vector(v_r2)
                r2_done = b.c['vp']

                def t_bc2(t, r2_done=r2_done):
                    t.wait_ge(_vp, r2_done)
                    t.matmul(PS4[0:64, 0:4 * QT], ones_r[64:65, 0:64],
                             rden[:, 0:4 * QT], start=True, stop=True,
                             tile_position=(64, 0))
                    b.inc(t.matmul(PS4[0:64, 256:256 + 4 * QT], ones_r[64:65, 0:64],
                                   rden[:, 256:256 + 4 * QT], start=True, stop=True,
                                   tile_position=(64, 0)), 'tp', 1)
                blk.tensor(t_bc2)
                bc2_done = b.c['tp']

                def s_c2(s, bc2_done=bc2_done):
                    s.wait_ge(_tp, bc2_done)
                    s.activation(out=A3[0:64, 256:256 + 4 * QT], in_=PS4[0:64, 0:4 * QT],
                                 func=AF.Copy)
                    b.inc(s.activation(out=A3[0:64, 384:384 + 4 * QT],
                                       in_=PS4[0:64, 256:256 + 4 * QT], func=AF.Copy), 'sp', 1)
                blk.scalar(s_c2)
                c2_done = b.c['sp']

                def v_o2(v, hb=hb, hc=hc, c2_done=c2_done):
                    v.wait_ge(_sp, c2_done)
                    ro_ = A2[hb:hb + 64, 2048 + hc * OWNLM:2048 + (hc + 1) * OWNLM]
                    ro3 = ro_.rearrange("p (i m) -> p i m", m=8)
                    v.tensor_mul(out=ro3[:, :, 0:4],
                                 in0=PS1[0:64, 0:4 * QT].rearrange("p (i m) -> p i m", m=4),
                                 in1=A3[0:64, 256:256 + 4 * QT].rearrange("p (i m) -> p i m", m=4))
                    b.inc(v.tensor_mul(out=ro3[:, :, 4:8],
                                       in0=PS5[0:64, 0:4 * QT].rearrange("p (i m) -> p i m", m=4),
                                       in1=A3[0:64, 384:384 + 4 * QT].rearrange("p (i m) -> p i m", m=4)),
                          'vp', 1)
                blk.vector(v_o2)
                r_tend[h] = b.c['tp']
                r_vend[h] = b.c['vp']
                r_send[h] = b.c['sp']
            b.barrier()

            if DEV_SUB == 33:
                def g_sub33(g):
                    b.dma(g, y[0:128, 0:2048], A2[:, 2048:4096], 'gq')
                    g.wait_ge(_gq, b.c['gq'])
                blk.gpsimd(g_sub33)
                b.barrier()
                return

            def dst_lmf2(fc):
                return A2[:, fc * OWNLM:(fc + 1) * OWNLM]

            emit_small_proj(wl2tout, bl2tout, 16, rhs_lo, dst_lmf2, mode='addres',
                            base=rhs_lmf1, bias_n=16)
            b.barrier()

            def rhs_lmf2(dc):
                return A2[:, dc * OWNLM:(dc + 1) * OWNLM]

            def dst_gkv(fc):
                return A5[:, fc * OWNLM:(fc + 1) * OWNLM]

            emit_small_proj(wlmkv, blmkv, 32, rhs_lmf2, dst_gkv, mode='copy',
                            store=(ag4_in, lambda fc: fc * 128), bias_n=32)
            b.barrier()
            if PHASE2:
                emit_ag(ag4_in, ag4_out)

            if DEV_DBG:
                def g_dbg2(g):
                    b.dma(g, dbg[:, 0:2048], A2[:, 2048:4096], 'gq')
                    b.dma(g, dbg[:, 2048:4096], A2[:, 0:2048], 'gq')
                    g.wait_ge(_gq, b.c['gq'])
                blk.gpsimd(g_dbg2)
                b.barrier()

            if DEV_SUB == 34:
                def g_sub34(g):
                    b.dma(g, y[0:2048, 0:128], ag4_in[0:2048, :], 'gq')
                    g.wait_ge(_gq, b.c['gq'])
                blk.gpsimd(g_sub34)
                b.barrier()
                return

            def q_gm(q):
                b.dma(q, A6[:, 0:QT * 128], c_gmask, 'sq')
            blk.sync(q_gm)
            gmask_sb = A6
            b.barrier()
            def v_ones_r2(v):
                v.memset(A5[:, 64:66 * QT:66], 1.0)
                b.inc(v.memset(A5[:, 2112 + 64:2112 + 66 * QT:66], 1.0), 'vp', 1)
            blk.vector(v_ones_r2)
            b.barrier()

            f_tend = {}
            f_vend = {}
            f_send = {}
            f_gend = {}
            for h in range(H2):
                coloff = (h % 2) * TOK
                qoff = coloff + 2 * TOK
                vtoff = (h % 2) * 2112
                hb = 64 * (h % 2)

                def q_h3(q, h=h, coloff=coloff):
                    if PHASE2 and h == 0:
                        q.wait_ge(_cc, b.c['cc'])
                    if h >= 2:
                        q.wait_ge(_tp, f_tend[h - 2])
                    kv_load(q, h, coloff, with_q=True)
                    if h >= 1:
                        q.wait_ge(_tp, f_tend[h - 1])
                    if PHASE2:
                        b.dma(q, A3[0:32, 896:896 + OWNLM], ag4_out[32 * h:32 * h + 32, :], wsn[h % 2])
                        b.dma(q, A3[0:32, 896 + OWNLM:896 + NL],
                              ag4_out[4096 + 32 * h:4096 + 32 * h + 32, :], wsn[h % 2])
                        b.dma(q, A3[32:64, 896:896 + OWNLM],
                              ag4_out[1024 + 32 * h:1024 + 32 * h + 32, :], wsn[h % 2])
                        b.dma(q, A3[32:64, 896 + OWNLM:896 + NL],
                              ag4_out[4096 + 1024 + 32 * h:4096 + 1024 + 32 * h + 32, :], wsn[h % 2])
                        b.dma(q, A3[64:128, 1024:1024 + OWNLM],
                              ag4_out[2048 + 64 * h:2048 + 64 * h + 64, :], wsn[h % 2])
                        b.dma(q, A3[64:128, 1024 + OWNLM:1024 + NL],
                              ag4_out[4096 + 2048 + 64 * h:4096 + 2048 + 64 * h + 64, :], wsn[h % 2])
                    else:
                        b.dma(q, A3[0:32, 896:896 + NL], ag4_in[32 * h:32 * h + 32, :], wsn[h % 2])
                        b.dma(q, A3[32:64, 896:896 + NL],
                              ag4_in[1024 + 32 * h:1024 + 32 * h + 32, :], wsn[h % 2])
                        b.dma(q, A3[64:128, 1024:1024 + NL],
                              ag4_in[2048 + 64 * h:2048 + 64 * h + 64, :], wsn[h % 2])
                blk.sync(q_h3)
                ld_mark = b.c[wsn[h % 2]]

                tpb = b.c['tp']
                vpb = b.c['vp']

                def t_v3(t, coloff=coloff, vtoff=vtoff, ld_mark=ld_mark,
                         tpb=tpb, vpb=vpb, h=h):
                    t.wait_ge(WS[h % 2], ld_mark)
                    if h >= 1:
                        t.wait_ge(_vp, f_vend[h - 1])
                        t.wait_ge(_sp, f_send[h - 1])
                    emit_vt(t, coloff, vtoff, tpb, vpb)
                    b.inc(t.transpose(PS5[0:128, 0:64], A3[64:128, 1024:1024 + NL],
                                      ident_sb[64:128, 64:128], tile_position=(64, 0)), 'tp', 1)
                blk.tensor(t_v3)
                gvt_tp = b.c['tp']

                def v_v3(v, vtoff=vtoff, tpb=tpb, gvt_tp=gvt_tp, h=h):
                    if h >= 2:
                        v.wait_ge(_g4, f_gend[h - 2])
                    emit_vt_ev(v, vtoff, tpb)
                    v.wait_ge(_tp, gvt_tp)
                    b.inc(v.tensor_copy(out=A3[:, 512:576], in_=PS5[0:128, 0:64]), 'vp', 1)
                blk.vector(v_v3)
                vt_done = b.c['vp']

                for i in range(QT):
                    t0 = b.c['tp']
                    s0 = b.c['sp']
                    v0 = b.c['vp']

                    def t_a(t, h=h, i=i, coloff=coloff, qoff=qoff, vtoff=vtoff,
                            vt_done=vt_done, t0=t0, s0=s0, v0=v0):
                        t.wait_ge(_vp, vt_done)
                        if i >= 1:
                            t.wait_ge(_sp, s0)
                        b.inc(t.matmul(PS0[0:128, 0:128],
                                       A1[0:64, coloff + i * 128:coloff + (i + 1) * 128],
                                       A1[0:64, qoff + i * 128:qoff + (i + 1) * 128],
                                       start=True, stop=True), 'tp', 1)
                        b.inc(t.matmul(PS2[0:128, 0:128],
                                       A3[0:64, 896:896 + NL],
                                       A1[0:64, qoff + i * 128:qoff + (i + 1) * 128],
                                       start=True, stop=True), 'tp', 1)
                        t.wait_ge(_vp, v0 + 1)
                        b.inc(t.matmul(PS1[0:128, 0:65], A3[:, 0:128],
                                       A5[:, vtoff + 66 * i:vtoff + 66 * i + 65],
                                       start=True, stop=True), 'tp', 1)
                        t.wait_ge(_vp, v0 + 2)
                        b.inc(t.matmul(PS3[0:128, 0:65], A3[:, 128:256],
                                       A3[:, 512:577], start=True, stop=True), 'tp', 1)
                        t.wait_ge(_vp, v0 + 10)
                        b.inc(t.transpose(PS4[0:64, 0:128], A3[:, 640:704],
                                          ident_sb[:]), 'tp', 1)
                    blk.tensor(t_a)

                    def s_a(s, t0=t0):
                        s.wait_ge(_tp, t0 + 1)
                        b.inc(s.activation(out=A3[:, 0:128], in_=PS0[0:128, 0:128],
                                           func=AF.Exp, scale=SC_B), 'sp', 1)
                        s.wait_ge(_tp, t0 + 2)
                        b.inc(s.activation(out=A3[:, 128:256], in_=PS2[0:128, 0:128],
                                           func=AF.Exp, scale=SC_B), 'sp', 1)
                    blk.scalar(s_a)

                    def v_a(v, h=h, i=i, hb=hb, t0=t0, s0=s0, v0=v0):
                        v.wait_ge(_sp, s0 + 1)
                        b.inc(v.tensor_mul(out=A3[:, 0:128], in0=A3[:, 0:128],
                                           in1=blktri_sb[:]), 'vp', 1)
                        v.wait_ge(_sp, s0 + 2)
                        b.inc(v.tensor_mul(out=A3[:, 128:256], in0=A3[:, 128:256],
                                           in1=gmask_sb[:, i * 128:(i + 1) * 128]), 'vp', 1)
                        v.wait_ge(_tp, t0 + 3)
                        m = b.inc(v.tensor_scalar_max(out=A3[:, 704:705],
                                                      in0=PS1[0:128, 64:65], scalar1=1e-9), 'vp', 1)
                        v.wait_ge(_vp, m)
                        m = b.inc(v.reciprocal(out=A3[:, 705:706], in_=A3[:, 704:705]), 'vp', 1)
                        v.wait_ge(_tp, t0 + 4)
                        m2 = b.inc(v.tensor_scalar_max(out=A3[:, 706:707],
                                                       in0=PS3[0:128, 64:65], scalar1=1e-9), 'vp', 1)
                        v.wait_ge(_vp, m2)
                        mg = b.inc(v.reciprocal(out=A3[:, 707:708], in_=A3[:, 706:707]), 'vp', 1)
                        m3 = b.inc(v.tensor_scalar_mul(out=A3[:, 768:832], in0=PS1[0:128, 0:64],
                                                       scalar1=A3[:, 705:706]), 'vp', 1)
                        v.wait_ge(_vp, mg)
                        b.inc(v.tensor_scalar_mul(out=A3[:, 832:896], in0=PS3[0:128, 0:64],
                                                  scalar1=A3[:, 707:708]), 'vp', 1)
                        v.wait_ge(_vp, m3 + 1)
                        m4 = b.inc(v.tensor_sub(out=A3[:, 640:704], in0=A3[:, 832:896],
                                                in1=A3[:, 768:832]), 'vp', 1)
                        v.wait_ge(_vp, m4)
                        b.inc(v.scalar_tensor_tensor(out=A3[:, 640:704], in0=A3[:, 640:704],
                                                     scalar=gate_t_sb[:, 32 * i + h:32 * i + h + 1],
                                                     in1=A3[:, 768:832], op0=AT.mult, op1=AT.add),
                              'vp', 1)
                        v.wait_ge(_tp, t0 + 5)
                        b.inc(v.tensor_copy(out=A2[hb:hb + 64, i * 128:(i + 1) * 128],
                                            in_=PS4[0:64, 0:128]), 'vp', 1)
                    blk.vector(v_a)

                st_v = b.c['vp']

                def g_o(g, h=h, hb=hb, st_v=st_v):
                    g.wait_ge(_vp, st_v)
                    if h >= 1:
                        g.wait_ge(_g4, f_gend[h - 1])
                    b.dma(g, scr_ao[64 * h:64 * h + 64, :], A2[hb:hb + 64, 0:TOK], 'g4')
                blk.gpsimd(g_o)
                f_tend[h] = b.c['tp']
                f_vend[h] = b.c['vp']
                f_send[h] = b.c['sp']
                f_gend[h] = b.c['g4']
            b.barrier()

        def dst_qkv_a(fc):
            if fc < 16:
                return (scr_q, fc * 128)
            if fc < 32:
                return (ag1_in, (fc - 16) * 128)
            return (ag1_in, D + (fc - 32) * 128)

        def emit_ag(in_ap, out_ap):
            def g_ag(g):
                b.inc(g.collective_compute(
                    "AllGather", AT.bypass,
                    replica_groups=[[0, 1], [2, 3], [4, 5], [6, 7]],
                    ins=[in_ap.opt()], outs=[out_ap.opt()]), 'cc', 1)
            blk.gpsimd(g_ag)

        emit_qkv_stage(wqkv_a, bqkv_a, 48, dst_qkv_a, x0, ln_here=True)
        if PHASE2:
            emit_ag(ag1_in, ag1_out)
        emit_attn_a()
        dst1 = y if DEV_STOP == 1 else xres
        emit_proj_res(scr_ao, wo_a, bo_a, x0, dst1, want_stats=(DEV_STOP > 1))

        if DEV_STOP >= 2:
            dst2 = y if DEV_STOP == 2 else xres
            emit_ffn(xres, dst2, want_stats=(DEV_STOP > 2))

        if DEV_STOP >= 3:
            emit_attn_b()
            if DEV_SUB == 0:
                dst3 = y if DEV_STOP == 3 else xres
                emit_proj_res(scr_ao, wout, bout, xres, dst3, want_stats=(DEV_STOP > 3))

        if DEV_STOP >= 4 and DEV_SUB == 0:
            emit_ffn(xres, y, want_stats=False)

        def g_fin(g):
            g.wait_ge(_gq, b.c['gq'])
        blk.gpsimd(g_fin)

    return nc


_NC_CACHE = {}


def kernel(**inputs):
    in_maps = stage_inputs(inputs)
    key = (PHASE2, DEV_STOP, DEV_SUB)
    if key not in _NC_CACHE:
        _NC_CACHE[key] = build_program()
    nc = _NC_CACHE[key]
    res = run_bass_kernel_spmd(nc, in_maps, core_ids=list(range(8)))
    out = np.empty((Bsz, L, D), dtype=np.float32)
    for core in range(8):
        if not PHASE2 and core % 2 == 1:
            continue
        s = core // 2
        own_tiles = list(range(core % 2, 16, 2)) if PHASE2 else list(range(16))
        yc = res.results[core]['y']
        for ii, tt in enumerate(own_tiles):
            out[s, 128 * tt:128 * tt + 128, :] = yc[:, 128 * ii:128 * (ii + 1)].T
    return out


if __name__ == "__main__":
    pass


# version 05 (final)
# speedup vs baseline: 2.0873x; 1.0722x over previous
import numpy as np
import concourse.bass as bass
import concourse.mybir as mybir
from concourse.bass_utils import run_bass_kernel_spmd

F32 = mybir.dt.float32
F32R = mybir.dt.float32r
AT = mybir.AluOpType
AF = mybir.ActivationFunctionType

D = 2048
L = 2048
Bsz = 4
NH = 16
H2 = 32
M4 = 4
NL = 128
I5 = 5632
DC = 16
VELT = 4097.0
EPS_LN = 1e-5
SC_A = float(1.0 / np.sqrt(128.0))
SC_B = float(1.0 / np.sqrt(64.0))

PHASE2 = True
TOK = L // 2 if PHASE2 else L
QT = TOK // 128
OWNLM = NL // 2 if PHASE2 else NL
TC = 512
NTC = TOK // TC
ICH = I5 // 128
DEV_STOP = 4
DEV_DBG = False
DEV_SUB = 0


class Bld:
    def __init__(self, nc, blk, sems):
        self.nc, self.blk, self.sem = nc, blk, sems
        self.c = {k: 0 for k in sems}

    def inc(self, instr, s, n=1):
        instr.then_inc(self.sem[s], n)
        self.c[s] += n
        return self.c[s]

    def dma(self, eng, out, in_, q):
        return self.inc(eng.dma_start(out=out, in_=in_), q, 16)

    def chk(self, s, expected):
        assert self.c[s] == expected, f"sem {s}: emitted {self.c[s]} != planned {expected}"

    def barrier(self):
        tgt = self.c['bar'] + 5
        syncq = [(q, self.c[q]) for q in ('sq', 'w0', 'w1', 'w2', 'w3')]
        gpq = [(q, self.c[q]) for q in ('gq', 'g0', 'g1', 'g2', 'g3', 'g4', 'f0', 'f1', 'f2', 'f3')]
        bar = self.sem['bar']

        def t_f(t):
            t.wait_ge(self.sem['sq'], self.prolog_sq)
            self.inc(t.matmul(self.scratch_ps[96:97, 508:509],
                              self.ident[96:97, 96:97], self.ident[96:97, 96:97],
                              start=True, stop=True, tile_position=(96, 96)), 'bar', 1)
            t.wait_ge(bar, tgt)

        def v_f(v):
            self.inc(v.memset(self.scratch_sm[64:65, 3071:3072], 0.0), 'bar', 1)
            v.wait_ge(bar, tgt)

        def s_f(s):
            self.inc(s.memzero(self.scratch_sm[64:65, 3070:3071]), 'bar', 1)
            s.wait_ge(bar, tgt)

        def q_f(eng):
            for q, v in syncq:
                if v:
                    eng.wait_ge(self.sem[q], v)
            eng.sem_inc(bar, 1)
            self.c['bar'] += 1
            eng.wait_ge(bar, tgt)

        def g_f(eng):
            for q, v in gpq:
                if v:
                    eng.wait_ge(self.sem[q], v)
            eng.sem_inc(bar, 1)
            self.c['bar'] += 1
            eng.wait_ge(bar, tgt)
        self.blk.tensor(t_f)
        self.blk.vector(v_f)
        self.blk.scalar(s_f)
        self.blk.sync(q_f)
        self.blk.gpsimd(g_f)
        assert self.c['bar'] == tgt


def _rope_perm():
    perm = np.zeros(D, dtype=np.int64)
    for h in range(H2):
        for j in range(32):
            perm[32 * h + j] = 64 * h + 2 * j
            perm[1024 + 32 * h + j] = 64 * h + 2 * j + 1
    return perm


def stage_inputs(inputs):
    f32 = np.float32
    perm = _rope_perm()

    wqkv_a = np.ascontiguousarray(inputs['L0_attn_a_Wqkv'], dtype=f32)
    bqkv_a = np.ascontiguousarray(np.asarray(inputs['L0_attn_a_bqkv'], dtype=f32).reshape(48, 128).T)
    wo_a = np.ascontiguousarray(inputs['L0_attn_a_Wo'], dtype=f32)
    bo_a = np.ascontiguousarray(np.asarray(inputs['L0_attn_a_bo'], dtype=f32).reshape(16, 128).T)

    wqkv_b = np.array(inputs['L0_attn_b_w_qkv'], dtype=f32)
    bqkv_b = np.array(inputs['L0_attn_b_b_qkv'], dtype=f32)
    wqkv_b[:, 0:D] = wqkv_b[:, 0:D][:, perm]
    wqkv_b[:, D:2 * D] = wqkv_b[:, D:2 * D][:, perm]
    bqkv_b[0:D] = bqkv_b[0:D][perm]
    bqkv_b[D:2 * D] = bqkv_b[D:2 * D][perm]
    wqkv_b = np.ascontiguousarray(wqkv_b)
    bqkv_b = np.ascontiguousarray(bqkv_b.reshape(48, 128).T)

    pqT = np.ascontiguousarray(np.asarray(inputs['L0_attn_b_pool_q'], dtype=f32).T[perm, :])

    wlmqkv = np.ascontiguousarray(inputs['L0_attn_b_w_lm_qkv'], dtype=f32)
    blmqkv = np.ascontiguousarray(np.asarray(inputs['L0_attn_b_b_lm_qkv'], dtype=f32).reshape(48, 128).T)
    wlmout = np.ascontiguousarray(inputs['L0_attn_b_w_lm_out'], dtype=f32)
    blmout = np.ascontiguousarray(np.asarray(inputs['L0_attn_b_b_lm_out'], dtype=f32).reshape(16, 128).T)
    wl2tq = np.ascontiguousarray(np.asarray(inputs['L0_attn_b_w_l2t_q'], dtype=f32)[:, perm])
    bl2tq = np.ascontiguousarray(np.asarray(inputs['L0_attn_b_b_l2t_q'], dtype=f32)[perm].reshape(16, 128).T)
    wl2tout = np.ascontiguousarray(inputs['L0_attn_b_w_l2t_out'], dtype=f32)
    bl2tout = np.ascontiguousarray(np.asarray(inputs['L0_attn_b_b_l2t_out'], dtype=f32).reshape(16, 128).T)
    wlmkv = np.array(inputs['L0_attn_b_w_lm_kv'], dtype=f32)
    blmkv = np.array(inputs['L0_attn_b_b_lm_kv'], dtype=f32)
    wlmkv[:, 0:D] = wlmkv[:, 0:D][:, perm]
    blmkv[0:D] = blmkv[0:D][perm]
    wlmkv = np.ascontiguousarray(wlmkv)
    blmkv = np.ascontiguousarray(blmkv.reshape(32, 128).T)
    wout = np.ascontiguousarray(inputs['L0_attn_b_w_out'], dtype=f32)
    bout = np.ascontiguousarray(np.asarray(inputs['L0_attn_b_b_out'], dtype=f32).reshape(16, 128).T)
    wgate = np.ascontiguousarray(inputs['L0_attn_b_w_gate'], dtype=f32)
    bgate_v = np.asarray(inputs['L0_attn_b_b_gate'], dtype=f32)
    bgate = np.zeros((128, 1), dtype=f32)
    bgate[96:128, 0] = bgate_v

    gwT = np.ascontiguousarray(np.asarray(inputs['L0_ffn_gate_w'], dtype=f32).T)
    uwT = np.ascontiguousarray(np.asarray(inputs['L0_ffn_up_w'], dtype=f32).T)
    dwT = np.ascontiguousarray(np.asarray(inputs['L0_ffn_down_w'], dtype=f32).T)
    gsv = np.ascontiguousarray(np.asarray(inputs['L0_ffn_gate_s'], dtype=f32).reshape(44, 128).T)
    usv = np.ascontiguousarray(np.asarray(inputs['L0_ffn_up_s'], dtype=f32).reshape(44, 128).T)
    dsv = np.ascontiguousarray(np.asarray(inputs['L0_ffn_down_s'], dtype=f32).reshape(16, 128).T)

    ident = np.eye(128, dtype=f32)
    tri = (np.arange(128)[:, None] <= np.arange(128)[None, :]).astype(f32)
    si = np.arange(128)
    blktri = (((si[:, None] // 64) == (si[None, :] // 64)) &
              ((si[:, None] % 64) <= (si[None, :] % 64))).astype(f32)
    l2tm = np.ascontiguousarray(
        np.tile(((si[:, None] // 64) == (np.arange(8)[None, :] // 4)).astype(f32), (1, QT)))
    inv = 1.0 / 10000.0 ** (np.arange(0, 64, 2) / 64.0)

    x_full = np.asarray(inputs['L0_attn_a_x'], dtype=f32)

    in_maps = []
    for core in range(8):
        s = core // 2
        par = core % 2
        own_tiles = list(range(par, 16, 2)) if PHASE2 else list(range(16))
        own_tok = np.concatenate([np.arange(128 * t, 128 * t + 128) for t in own_tiles])
        own_blocks = [t for tt in own_tiles for t in (2 * tt, 2 * tt + 1)]

        xT = np.ascontiguousarray(x_full[s][own_tok].T)
        ang = own_tok[None, :].astype(np.float64) * inv[:, None]
        cos128 = np.ascontiguousarray(np.tile(np.cos(ang), (4, 1)).astype(f32))
        sin128 = np.ascontiguousarray(np.tile(np.sin(ang), (4, 1)).astype(f32))

        if PHASE2:
            ev = [t for tt in range(0, 16, 2) for t in (2 * tt, 2 * tt + 1)]
            od = [t for tt in range(1, 16, 2) for t in (2 * tt, 2 * tt + 1)]
            lm_blk = np.repeat(np.array(ev + od), M4)
        else:
            lm_blk = np.repeat(np.arange(32), M4)
        own_lm_blk = np.repeat(np.array(own_blocks), M4)
        lmask = (lm_blk[:, None] <= own_lm_blk[None, :]).astype(f32)
        gm = np.zeros((128, QT * 128), dtype=f32)
        for ii, tt in enumerate(own_tiles):
            q_blk = np.repeat(np.array([2 * tt, 2 * tt + 1]), 64)
            gm[:, ii * 128:(ii + 1) * 128] = (lm_blk[:, None] < q_blk[None, :]).astype(f32)

        if PHASE2:
            am = np.zeros((128, QT * 2 * 128), dtype=f32)
            for ii, tt in enumerate(own_tiles):
                for j, St in enumerate((2 * ii, 2 * ii + 1)):
                    if St < tt:
                        mblk = np.ones((128, 128), dtype=f32)
                    elif St == tt:
                        mblk = tri
                    else:
                        mblk = np.zeros((128, 128), dtype=f32)
                    am[:, (2 * ii + j) * 128:(2 * ii + j + 1) * 128] = mblk
            m_amask = am

        pqa = np.zeros((64, 128), dtype=f32)
        for h in range(H2):
            pqa[0:32, 4 * h:4 * h + 4] = pqT[32 * h:32 * h + 32, :]
            pqa[32:64, 4 * h:4 * h + 4] = pqT[1024 + 32 * h:1024 + 32 * h + 32, :]

        m = {
            'x0': xT,
            'wqkv_a': wqkv_a, 'bqkv_a': bqkv_a, 'wo_a': wo_a, 'bo_a': bo_a,
            'wqkv_b': wqkv_b, 'bqkv_b': bqkv_b, 'pqT': pqT,
            'wlmqkv': wlmqkv, 'blmqkv': blmqkv, 'wlmout': wlmout, 'blmout': blmout,
            'wl2tq': wl2tq, 'bl2tq': bl2tq, 'wl2tout': wl2tout, 'bl2tout': bl2tout,
            'wlmkv': wlmkv, 'blmkv': blmkv, 'wout': wout, 'bout': bout,
            'wgate': wgate, 'bgate': bgate,
            'gwT': gwT, 'uwT': uwT, 'dwT': dwT, 'gs': gsv, 'us': usv, 'ds': dsv,
            'c_ident': ident, 'c_tri': tri, 'c_blktri': blktri, 'c_l2tm': l2tm,
            'c_lmask': np.ascontiguousarray(lmask), 'c_gmask': np.ascontiguousarray(gm),
            'c_cos': cos128, 'c_sin': sin128, 'c_pqa': np.ascontiguousarray(pqa),
        }
        if PHASE2:
            m['c_amask'] = np.ascontiguousarray(m_amask)
        in_maps.append(m)
    return in_maps


def build_program():
    nc = bass.Bass("TRN2", target_bir_lowering=False)

    def din(name, shape, dt=F32):
        return nc.dram_tensor(name, shape, dt, kind="ExternalInput").ap()

    x0 = din('x0', [D, TOK])
    wqkv_a = din('wqkv_a', [D, 3 * D]); bqkv_a = din('bqkv_a', [128, 48])
    wo_a = din('wo_a', [D, D]); bo_a = din('bo_a', [128, 16])
    wqkv_b = din('wqkv_b', [D, 3 * D]); bqkv_b = din('bqkv_b', [128, 48])
    pqT = din('pqT', [D, M4])
    wlmqkv = din('wlmqkv', [D, 3 * D]); blmqkv = din('blmqkv', [128, 48])
    wlmout = din('wlmout', [D, D]); blmout = din('blmout', [128, 16])
    wl2tq = din('wl2tq', [D, D]); bl2tq = din('bl2tq', [128, 16])
    wl2tout = din('wl2tout', [D, D]); bl2tout = din('bl2tout', [128, 16])
    wlmkv = din('wlmkv', [D, 2 * D]); blmkv = din('blmkv', [128, 32])
    wout = din('wout', [D, D]); bout = din('bout', [128, 16])
    wgate = din('wgate', [D, H2]); bgate = din('bgate', [128, 1])
    gwT = din('gwT', [D, I5])
    uwT = din('uwT', [D, I5])
    dwT = din('dwT', [I5, D])
    gs = din('gs', [128, 44]); us = din('us', [128, 44]); ds = din('ds', [128, 16])
    c_ident = din('c_ident', [128, 128]); c_tri = din('c_tri', [128, 128])
    c_blktri = din('c_blktri', [128, 128]); c_l2tm = din('c_l2tm', [128, 8 * QT])
    c_lmask = din('c_lmask', [128, OWNLM]); c_gmask = din('c_gmask', [128, QT * 128])
    c_cos = din('c_cos', [128, TOK]); c_sin = din('c_sin', [128, TOK])
    c_pqa = din('c_pqa', [64, 128])
    c_amask = din('c_amask', [128, QT * 2 * 128]) if PHASE2 else None
    dbg = (nc.dram_tensor('dbg', [128, 4096], F32, kind="ExternalOutput").ap()
           if DEV_DBG else None)

    y = nc.dram_tensor('y', [D, TOK], F32, kind="ExternalOutput").ap()

    xres = nc.dram_tensor('xres', [D, TOK], F32).ap()
    scr_ao = nc.dram_tensor('scr_ao', [D, TOK], F32).ap()
    scr_q = nc.dram_tensor('scr_q', [D, TOK], F32).ap()
    ag1_in = nc.dram_tensor('ag1_in', [2 * D, TOK], F32).ap()
    ag1_out = nc.dram_tensor('ag1_out', [4 * D, TOK], F32).ap() if PHASE2 else None
    ag3_out = nc.dram_tensor('ag3_out', [4 * D, OWNLM], F32).ap() if PHASE2 else None
    ag4_out = nc.dram_tensor('ag4_out', [4 * D, OWNLM], F32).ap() if PHASE2 else None
    kvb = nc.dram_tensor('kvb', [2 * D, TOK], F32).ap()
    ag3_in = nc.dram_tensor('ag3_in', [2 * D, OWNLM], F32).ap()
    ag4_in = nc.dram_tensor('ag4_in', [2 * D, OWNLM], F32).ap()

    def d3(ap_, C):
        return ap_.rearrange("(c p) t -> p c t", p=128)

    def sb3(t_, C, w):
        return t_.rearrange("p (c t) -> p c t", c=C)

    import contextlib
    with contextlib.ExitStack() as _st:
        A1 = _st.enter_context(nc.sbuf_tensor('A1', [128, 8192], F32))
        A2 = _st.enter_context(nc.sbuf_tensor('A2', [128, 8192], F32))
        A3 = _st.enter_context(nc.sbuf_tensor('A3', [128, 3072], F32))
        A4 = _st.enter_context(nc.sbuf_tensor('A4', [128, 4096], F32))
        A5 = _st.enter_context(nc.sbuf_tensor('A5', [128, 8192], F32))
        A6 = _st.enter_context(nc.sbuf_tensor('A6', [128, 8192], F32))
        RW = _st.enter_context(nc.sbuf_tensor('RW', [128, 8192], mybir.dt.bfloat16))
        SM = _st.enter_context(nc.sbuf_tensor('SM', [128, 3072], F32))
        pqa_sb = _st.enter_context(nc.sbuf_tensor('pqa_sb', [64, 128], F32))
        gate_t_sb = _st.enter_context(nc.sbuf_tensor('gate_t_sb', [128, 32 * QT], F32))
        ident_sb = _st.enter_context(nc.sbuf_tensor('ident_sb', [128, 128], F32))
        tri_sb = _st.enter_context(nc.sbuf_tensor('tri_sb', [128, 128], F32))
        blktri_sb = _st.enter_context(nc.sbuf_tensor('blktri_sb', [128, 128], F32))
        l2tm_sb = _st.enter_context(nc.sbuf_tensor('l2tm_sb', [128, 8 * QT], F32))
        lmask_sb = _st.enter_context(nc.sbuf_tensor('lmask_sb', [128, OWNLM], F32))
        ones_c = _st.enter_context(nc.sbuf_tensor('ones_c', [128, 1], F32))
        ones_r = _st.enter_context(nc.sbuf_tensor('ones_r', [128, 128], F32))
        bias_sb = _st.enter_context(nc.sbuf_tensor('bias_sb', [128, 48], F32))
        bias2_sb = _st.enter_context(nc.sbuf_tensor('bias2_sb', [128, 1], F32))
        gsc_sb = _st.enter_context(nc.sbuf_tensor('gsc_sb', [128, 44], F32))
        usc_sb = _st.enter_context(nc.sbuf_tensor('usc_sb', [128, 44], F32))
        dsc_sb = _st.enter_context(nc.sbuf_tensor('dsc_sb', [128, 16], F32))
        PS0 = _st.enter_context(nc.psum_tensor('PS0', [128, 512], F32))
        PS1 = _st.enter_context(nc.psum_tensor('PS1', [128, 512], F32))
        PS2 = _st.enter_context(nc.psum_tensor('PS2', [128, 512], F32))
        PS3 = _st.enter_context(nc.psum_tensor('PS3', [128, 512], F32))
        PS4 = _st.enter_context(nc.psum_tensor('PS4', [128, 512], F32))
        PS5 = _st.enter_context(nc.psum_tensor('PS5', [128, 512], F32))
        PS6 = _st.enter_context(nc.psum_tensor('PS6', [128, 512], F32))
        PS7 = _st.enter_context(nc.psum_tensor('PS7', [128, 512], F32))
        _sq = _st.enter_context(nc.semaphore('sq'))
        _gq = _st.enter_context(nc.semaphore('gq'))
        _w0 = _st.enter_context(nc.semaphore('w0'))
        _w1 = _st.enter_context(nc.semaphore('w1'))
        _g0 = _st.enter_context(nc.semaphore('g0'))
        _g1 = _st.enter_context(nc.semaphore('g1'))
        _g2 = _st.enter_context(nc.semaphore('g2'))
        _g3 = _st.enter_context(nc.semaphore('g3'))
        _g4 = _st.enter_context(nc.semaphore('g4'))
        _w2 = _st.enter_context(nc.semaphore('w2'))
        _w3 = _st.enter_context(nc.semaphore('w3'))
        _f0 = _st.enter_context(nc.semaphore('f0'))
        _f1 = _st.enter_context(nc.semaphore('f1'))
        _f2 = _st.enter_context(nc.semaphore('f2'))
        _f3 = _st.enter_context(nc.semaphore('f3'))
        _tp = _st.enter_context(nc.semaphore('tp'))
        _vp = _st.enter_context(nc.semaphore('vp'))
        _sp = _st.enter_context(nc.semaphore('sp'))
        _cc = _st.enter_context(nc.semaphore('cc'))
        _bar = _st.enter_context(nc.semaphore('bar'))
        blk = _st.enter_context(nc.Block())
        b = Bld(nc, blk, {'sq': _sq, 'gq': _gq, 'tp': _tp, 'vp': _vp,
                          'sp': _sp, 'cc': _cc, 'bar': _bar,
                          'w0': _w0, 'w1': _w1, 'w2': _w2, 'w3': _w3,
                          'f0': _f0, 'f1': _f1, 'f2': _f2, 'f3': _f3,
                          'g0': _g0, 'g1': _g1, 'g2': _g2, 'g3': _g3, 'g4': _g4})
        WS2 = [_w2, _w3]
        wsn2 = ['w2', 'w3']
        FS = [_f0, _f1]
        fsn = ['f0', 'f1']
        FS2 = [_f2, _f3]
        fsn2 = ['f2', 'f3']
        b.scratch_ps = PS7
        b.scratch_sm = SM
        b.ones_c = ones_c
        b.ident = ident_sb
        WS = [_w0, _w1]
        GS = [_g0, _g1, _g2, _g3]
        wsn = ['w0', 'w1']
        gsn = ['g0', 'g1', 'g2', 'g3']
        st_m = SM[0:1, 0:TOK]
        tmp_a = SM[0:1, 2048:2048 + TC]
        tmp_b = SM[0:1, 2560:2560 + TC]
        st_r = SM[32:33, 0:TOK]
        cvec = SM[32:33, 2048:3072]
        rden = SM[64:65, 0:TOK]
        gate_sm = SM[96:128, 0:TOK]

        def prolog_q(q):
            b.dma(q, ident_sb[:], c_ident, 'sq')
            b.dma(q, tri_sb[:], c_tri, 'sq')
            b.dma(q, blktri_sb[:], c_blktri, 'sq')
            b.dma(q, l2tm_sb[:], c_l2tm, 'sq')
            b.dma(q, lmask_sb[:], c_lmask, 'sq')
        blk.sync(prolog_q)
        b.prolog_sq = b.c['sq']

        def prolog_v(v):
            v.memset(ones_c[:], 1.0)
            v.memset(ones_r[:], 1.0)
        blk.vector(prolog_v)
        b.barrier()

        def emit_ln_stats(xbuf, tc0):
            sp0, tp0, vp0 = b.c['sp'], b.c['tp'], b.c['vp']

            def s_body(s):
                for dc in range(DC):
                    if dc >= 2:
                        s.wait_ge(_tp, tp0 + dc - 1)
                    b.inc(s.activation(out=A3[:, (dc % 2) * TC:(dc % 2 + 1) * TC],
                                       in_=xbuf[:, dc * TC:(dc + 1) * TC],
                                       func=AF.Square), 'sp', 1)
            blk.scalar(s_body)

            def t_body(t):
                for dc in range(DC):
                    t.matmul(PS6[0:1, 0:TC], ones_c[:], xbuf[:, dc * TC:(dc + 1) * TC],
                             start=(dc == 0), stop=(dc == DC - 1))
                    t.wait_ge(_sp, sp0 + dc + 1)
                    b.inc(t.matmul(PS7[0:1, 0:TC], ones_c[:],
                                   A3[:, (dc % 2) * TC:(dc % 2 + 1) * TC],
                                   start=(dc == 0), stop=(dc == DC - 1)), 'tp', 1)
            blk.tensor(t_body)
            b.chk('tp', tp0 + DC)
            b.chk('sp', sp0 + DC)
            emit_stats_finish(tc0)

        def emit_stats_finish(tc0):
            tdone = b.c['tp']

            def v1(v):
                v.wait_ge(_tp, tdone)
                v.tensor_scalar_mul(out=st_m[:, tc0:tc0 + TC], in0=PS6[0:1, 0:TC], scalar1=1.0 / D)
                m1 = b.inc(v.tensor_scalar(out=tmp_a, in0=PS7[0:1, 0:TC], scalar1=1.0 / D,
                                           scalar2=EPS_LN, op0=AT.mult, op1=AT.add), 'vp', 1)
                v.wait_ge(_vp, m1)
                m2 = b.inc(v.tensor_mul(out=tmp_b, in0=st_m[:, tc0:tc0 + TC],
                                        in1=st_m[:, tc0:tc0 + TC]), 'vp', 1)
                v.wait_ge(_vp, m2)
                b.inc(v.tensor_sub(out=tmp_a, in0=tmp_a, in1=tmp_b), 'vp', 1)
            blk.vector(v1)
            vdone = b.c['vp']

            def s1(s):
                s.wait_ge(_vp, vdone)
                b.inc(s.activation(out=tmp_b, in_=tmp_a, func=AF.Sqrt), 'sp', 1)
            blk.scalar(s1)
            sdone = b.c['sp']

            def v2(v):
                v.wait_ge(_sp, sdone)
                b.inc(v.reciprocal(out=st_r[:, tc0:tc0 + TC], in_=tmp_b), 'vp', 1)
            blk.vector(v2)

        def emit_ln_apply(xbuf, xnbuf, tc0):
            vp0 = b.c['vp']

            def t1(t):
                t.wait_ge(_vp, vp0)
                t.matmul(PS4[:, 0:TC], ones_r[0:1, :], st_m[:, tc0:tc0 + TC], start=True, stop=True)
                b.inc(t.matmul(PS5[:, 0:TC], ones_r[32:33, :], st_r[:, tc0:tc0 + TC],
                               start=True, stop=True), 'tp', 1)
            blk.tensor(t1)
            tdone = b.c['tp']

            def v1(v):
                v.wait_ge(_tp, tdone)
                for dc in range(DC):
                    i = v.tensor_sub(out=xnbuf[:, dc * TC:(dc + 1) * TC],
                                     in0=xbuf[:, dc * TC:(dc + 1) * TC], in1=PS4[:, 0:TC])
                    if dc == DC - 1:
                        m1 = b.inc(i, 'vp', 1)
                        v.wait_ge(_vp, m1)
                for dc in range(DC):
                    i = v.tensor_mul(out=xnbuf[:, dc * TC:(dc + 1) * TC],
                                     in0=xnbuf[:, dc * TC:(dc + 1) * TC], in1=PS5[:, 0:TC])
                b.inc(i, 'vp', 1)
            blk.vector(v1)

        def emit_qkv_stage(W_ap, Bias_ap, nfc, dst, xsrc, ln_here, rope=False,
                           gate=False):
            def q_bias(q):
                b.dma(q, bias_sb[:, 0:nfc], Bias_ap, 'sq')
                if gate:
                    b.dma(q, bias2_sb[:, 0:1], bgate, 'sq')
            blk.sync(q_bias)

            rope_pairs = []
            rope_mem = {}
            if rope:
                for c in range(8):
                    rope_pairs.append((c, c + 8))
                for c in range(8):
                    rope_pairs.append((16 + c, 24 + c))
                for (fe, fo) in rope_pairs:
                    rope_mem[fe] = (fe, fo)
                    rope_mem[fo] = (fe, fo)
                order = [f for pr in rope_pairs for f in pr] + list(range(32, nfc))
            else:
                order = list(range(nfc))
            if gate:
                order = order + [nfc]

            total = nfc + (1 if gate else 0)
            for tc in range(NTC):
                tc0 = tc * TC

                def q_x(q):
                    b.dma(q, sb3(A1[:], DC, TC)[:, :, 0:TC], d3(xsrc, DC)[:, :, tc0:tc0 + TC], 'sq')
                blk.sync(q_x)
                xld = b.c['sq']

                def tw(t):
                    t.wait_ge(_sq, xld)
                blk.tensor(tw)

                def sw(s):
                    s.wait_ge(_sq, xld)
                blk.scalar(sw)
                if ln_here:
                    emit_ln_stats(A1, tc0)
                emit_ln_apply(A1, A2, tc0)
                xn_ready = b.c['vp']

                tp0 = b.c['tp']
                sp0 = b.c['sp']
                vp0 = b.c['vp']
                wb0 = [b.c['w0'], b.c['w1']]
                gb0 = [b.c['g0'], b.c['g1'], b.c['g2'], b.c['g3']]
                g40 = b.c['g4']

                def pair_idx(fc):
                    pr = rope_mem[fc]
                    return rope_pairs.index(pr)

                def w_mark(idx):
                    return wb0[idx % 2] + 16 * (idx // 2 + 1)

                pos = {fc: idx for idx, fc in enumerate(order)}

                slot_use = {0: [], 1: [], 2: [], 3: []}
                for idx, fc in enumerate(order):
                    if gate and fc == nfc:
                        continue
                    if rope and fc in rope_mem:
                        continue
                    slot_use[idx % 4].append(idx)

                def g_mark(idx):
                    sl = idx % 4
                    k = slot_use[sl].index(idx) + 1
                    return gb0[sl] + 16 * k

                def g4_mark(pi):
                    return g40 + 32 * (pi + 1)

                def q_w(q):
                    for idx, fc in enumerate(order):
                        bb = idx % 2
                        if idx >= 2:
                            q.wait_ge(_tp, tp0 + idx - 1)
                        if gate and fc == nfc:
                            b.dma(q, sb3(A4[:, bb * 2048:bb * 2048 + DC * 32], DC, 32),
                                  d3(wgate, DC), wsn[bb])
                        else:
                            b.dma(q, sb3(A4[:, bb * 2048:(bb + 1) * 2048], DC, 128),
                                  d3(W_ap, DC)[:, :, fc * 128:(fc + 1) * 128], wsn[bb])
                blk.sync(q_w)

                def t_mm(t):
                    t.wait_ge(_vp, xn_ready)
                    for idx, fc in enumerate(order):
                        bb = idx % 2
                        is_gate = gate and fc == nfc
                        wid = 32 if is_gate else 128
                        ps = PS0 if bb == 0 else PS1
                        t.wait_ge(WS[bb], w_mark(idx))
                        if idx >= 2:
                            t.wait_ge(_sp, sp0 + idx - 1)
                        for dc in range(DC):
                            i = t.matmul(ps[0:wid, 0:TC],
                                         A4[:, bb * 2048 + dc * wid:bb * 2048 + (dc + 1) * wid],
                                         A2[:, dc * TC:(dc + 1) * TC],
                                         start=(dc == 0), stop=(dc == DC - 1))
                        b.inc(i, 'tp', 1)
                blk.tensor(t_mm)
                b.chk('tp', tp0 + total)

                def s_evac(s):
                    for idx, fc in enumerate(order):
                        ps = PS0 if idx % 2 == 0 else PS1
                        slot = idx % 4
                        s.wait_ge(_tp, tp0 + idx + 1)
                        if idx >= 4:
                            pidx = idx - 4
                            pfc = order[pidx]
                            if rope and pfc in rope_mem:
                                s.wait_ge(_vp, vp0 + 2 * (pair_idx(pfc) + 1))
                            elif not (gate and pfc == nfc):
                                s.wait_ge(GS[pidx % 4], g_mark(pidx))
                        if gate and fc == nfc:
                            i = s.activation(out=gate_sm[:, tc0:tc0 + TC], in_=ps[0:32, 0:TC],
                                             func=AF.Sigmoid, bias=bias2_sb[96:128, 0:1])
                        else:
                            i = s.activation(out=A3[:, slot * TC:(slot + 1) * TC],
                                             in_=ps[0:128, 0:TC],
                                             func=AF.Identity, bias=bias_sb[:, fc:fc + 1])
                        b.inc(i, 'sp', 1)
                blk.scalar(s_evac)
                b.chk('sp', sp0 + total)

                if rope:
                    def v_rope(v):
                        for pi, (fe, fo) in enumerate(rope_pairs):
                            v.wait_ge(_sp, sp0 + max(pos[fe], pos[fo]) + 1)
                            if pi >= 1:
                                v.wait_ge(_g4, g4_mark(pi - 1))
                            xe = A3[:, (pos[fe] % 4) * TC:(pos[fe] % 4 + 1) * TC]
                            xo = A3[:, (pos[fo] % 4) * TC:(pos[fo] % 4 + 1) * TC]
                            oe = A5[:, 0:TC]
                            oo = A5[:, TC:2 * TC]
                            t1_ = A5[:, 2 * TC:3 * TC]
                            t2_ = A5[:, 3 * TC:4 * TC]
                            cosT = A6[:, tc0:tc0 + TC]
                            sinT = A6[:, 4096 + tc0:4096 + tc0 + TC]
                            v.tensor_mul(out=oe, in0=xe, in1=cosT)
                            v.tensor_mul(out=t1_, in0=xo, in1=sinT)
                            v.tensor_mul(out=oo, in0=xe, in1=sinT)
                            m = b.inc(v.tensor_mul(out=t2_, in0=xo, in1=cosT), 'vp', 1)
                            v.wait_ge(_vp, m)
                            v.tensor_sub(out=oe, in0=oe, in1=t1_)
                            b.inc(v.tensor_add(out=oo, in0=oo, in1=t2_), 'vp', 1)
                    blk.vector(v_rope)
                    b.chk('vp', vp0 + 2 * len(rope_pairs))

                def g_st(g):
                    for idx, fc in enumerate(order):
                        if gate and fc == nfc:
                            continue
                        if rope and fc in rope_mem:
                            pr = rope_mem[fc]
                            if fc != pr[1]:
                                continue
                            g.wait_ge(_vp, vp0 + 2 * (pair_idx(fc) + 1))
                            da_e, r0_e = dst(pr[0])
                            da_o, r0_o = dst(pr[1])
                            b.dma(g, d3(da_e, 0)[:, r0_e // 128, tc0:tc0 + TC], A5[:, 0:TC], 'g4')
                            b.dma(g, d3(da_o, 0)[:, r0_o // 128, tc0:tc0 + TC], A5[:, TC:2 * TC], 'g4')
                        else:
                            slot = idx % 4
                            g.wait_ge(_sp, sp0 + idx + 1)
                            da, r0 = dst(fc)
                            b.dma(g, d3(da, 0)[:, r0 // 128, tc0:tc0 + TC],
                                  A3[:, slot * TC:(slot + 1) * TC], gsn[slot])
                blk.gpsimd(g_st)
                b.barrier()

        def emit_attn_a():
            SEQ = L
            NS_T = SEQ // 128
            koff, voff, qoff2, vtoff2 = 0, SEQ, 2 * SEQ, 2 * SEQ + TOK

            def nS(i):
                return 2 * i + 2 if PHASE2 else i + 1

            def stcol(St):
                if PHASE2:
                    return (St // 2) * 128 + (St % 2) * 1024
                return St * 128

            if PHASE2:
                def q_am(q):
                    b.dma(q, A6[:, 4096:4096 + QT * 256], c_amask, 'sq')
                blk.sync(q_am)
                b.barrier()

            h_tend = {}
            h_vend = {}
            h_send = {}
            h_gend = {}
            for h in range(NH):
                slot = A1 if h % 2 == 0 else A5

                def q_head(q, h=h, slot=slot):
                    if PHASE2:
                        q.wait_ge(_cc, 1)
                    if h >= 2:
                        q.wait_ge(_tp, h_tend[h - 2])
                    sl = wsn[h % 2]
                    if PHASE2:
                        b.dma(q, slot[:, koff:koff + TOK], ag1_out[h * 128:(h + 1) * 128, :], sl)
                        b.dma(q, slot[:, koff + TOK:koff + 2 * TOK],
                              ag1_out[2 * D + h * 128:2 * D + (h + 1) * 128, :], sl)
                        b.dma(q, slot[:, voff:voff + TOK],
                              ag1_out[D + h * 128:D + (h + 1) * 128, :], sl)
                        b.dma(q, slot[:, voff + TOK:voff + 2 * TOK],
                              ag1_out[3 * D + h * 128:3 * D + (h + 1) * 128, :], sl)
                    else:
                        b.dma(q, slot[:, koff:koff + SEQ], ag1_in[h * 128:(h + 1) * 128, :], sl)
                        b.dma(q, slot[:, voff:voff + SEQ], ag1_in[D + h * 128:D + (h + 1) * 128, :], sl)
                    b.dma(q, slot[:, qoff2:qoff2 + TOK], scr_q[h * 128:(h + 1) * 128, :], sl)
                blk.sync(q_head)
                ld_mark = b.c[wsn[h % 2]]

                tp0 = b.c['tp']
                vp0 = b.c['vp']

                def t_vt(t, slot=slot, ld_mark=ld_mark, tp0=tp0, vp0=vp0, h=h):
                    t.wait_ge(WS[h % 2], ld_mark)
                    if h >= 1:
                        t.wait_ge(_vp, h_vend[h - 1])
                        t.wait_ge(_sp, h_send[h - 1])
                    for St in range(NS_T):
                        ps = PS2 if St % 2 == 0 else PS3
                        if St >= 2:
                            t.wait_ge(_vp, vp0 + St - 1)
                        b.inc(t.transpose(ps[0:128, 0:128],
                                          slot[:, voff + stcol(St):voff + stcol(St) + 128],
                                          ident_sb[:]), 'tp', 1)
                blk.tensor(t_vt)

                def v_vt(v, slot=slot, tp0=tp0):
                    for St in range(NS_T):
                        ps = PS2 if St % 2 == 0 else PS3
                        v.wait_ge(_tp, tp0 + St + 1)
                        b.inc(v.tensor_copy(out=slot[:, vtoff2 + St * 128:vtoff2 + (St + 1) * 128],
                                            in_=ps[0:128, 0:128]), 'vp', 1)
                blk.vector(v_vt)
                b.chk('tp', tp0 + NS_T)
                b.chk('vp', vp0 + NS_T)
                vt_done = b.c['vp']

                T0 = b.c['tp']
                S0 = b.c['sp']
                V0 = b.c['vp']

                def t_off(i):
                    return T0 + sum(2 * nS(j) + 1 for j in range(i))

                def sc_mark(i, St):
                    return t_off(i) + (1 if St == 0 else 2 * St)

                def od_mark(i, St):
                    n = nS(i)
                    return t_off(i) + (2 * St + 3 if St <= n - 2 else 2 * n)

                def bc_mark(i):
                    return t_off(i) + 2 * nS(i) + 1

                def s_off(i):
                    return S0 + sum(nS(j) + 1 for j in range(i))

                def exp_mark(i, St):
                    return s_off(i) + St + 1

                def cpy_mark(i):
                    return s_off(i) + nS(i) + 1

                MP = 2 if PHASE2 else 1

                def mask_mark(i, j=0):
                    return V0 + (MP + 2) * i + j + 1

                def rden_mark(i):
                    return V0 + (MP + 2) * i + MP + 1

                def outev_mark(i):
                    return V0 + (MP + 2) * i + MP + 2

                def masked_tiles(i):
                    if PHASE2:
                        return (nS(i) - 2, nS(i) - 1)
                    return (nS(i) - 1,)

                def t_all(t, h=h, slot=slot, vt_done=vt_done):
                    t.wait_ge(_vp, vt_done)
                    for i in range(QT):
                        n = nS(i)
                        out_ps = PS2 if i % 2 == 0 else PS3
                        den_ps = PS6 if i % 2 == 0 else PS7
                        if i >= 2:
                            t.wait_ge(_vp, outev_mark(i - 2))

                        def emit_sc(St, i=i):
                            ps = PS0 if St % 2 == 0 else PS1
                            if St >= 2:
                                t.wait_ge(_sp, exp_mark(i, St - 2))
                            b.inc(t.matmul(ps[0:128, 0:128],
                                           slot[:, koff + stcol(St):koff + stcol(St) + 128],
                                           slot[:, qoff2 + i * 128:qoff2 + (i + 1) * 128],
                                           start=True, stop=True), 'tp', 1)

                        def emit_out(St, i=i, n=n, out_ps=out_ps, den_ps=den_ps):
                            if St in masked_tiles(i):
                                t.wait_ge(_vp, mask_mark(i, St - masked_tiles(i)[0]))
                            else:
                                t.wait_ge(_sp, exp_mark(i, St))
                            e_ap = A2[:, (St % 4) * 128:(St % 4 + 1) * 128]
                            t.matmul(out_ps[0:128, 0:128],
                                     slot[:, vtoff2 + St * 128:vtoff2 + (St + 1) * 128],
                                     e_ap, start=(St == 0), stop=(St == n - 1))
                            b.inc(t.matmul(den_ps[0:1, 0:128], ones_c[:], e_ap,
                                           start=(St == 0), stop=(St == n - 1)), 'tp', 1)

                        emit_sc(0)
                        for St in range(1, n):
                            emit_sc(St)
                            emit_out(St - 1)
                        emit_out(n - 1)
                        t.wait_ge(_vp, rden_mark(i))
                        if i >= 1:
                            t.wait_ge(_sp, cpy_mark(i - 1))
                        b.inc(t.matmul(PS4[:, 0:128], ones_r[64:65, :],
                                       rden[:, i * 128:(i + 1) * 128], start=True, stop=True),
                              'tp', 1)
                blk.tensor(t_all)

                def s_all(s, h=h):
                    if h >= 1:
                        s.wait_ge(_tp, h_tend[h - 1])
                    for i in range(QT):
                        n = nS(i)
                        for St in range(n):
                            ps = PS0 if St % 2 == 0 else PS1
                            s.wait_ge(_tp, sc_mark(i, St))
                            if St >= 4:
                                s.wait_ge(_tp, od_mark(i, St - 4))
                            elif i >= 1:
                                s.wait_ge(_tp, od_mark(i - 1, nS(i - 1) - 1))
                            b.inc(s.activation(out=A2[:, (St % 4) * 128:(St % 4 + 1) * 128],
                                               in_=ps[0:128, 0:128], func=AF.Exp, scale=SC_A),
                                  'sp', 1)
                        s.wait_ge(_tp, bc_mark(i))
                        if i >= 2:
                            s.wait_ge(_vp, outev_mark(i - 2))
                        b.inc(s.activation(out=A3[:, (i % 2) * 128:(i % 2 + 1) * 128],
                                           in_=PS4[:, 0:128], func=AF.Copy), 'sp', 1)
                blk.scalar(s_all)

                def v_all(v, h=h):
                    if h >= 1:
                        v.wait_ge(_g4, h_gend[h - 1])
                    for i in range(QT):
                        n = nS(i)
                        mts = masked_tiles(i)
                        for St in mts:
                            v.wait_ge(_sp, exp_mark(i, St))
                            if PHASE2:
                                mask_ap = A6[:, 4096 + (2 * i + (St - mts[0])) * 128:
                                             4096 + (2 * i + (St - mts[0]) + 1) * 128]
                            else:
                                mask_ap = tri_sb[:]
                            b.inc(v.tensor_mul(out=A2[:, (St % 4) * 128:(St % 4 + 1) * 128],
                                               in0=A2[:, (St % 4) * 128:(St % 4 + 1) * 128],
                                               in1=mask_ap), 'vp', 1)
                        v.wait_ge(_tp, od_mark(i, n - 1))
                        den_ps = PS6 if i % 2 == 0 else PS7
                        b.inc(v.reciprocal(out=rden[:, i * 128:(i + 1) * 128],
                                           in_=den_ps[0:1, 0:128]), 'vp', 1)
                        v.wait_ge(_sp, cpy_mark(i))
                        out_ps = PS2 if i % 2 == 0 else PS3
                        b.inc(v.tensor_mul(out=A6[:, i * 128:(i + 1) * 128],
                                           in0=out_ps[0:128, 0:128],
                                           in1=A3[:, (i % 2) * 128:(i % 2 + 1) * 128]), 'vp', 1)
                blk.vector(v_all)

                nmm = sum(2 * nS(j) + 1 for j in range(QT))
                b.chk('tp', T0 + nmm)
                b.chk('sp', S0 + sum(nS(j) + 1 for j in range(QT)))
                b.chk('vp', V0 + (MP + 2) * QT)

                h_tend[h] = b.c['tp']
                h_vend[h] = b.c['vp']
                h_send[h] = b.c['sp']
                st_mark = b.c['vp']

                def g_out(g, h=h, st_mark=st_mark):
                    g.wait_ge(_vp, st_mark)
                    if h >= 1:
                        g.wait_ge(_g4, h_gend[h - 1])
                    b.dma(g, scr_ao[h * 128:(h + 1) * 128, :], A6[:, 0:TOK], 'g4')
                blk.gpsimd(g_out)
                h_gend[h] = b.c['g4']
            b.barrier()

        def emit_proj_res(src_ap, W_ap, Bias_ap, xold_ap, dst_ap, want_stats):
            def q_bias(q):
                b.dma(q, bias_sb[:, 0:16], Bias_ap, 'sq')
            blk.sync(q_bias)

            for tc in range(NTC):
                tc0 = tc * TC

                def q_in(q):
                    b.dma(q, sb3(A1[:], DC, TC)[:, :, 0:TC], d3(src_ap, DC)[:, :, tc0:tc0 + TC], 'sq')
                    b.dma(q, sb3(A5[:], DC, TC)[:, :, 0:TC], d3(xold_ap, DC)[:, :, tc0:tc0 + TC], 'sq')
                blk.sync(q_in)
                xld = b.c['sq']

                K_ = 2 if want_stats else 1
                tp0 = b.c['tp']
                vp0 = b.c['vp']
                sp0 = b.c['sp']
                wb0 = [b.c['w0'], b.c['w1']]
                gb0 = [b.c['g0'], b.c['g1'], b.c['g2'], b.c['g3']]

                def q_w(q):
                    for fc in range(DC):
                        bb = fc % 2
                        if fc >= 2:
                            q.wait_ge(_tp, tp0 + K_ * (fc - 1))
                        b.dma(q, sb3(A4[:, bb * 2048:(bb + 1) * 2048], DC, 128),
                              d3(W_ap, DC)[:, :, fc * 128:(fc + 1) * 128], wsn[bb])
                blk.sync(q_w)

                def t_mm(t):
                    t.wait_ge(_sq, xld)
                    for fc in range(DC):
                        bb = fc % 2
                        ps = PS0 if bb == 0 else PS1
                        t.wait_ge(WS[bb], wb0[bb] + 16 * (fc // 2 + 1))
                        if fc >= 2:
                            t.wait_ge(_vp, vp0 + fc - 1)
                        for dc in range(DC):
                            i = t.matmul(ps[0:128, 0:TC],
                                         A4[:, bb * 2048 + dc * 128:bb * 2048 + (dc + 1) * 128],
                                         A1[:, dc * TC:(dc + 1) * TC],
                                         start=(dc == 0), stop=(dc == DC - 1))
                        b.inc(i, 'tp', 1)
                        if want_stats:
                            t.wait_ge(_vp, vp0 + fc + 1)
                            t.matmul(PS6[0:1, 0:TC], ones_c[:],
                                     A2[:, (fc % 4) * TC:(fc % 4 + 1) * TC],
                                     start=(fc == 0), stop=(fc == DC - 1))
                            t.wait_ge(_sp, sp0 + fc + 1)
                            b.inc(t.matmul(PS7[0:1, 0:TC], ones_c[:],
                                           A3[:, (fc % 2) * TC:(fc % 2 + 1) * TC],
                                           start=(fc == 0), stop=(fc == DC - 1)), 'tp', 1)
                blk.tensor(t_mm)

                def v_evac(v):
                    for fc in range(DC):
                        ps = PS0 if fc % 2 == 0 else PS1
                        v.wait_ge(_tp, tp0 + K_ * fc + 1)
                        if fc >= 4:
                            v.wait_ge(GS[fc % 4], gb0[fc % 4] + 16 * (fc // 4))
                            if want_stats:
                                v.wait_ge(_sp, sp0 + fc - 3)
                        b.inc(v.scalar_tensor_tensor(
                            out=A2[:, (fc % 4) * TC:(fc % 4 + 1) * TC],
                            in0=ps[0:128, 0:TC], scalar=bias_sb[:, fc:fc + 1],
                            in1=A5[:, fc * TC:(fc + 1) * TC], op0=AT.add, op1=AT.add), 'vp', 1)
                blk.vector(v_evac)

                if want_stats:
                    def s_sq(s):
                        for fc in range(DC):
                            s.wait_ge(_vp, vp0 + fc + 1)
                            if fc >= 2:
                                s.wait_ge(_tp, tp0 + 2 * fc)
                            b.inc(s.activation(out=A3[:, (fc % 2) * TC:(fc % 2 + 1) * TC],
                                               in_=A2[:, (fc % 4) * TC:(fc % 4 + 1) * TC],
                                               func=AF.Square), 'sp', 1)
                    blk.scalar(s_sq)

                def g_st(g):
                    for fc in range(DC):
                        g.wait_ge(_vp, vp0 + fc + 1)
                        b.dma(g, d3(dst_ap, DC)[:, fc, tc0:tc0 + TC],
                              A2[:, (fc % 4) * TC:(fc % 4 + 1) * TC], gsn[fc % 4])
                blk.gpsimd(g_st)

                b.chk('tp', tp0 + K_ * DC)
                b.chk('vp', vp0 + DC)
                if want_stats:
                    b.chk('sp', sp0 + DC)
                    emit_stats_finish(tc0)
                b.barrier()

        def emit_ffn(xold_ap, dst_ap, want_stats):
            GRP = 4
            NG = ICH // GRP
            BF16 = mybir.dt.bfloat16
            xh_bf = A1[:, 0:4096].bitcast(BF16)
            xl_bf = A1[:, 4096:8192].bitcast(BF16)
            hh_bf = A5[:, 0:1024].bitcast(BF16)
            hl_bf = A5[:, 1024:2048].bitcast(BF16)
            dw_bf = A4[:, 0:512].bitcast(BF16)

            def q_sc(q):
                b.dma(q, gsc_sb[:, 0:44], gs, 'sq')
                b.dma(q, usc_sb[:, 0:44], us, 'sq')
                b.dma(q, dsc_sb[:, 0:16], ds, 'sq')
            blk.sync(q_sc)

            for tc in range(NTC):
                tc0 = tc * TC

                def q_x(q):
                    b.dma(q, sb3(A6[:], DC, TC)[:, :, 0:TC], d3(xold_ap, DC)[:, :, tc0:tc0 + TC], 'sq')
                blk.sync(q_x)
                xld = b.c['sq']

                def tw(t):
                    t.wait_ge(_sq, xld)
                blk.tensor(tw)
                emit_ln_apply(A6, A2, tc0)
                xn_ready = b.c['vp']

                def v_split(v):
                    v.wait_ge(_vp, xn_ready)
                    for dc in range(DC):
                        m = b.inc(v.tensor_copy(out=xh_bf[:, dc * TC:(dc + 1) * TC],
                                                in_=A2[:, dc * TC:(dc + 1) * TC]), 'vp', 1)
                        v.wait_ge(_vp, m)
                        b.inc(v.tensor_sub(out=xl_bf[:, dc * TC:(dc + 1) * TC],
                                           in0=A2[:, dc * TC:(dc + 1) * TC],
                                           in1=xh_bf[:, dc * TC:(dc + 1) * TC]), 'vp', 1)
                blk.vector(v_split)
                split_done = b.c['vp']

                def xh_r(dc):
                    return xh_bf[:, dc * TC:(dc + 1) * TC]

                def xl_r(dc):
                    return xl_bf[:, dc * TC:(dc + 1) * TC]

                tp0 = b.c['tp']
                sp0 = b.c['sp']
                vp0 = b.c['vp']
                wb0 = [b.c['f0'], b.c['f1']]
                wb20 = [b.c['f2'], b.c['f3']]

                gu_mark = {}
                dn_tp_mark = {}
                tpc = tp0
                for ic in range(ICH):
                    tpc += 1; gu_mark[(ic, 0)] = tpc
                    tpc += 1; gu_mark[(ic, 1)] = tpc
                    if ic % GRP == GRP - 1:
                        grp = ic // GRP
                        for fcp in range(DC):
                            tpc += 1; dn_tp_mark[(grp, fcp)] = tpc
                tp_end = tpc

                hl_mark = {}
                dn_vp_mark = {}
                vpc = vp0
                for ic in range(ICH):
                    vpc += 5; hl_mark[ic] = vpc
                    if ic % GRP == GRP - 1:
                        grp = ic // GRP
                        for fcp in range(DC):
                            vpc += 1; dn_vp_mark[(grp, fcp)] = vpc
                vp_end = vpc

                def silu_mark(ic, which):
                    return sp0 + 2 * ic + which + 1

                def w_mark_gu(ic):
                    return wb0[ic % 2] + 32 * (ic // 2 + 1)

                def w_mark_dn(grp, fcp):
                    k = grp * DC + fcp
                    return wb20[k % 2] + 16 * (k // 2 + 1)

                def q_w(g):
                    for ic in range(ICH):
                        bb = ic % 2
                        if ic >= 2:
                            g.wait_ge(_tp, gu_mark[(ic - 2, 1)])
                        b.dma(g, sb3(RW[:, bb * 2048:(bb + 1) * 2048], DC, 128),
                              d3(gwT, DC)[:, :, ic * 128:(ic + 1) * 128], fsn[bb])
                        b.dma(g, sb3(RW[:, 4096 + bb * 2048:4096 + (bb + 1) * 2048], DC, 128),
                              d3(uwT, DC)[:, :, ic * 128:(ic + 1) * 128], fsn[bb])
                        if ic % GRP == GRP - 1:
                            grp = ic // GRP
                            for fcp in range(DC):
                                k = grp * DC + fcp
                                bb2 = k % 2
                                if k >= 2:
                                    kk = k - 2
                                    g.wait_ge(_tp, dn_tp_mark[(kk // DC, kk % DC)])
                                b.dma(g, sb3(dw_bf[:, bb2 * 512:(bb2 + 1) * 512], GRP, 128),
                                      dwT.rearrange("(g p) t -> p g t", p=128)[
                                          :, 4 * grp:4 * grp + GRP, fcp * 128:(fcp + 1) * 128],
                                      fsn2[bb2])
                blk.gpsimd(q_w)

                def t_mm(t):
                    t.wait_ge(_vp, split_done)
                    for ic in range(ICH):
                        bb = ic % 2
                        t.wait_ge(FS[bb], w_mark_gu(ic))
                        if ic >= 1:
                            t.wait_ge(_sp, silu_mark(ic - 1, 0))
                        for p in range(2):
                            for dc in range(DC):
                                i = t.matmul(PS0[0:128, 0:TC],
                                             RW[:, bb * 2048 + dc * 128:bb * 2048 + (dc + 1) * 128],
                                             xh_r(dc) if p == 0 else xl_r(dc),
                                             start=(p == 0 and dc == 0),
                                             stop=(p == 1 and dc == DC - 1))
                        b.inc(i, 'tp', 1)
                        if ic >= 1:
                            t.wait_ge(_sp, silu_mark(ic - 1, 1))
                        for p in range(2):
                            for dc in range(DC):
                                i = t.matmul(PS1[0:128, 0:TC],
                                             RW[:, 4096 + bb * 2048 + dc * 128:4096 + bb * 2048 + (dc + 1) * 128],
                                             xh_r(dc) if p == 0 else xl_r(dc),
                                             start=(p == 0 and dc == 0),
                                             stop=(p == 1 and dc == DC - 1))
                        b.inc(i, 'tp', 1)
                        if ic % GRP == GRP - 1:
                            grp = ic // GRP
                            t.wait_ge(_vp, hl_mark[ic])
                            for fcp in range(DC):
                                k = grp * DC + fcp
                                bb2 = k % 2
                                ps = PS2 if fcp % 2 == 0 else PS3
                                t.wait_ge(FS2[bb2], w_mark_dn(grp, fcp))
                                if k >= 2:
                                    kk = k - 2
                                    t.wait_ge(_vp, dn_vp_mark[(kk // DC, kk % DC)])
                                for p in range(2):
                                    for icg in range(GRP):
                                        hsrc = (hh_bf[:, icg * TC:(icg + 1) * TC] if p == 0
                                                else hl_bf[:, icg * TC:(icg + 1) * TC])
                                        i = t.matmul(ps[0:128, 0:TC],
                                                     dw_bf[:, bb2 * 512 + icg * 128:bb2 * 512 + (icg + 1) * 128],
                                                     hsrc,
                                                     start=(p == 0 and icg == 0),
                                                     stop=(p == 1 and icg == GRP - 1))
                                b.inc(i, 'tp', 1)
                blk.tensor(t_mm)
                b.chk('tp', tp_end)

                def s_ev(s):
                    for ic in range(ICH):
                        s.wait_ge(_tp, gu_mark[(ic, 0)])
                        if ic >= 1:
                            s.wait_ge(_vp, vp0 + 5 * (ic - 1) + 1 + 16 * ((ic - 1) // GRP))
                        b.inc(s.activation(out=A3[:, 4 * TC:5 * TC], in_=PS0[0:128, 0:TC],
                                           func=AF.Silu, scale=gsc_sb[:, ic:ic + 1]), 'sp', 1)
                        s.wait_ge(_tp, gu_mark[(ic, 1)])
                        b.inc(s.activation(out=A3[:, 5 * TC:6 * TC], in_=PS1[0:128, 0:TC],
                                           func=AF.Copy, scale=usc_sb[:, ic:ic + 1]), 'sp', 1)
                blk.scalar(s_ev)
                b.chk('sp', sp0 + 2 * ICH)

                def v_h(v):
                    for ic in range(ICH):
                        icg = ic % GRP
                        v.wait_ge(_sp, silu_mark(ic, 1))
                        if ic >= 1:
                            v.wait_ge(_vp, hl_mark[ic - 1])
                        if ic >= GRP:
                            v.wait_ge(_tp, dn_tp_mark[(ic // GRP - 1, DC - 1)])
                        hbuf = A3[:, 0:TC]
                        m = b.inc(v.tensor_mul(out=hbuf, in0=A3[:, 4 * TC:5 * TC],
                                               in1=A3[:, 5 * TC:6 * TC]), 'vp', 1)
                        v.wait_ge(_vp, m)
                        m = b.inc(v.tensor_copy(out=hh_bf[:, icg * TC:(icg + 1) * TC],
                                                in_=hbuf), 'vp', 1)
                        v.wait_ge(_vp, m)
                        m = b.inc(v.tensor_sub(out=hl_bf[:, icg * TC:(icg + 1) * TC],
                                               in0=hbuf,
                                               in1=hh_bf[:, icg * TC:(icg + 1) * TC]), 'vp', 1)
                        v.wait_ge(_vp, m)
                        b.inc(v.memset(SM[64:65, 3068:3069], 0.0), 'vp', 1)
                        b.inc(v.memset(SM[64:65, 3067:3068], 0.0), 'vp', 1)
                        if ic % GRP == GRP - 1:
                            grp = ic // GRP
                            for fcp in range(DC):
                                ps = PS2 if fcp % 2 == 0 else PS3
                                v.wait_ge(_tp, dn_tp_mark[(grp, fcp)])
                                if grp >= 1:
                                    v.wait_ge(_vp, dn_vp_mark[(grp - 1, fcp)])
                                if grp == 0:
                                    i = v.tensor_copy(out=A6[:, fcp * TC:(fcp + 1) * TC],
                                                      in_=ps[0:128, 0:TC])
                                else:
                                    i = v.tensor_add(out=A6[:, fcp * TC:(fcp + 1) * TC],
                                                     in0=A6[:, fcp * TC:(fcp + 1) * TC],
                                                     in1=ps[0:128, 0:TC])
                                b.inc(i, 'vp', 1)
                blk.vector(v_h)
                b.chk('vp', vp_end)
                acc_done = b.c['vp']

                def q_xold(q):
                    q.wait_ge(_vp, acc_done)
                    b.dma(q, sb3(A5[:], DC, TC)[:, :, 0:TC], d3(xold_ap, DC)[:, :, tc0:tc0 + TC], 'sq')
                blk.sync(q_xold)
                xold_ld = b.c['sq']

                fin_vp0 = b.c['vp']
                fin_sp0 = b.c['sp']
                fin_tp0 = b.c['tp']
                gb0 = [b.c['g0'], b.c['g1'], b.c['g2'], b.c['g3']]

                def v_fin(v):
                    v.wait_ge(_sq, xold_ld)
                    v.wait_ge(_vp, acc_done)
                    for fcp in range(DC):
                        if fcp >= 4:
                            v.wait_ge(GS[fcp % 4], gb0[fcp % 4] + 16 * (fcp // 4))
                            if want_stats:
                                v.wait_ge(_sp, fin_sp0 + fcp - 3)
                                v.wait_ge(_tp, fin_tp0 + fcp - 3)
                        b.inc(v.scalar_tensor_tensor(
                            out=A2[:, (fcp % 4) * TC:(fcp % 4 + 1) * TC],
                            in0=A6[:, fcp * TC:(fcp + 1) * TC], scalar=dsc_sb[:, fcp:fcp + 1],
                            in1=A5[:, fcp * TC:(fcp + 1) * TC], op0=AT.mult, op1=AT.add), 'vp', 1)
                blk.vector(v_fin)

                def g_fin2(g):
                    for fcp in range(DC):
                        g.wait_ge(_vp, fin_vp0 + fcp + 1)
                        b.dma(g, d3(dst_ap, DC)[:, fcp, tc0:tc0 + TC],
                              A2[:, (fcp % 4) * TC:(fcp % 4 + 1) * TC], gsn[fcp % 4])
                blk.gpsimd(g_fin2)

                if want_stats:
                    def s_sq2(s):
                        for fcp in range(DC):
                            s.wait_ge(_vp, fin_vp0 + fcp + 1)
                            if fcp >= 2:
                                s.wait_ge(_tp, fin_tp0 + fcp - 1)
                            b.inc(s.activation(out=A3[:, (fcp % 2) * TC:(fcp % 2 + 1) * TC],
                                               in_=A2[:, (fcp % 4) * TC:(fcp % 4 + 1) * TC],
                                               func=AF.Square), 'sp', 1)
                    blk.scalar(s_sq2)

                    def t_st(t):
                        for fcp in range(DC):
                            t.wait_ge(_vp, fin_vp0 + fcp + 1)
                            t.matmul(PS6[0:1, 0:TC], ones_c[:],
                                     A2[:, (fcp % 4) * TC:(fcp % 4 + 1) * TC],
                                     start=(fcp == 0), stop=(fcp == DC - 1))
                            t.wait_ge(_sp, fin_sp0 + fcp + 1)
                            b.inc(t.matmul(PS7[0:1, 0:TC], ones_c[:],
                                           A3[:, (fcp % 2) * TC:(fcp % 2 + 1) * TC],
                                           start=(fcp == 0), stop=(fcp == DC - 1)), 'tp', 1)
                    blk.tensor(t_st)
                    emit_stats_finish(tc0)
                b.barrier()

        def emit_small_proj(W_ap, Bias_ap, nfc, rhs, dstf, mode='copy', base=None,
                            store=None, bias_n=None):
            bias_n = bias_n or nfc

            def q_bias(q):
                b.dma(q, bias_sb[:, 0:bias_n], Bias_ap, 'sq')
            blk.sync(q_bias)
            bld = b.c['sq']

            tp0 = b.c['tp']
            sp0 = b.c['sp']
            vp0 = b.c['vp']
            wb0 = [b.c['w0'], b.c['w1']]
            gb0 = b.c['g0']

            def q_w(q):
                for fc in range(nfc):
                    bb = fc % 2
                    if fc >= 2:
                        q.wait_ge(_tp, tp0 + fc - 1)
                    b.dma(q, sb3(A4[:, bb * 2048:(bb + 1) * 2048], DC, 128),
                          d3(W_ap, DC)[:, :, fc * 128:(fc + 1) * 128], wsn[bb])
            blk.sync(q_w)

            def t_mm(t):
                t.wait_ge(_sq, bld)
                for fc in range(nfc):
                    bb = fc % 2
                    ps = PS0 if bb == 0 else PS1
                    t.wait_ge(WS[bb], wb0[bb] + 16 * (fc // 2 + 1))
                    if fc >= 2:
                        if mode == 'copy':
                            t.wait_ge(_sp, sp0 + fc - 1)
                        else:
                            t.wait_ge(_vp, vp0 + fc - 1)
                    for dc in range(DC):
                        i = t.matmul(ps[0:128, 0:OWNLM],
                                     A4[:, bb * 2048 + dc * 128:bb * 2048 + (dc + 1) * 128],
                                     rhs(dc), start=(dc == 0), stop=(dc == DC - 1))
                    b.inc(i, 'tp', 1)
            blk.tensor(t_mm)

            if mode == 'copy':
                def s_ev(s):
                    for fc in range(nfc):
                        ps = PS0 if fc % 2 == 0 else PS1
                        s.wait_ge(_tp, tp0 + fc + 1)
                        b.inc(s.activation(out=dstf(fc), in_=ps[0:128, 0:OWNLM],
                                           func=AF.Identity, bias=bias_sb[:, fc:fc + 1]), 'sp', 1)
                blk.scalar(s_ev)
            else:
                def v_ev(v):
                    for fc in range(nfc):
                        ps = PS0 if fc % 2 == 0 else PS1
                        v.wait_ge(_tp, tp0 + fc + 1)
                        b.inc(v.scalar_tensor_tensor(out=dstf(fc), in0=ps[0:128, 0:OWNLM],
                                                     scalar=bias_sb[:, fc:fc + 1],
                                                     in1=base(fc), op0=AT.add, op1=AT.add),
                              'vp', 1)
                blk.vector(v_ev)

            if store is not None:
                da, row0f = store

                def g_s(g):
                    for fc in range(nfc):
                        if mode == 'copy':
                            g.wait_ge(_sp, sp0 + fc + 1)
                        else:
                            g.wait_ge(_vp, vp0 + fc + 1)
                        b.dma(g, d3(da, 0)[:, row0f(fc) // 128, :], dstf(fc), 'g0')
                blk.gpsimd(g_s)

        def emit_attn_b():
            def q_cs(q):
                b.dma(q, A6[:, 0:TOK], c_cos, 'sq')
                b.dma(q, A6[:, 4096:4096 + TOK], c_sin, 'sq')
                b.dma(q, pqa_sb[:], c_pqa, 'sq')
            blk.sync(q_cs)

            def dst_qkv_b(fc):
                if fc < 16:
                    return (scr_q, fc * 128)
                if fc < 32:
                    return (kvb, (fc - 16) * 128)
                return (kvb, D + (fc - 32) * 128)

            emit_qkv_stage(wqkv_b, bqkv_b, 48, dst_qkv_b, xres, ln_here=False,
                           rope=True, gate=True)

            gt_tp0 = b.c['tp']
            gt_vp0 = b.c['vp']

            def t_gt(t):
                for i in range(QT):
                    ps = PS2 if i % 2 == 0 else PS3
                    if i >= 2:
                        t.wait_ge(_vp, gt_vp0 + i - 1)
                    b.inc(t.transpose(ps[0:128, 0:32],
                                      gate_sm[:, i * 128:(i + 1) * 128],
                                      ident_sb[96:128, 96:128],
                                      tile_position=(96, 0)), 'tp', 1)
            blk.tensor(t_gt)

            def v_gt(v):
                for i in range(QT):
                    ps = PS2 if i % 2 == 0 else PS3
                    v.wait_ge(_tp, gt_tp0 + i + 1)
                    b.inc(v.tensor_copy(out=gate_t_sb[:, i * 32:(i + 1) * 32],
                                        in_=ps[0:128, 0:32]), 'vp', 1)
            blk.vector(v_gt)
            b.barrier()

            def kv_load(q, h, coloff, with_q=False):
                b.dma(q, A1[0:32, coloff:coloff + TOK], kvb[32 * h:32 * h + 32, :], wsn[h % 2])
                b.dma(q, A1[32:64, coloff:coloff + TOK],
                      kvb[1024 + 32 * h:1024 + 32 * h + 32, :], wsn[h % 2])
                b.dma(q, A1[64:128, coloff:coloff + TOK],
                      kvb[2048 + 64 * h:2048 + 64 * h + 64, :], wsn[h % 2])
                if with_q:
                    qoff = coloff + 2 * TOK
                    b.dma(q, A1[0:32, qoff:qoff + TOK], scr_q[32 * h:32 * h + 32, :], wsn[h % 2])
                    b.dma(q, A1[32:64, qoff:qoff + TOK],
                          scr_q[1024 + 32 * h:1024 + 32 * h + 32, :], wsn[h % 2])

            def emit_vt(t, coloff, vtoff, tpb, vpb):
                for i in range(QT):
                    ps = PS2 if i % 2 == 0 else PS3
                    if i >= 2:
                        t.wait_ge(_vp, vpb + i - 1)
                    b.inc(t.transpose(ps[0:128, 0:64],
                                      A1[64:128, coloff + i * 128:coloff + (i + 1) * 128],
                                      ident_sb[64:128, 64:128], tile_position=(64, 0)), 'tp', 1)

            def emit_vt_ev(v, vtoff, tpb):
                for i in range(QT):
                    ps = PS2 if i % 2 == 0 else PS3
                    v.wait_ge(_tp, tpb + i + 1)
                    b.inc(v.tensor_copy(out=A5[:, vtoff + 66 * i:vtoff + 66 * i + 64],
                                        in_=ps[0:128, 0:64]), 'vp', 1)

            def v_ones(v):
                v.memset(A5[:, 64:66 * QT:66], 1.0)
                b.inc(v.memset(A5[:, 2112 + 64:2112 + 66 * QT:66], 1.0), 'vp', 1)
            blk.vector(v_ones)
            b.barrier()

            p_tend = {}
            p_vend = {}
            p_send = {}
            for h in range(H2):
                coloff = (h % 2) * TOK
                vtoff = (h % 2) * 2112
                hb = 64 * (h % 2)
                hc = h // 2

                def q_h(q, h=h, coloff=coloff):
                    if h >= 2:
                        q.wait_ge(_tp, p_tend[h - 2])
                    kv_load(q, h, coloff)
                blk.sync(q_h)
                ld_mark = b.c[wsn[h % 2]]

                tpb = b.c['tp']
                vpb = b.c['vp']

                def t_p(t, coloff=coloff, vtoff=vtoff, ld_mark=ld_mark,
                        tpb=tpb, vpb=vpb, h=h):
                    t.wait_ge(WS[h % 2], ld_mark)
                    if h >= 1:
                        t.wait_ge(_vp, p_vend[h - 1])
                        t.wait_ge(_sp, p_send[h - 1])
                    emit_vt(t, coloff, vtoff, tpb, vpb)
                blk.tensor(t_p)

                def v_p(v, vtoff=vtoff, tpb=tpb):
                    emit_vt_ev(v, vtoff, tpb)
                blk.vector(v_p)
                vt_done = b.c['vp']

                def t_sc(t, h=h, coloff=coloff, vt_done=vt_done):
                    t.wait_ge(_vp, vt_done)
                    for i in range(QT):
                        b.inc(t.matmul(PS0[0:128, 4 * i:4 * i + 4],
                                       A1[0:64, coloff + i * 128:coloff + (i + 1) * 128],
                                       pqa_sb[0:64, 4 * h:4 * h + 4],
                                       start=True, stop=True), 'tp', 1)
                blk.tensor(t_sc)
                sc_done = b.c['tp']

                def s_e(s, sc_done=sc_done):
                    s.wait_ge(_tp, sc_done)
                    b.inc(s.activation(out=A3[:, 0:4 * QT], in_=PS0[0:128, 0:4 * QT],
                                       func=AF.Exp, scale=SC_B), 'sp', 1)
                blk.scalar(s_e)
                e_done = b.c['sp']

                def t_lm(t, vtoff=vtoff, e_done=e_done):
                    t.wait_ge(_sp, e_done)
                    for i in range(QT):
                        for bb_ in range(2):
                            psB = PS1 if bb_ == 0 else PS5
                            b.inc(t.matmul(psB[0:65, 4 * i:4 * i + 4],
                                           A5[64 * bb_:64 * bb_ + 64,
                                              vtoff + 66 * i:vtoff + 66 * i + 65],
                                           A3[64 * bb_:64 * bb_ + 64, 4 * i:4 * i + 4],
                                           start=True, stop=True,
                                           tile_position=(64 * bb_, 0)), 'tp', 1)
                blk.tensor(t_lm)
                lm_done = b.c['tp']

                def v_r(v, lm_done=lm_done):
                    v.wait_ge(_tp, lm_done)
                    m = b.inc(v.reciprocal(out=rden[:, 0:4 * QT], in_=PS1[64:65, 0:4 * QT]), 'vp', 1)
                    b.inc(v.reciprocal(out=rden[:, 256:256 + 4 * QT],
                                       in_=PS5[64:65, 0:4 * QT]), 'vp', 1)
                blk.vector(v_r)
                r_done = b.c['vp']

                def t_bc(t, r_done=r_done):
                    t.wait_ge(_vp, r_done)
                    t.matmul(PS4[0:64, 0:4 * QT], ones_r[64:65, 0:64],
                             rden[:, 0:4 * QT], start=True, stop=True,
                             tile_position=(64, 0))
                    b.inc(t.matmul(PS4[0:64, 256:256 + 4 * QT], ones_r[64:65, 0:64],
                                   rden[:, 256:256 + 4 * QT], start=True, stop=True,
                                   tile_position=(64, 0)), 'tp', 1)
                blk.tensor(t_bc)
                bc_done = b.c['tp']

                def s_c(s, bc_done=bc_done):
                    s.wait_ge(_tp, bc_done)
                    s.activation(out=A3[0:64, 256:256 + 4 * QT], in_=PS4[0:64, 0:4 * QT],
                                 func=AF.Copy)
                    b.inc(s.activation(out=A3[0:64, 384:384 + 4 * QT],
                                       in_=PS4[0:64, 256:256 + 4 * QT], func=AF.Copy), 'sp', 1)
                blk.scalar(s_c)
                c_done = b.c['sp']

                def v_out(v, hb=hb, hc=hc, c_done=c_done):
                    v.wait_ge(_sp, c_done)
                    lmf = A2[hb:hb + 64, hc * OWNLM:(hc + 1) * OWNLM]
                    lmf3 = lmf.rearrange("p (i m) -> p i m", m=8)
                    v.tensor_mul(out=lmf3[:, :, 0:4],
                                 in0=PS1[0:64, 0:4 * QT].rearrange("p (i m) -> p i m", m=4),
                                 in1=A3[0:64, 256:256 + 4 * QT].rearrange("p (i m) -> p i m", m=4))
                    b.inc(v.tensor_mul(out=lmf3[:, :, 4:8],
                                       in0=PS5[0:64, 0:4 * QT].rearrange("p (i m) -> p i m", m=4),
                                       in1=A3[0:64, 384:384 + 4 * QT].rearrange("p (i m) -> p i m", m=4)),
                          'vp', 1)
                blk.vector(v_out)
                p_tend[h] = b.c['tp']
                p_vend[h] = b.c['vp']
                p_send[h] = b.c['sp']
            b.barrier()


            if DEV_SUB == 31:
                def g_sub31(g):
                    b.dma(g, y[0:128, 0:2048], A2[:, 0:2048], 'gq')
                    g.wait_ge(_gq, b.c['gq'])
                blk.gpsimd(g_sub31)
                b.barrier()
                return

            def rhs_lmf0(dc):
                return A2[:, dc * OWNLM:(dc + 1) * OWNLM]

            def dst_lqkv(fc):
                return A5[:, fc * OWNLM:(fc + 1) * OWNLM]

            emit_small_proj(wlmqkv, blmqkv, 48, rhs_lmf0, dst_lqkv, mode='copy',
                            store=None, bias_n=48)
            st_sp = b.c['sp']

            def g_lkv(g):
                g.wait_ge(_sp, st_sp)
                for fc in range(16, 48):
                    b.dma(g, d3(ag3_in, 0)[:, fc - 16, :], dst_lqkv(fc), 'gq')
            blk.gpsimd(g_lkv)
            b.barrier()
            if PHASE2:
                emit_ag(ag3_in, ag3_out)

            def q_lkv(q):
                if PHASE2:
                    q.wait_ge(_cc, b.c['cc'])
                    a6v = A6[:, 0:32 * NL].rearrange("p (c l) -> p c l", l=NL)
                    b.dma(q, a6v[:, 0:16, 0:OWNLM], d3(ag3_out[0:2048, :], 16), 'sq')
                    b.dma(q, a6v[:, 0:16, OWNLM:NL], d3(ag3_out[4096:6144, :], 16), 'sq')
                    b.dma(q, a6v[:, 16:32, 0:OWNLM], d3(ag3_out[2048:4096, :], 16), 'sq')
                    b.dma(q, a6v[:, 16:32, OWNLM:NL], d3(ag3_out[6144:8192, :], 16), 'sq')
                else:
                    b.dma(q, sb3(A6[:, 0:32 * NL], 32, NL), d3(ag3_in, 32), 'sq')
            blk.sync(q_lkv)
            lkv_ld = b.c['sq']

            def v_ones2(v):
                b.inc(v.memset(A3[:, 576:577], 1.0), 'vp', 1)
            blk.vector(v_ones2)

            l_vend = {}
            l_send = {}
            for h in range(H2):
                hb = 64 * (h % 2)
                hc = h // 2
                tp_l0 = b.c['tp']
                vp_l0 = b.c['vp']
                sp_l0 = b.c['sp']

                def t_l(t, h=h, hb=hb, hc=hc, tp_l0=tp_l0, vp_l0=vp_l0):
                    t.wait_ge(_sq, lkv_ld)
                    if h >= 1:
                        t.wait_ge(_vp, l_vend[h - 1])
                        t.wait_ge(_sp, l_send[h - 1])
                    b.inc(t.matmul(PS0[0:128, 0:OWNLM],
                                   A6[hb:hb + 64, hc * NL:(hc + 1) * NL],
                                   A5[hb:hb + 64, (h // 2) * OWNLM + 0:(h // 2) * OWNLM + OWNLM]
                                   if False else
                                   A5[hb:hb + 64, hc * OWNLM:(hc + 1) * OWNLM],
                                   start=True, stop=True, tile_position=(hb, 0)), 'tp', 1)
                    b.inc(t.transpose(PS2[0:128, 0:64],
                                      A6[hb:hb + 64, (16 + hc) * NL:(16 + hc) * NL + 128],
                                      ident_sb[hb:hb + 64, hb:hb + 64],
                                      tile_position=(hb, 0)), 'tp', 1)
                    t.wait_ge(_vp, vp_l0 + 2)
                    b.inc(t.matmul(PS1[0:65, 0:OWNLM], A3[:, 512:577],
                                   A3[:, 0:OWNLM], start=True, stop=True), 'tp', 1)
                    t.wait_ge(_vp, vp_l0 + 4)
                    b.inc(t.matmul(PS4[0:64, 0:OWNLM], ones_r[64:65, 0:64],
                                   rden[:, 0:OWNLM], start=True, stop=True,
                                   tile_position=(64, 0)), 'tp', 1)
                blk.tensor(t_l)

                def s_l(s, tp_l0=tp_l0):
                    s.wait_ge(_tp, tp_l0 + 1)
                    b.inc(s.activation(out=A3[:, 0:OWNLM], in_=PS0[0:128, 0:OWNLM],
                                       func=AF.Exp, scale=SC_B), 'sp', 1)
                    s.wait_ge(_tp, tp_l0 + 4)
                    b.inc(s.activation(out=A3[0:64, 768:768 + OWNLM], in_=PS4[0:64, 0:OWNLM],
                                       func=AF.Copy), 'sp', 1)
                blk.scalar(s_l)

                def v_l(v, hb=hb, hc=hc, tp_l0=tp_l0, sp_l0=sp_l0):
                    v.wait_ge(_sp, sp_l0 + 1)
                    b.inc(v.tensor_mul(out=A3[:, 0:OWNLM], in0=A3[:, 0:OWNLM],
                                       in1=lmask_sb[:]), 'vp', 1)
                    v.wait_ge(_tp, tp_l0 + 2)
                    b.inc(v.tensor_copy(out=A3[:, 512:576], in_=PS2[0:128, 0:64]), 'vp', 1)
                    v.wait_ge(_tp, tp_l0 + 3)
                    m = b.inc(v.tensor_scalar_max(out=rden[:, 0:OWNLM],
                                                  in0=PS1[64:65, 0:OWNLM], scalar1=1e-9), 'vp', 1)
                    v.wait_ge(_vp, m)
                    b.inc(v.reciprocal(out=rden[:, 0:OWNLM], in_=rden[:, 0:OWNLM]), 'vp', 1)
                    v.wait_ge(_sp, sp_l0 + 2)
                    b.inc(v.tensor_mul(out=A2[hb:hb + 64, 2048 + hc * OWNLM:2048 + (hc + 1) * OWNLM],
                                       in0=PS1[0:64, 0:OWNLM],
                                       in1=A3[0:64, 768:768 + OWNLM]), 'vp', 1)
                blk.vector(v_l)
                l_vend[h] = b.c['vp']
                l_send[h] = b.c['sp']
            b.barrier()

            if DEV_SUB == 32:
                def g_sub32(g):
                    b.dma(g, y[0:128, 0:2048], A2[:, 2048:4096], 'gq')
                    g.wait_ge(_gq, b.c['gq'])
                blk.gpsimd(g_sub32)
                b.barrier()
                return

            def rhs_lo(dc):
                return A2[:, 2048 + dc * OWNLM:2048 + (dc + 1) * OWNLM]

            def dst_lmf1(fc):
                return A2[:, 4096 + fc * OWNLM:4096 + (fc + 1) * OWNLM]

            def base_lmf0(fc):
                return A2[:, fc * OWNLM:(fc + 1) * OWNLM]

            emit_small_proj(wlmout, blmout, 16, rhs_lo, dst_lmf1, mode='addres',
                            base=base_lmf0, bias_n=16)
            b.barrier()

            def rhs_lmf1(dc):
                return A2[:, 4096 + dc * OWNLM:4096 + (dc + 1) * OWNLM]

            def dst_l2q(fc):
                return A2[:, 6144 + fc * OWNLM:6144 + (fc + 1) * OWNLM]

            emit_small_proj(wl2tq, bl2tq, 16, rhs_lmf1, dst_l2q, mode='copy', bias_n=16)
            b.barrier()

            def v_ones_r(v):
                v.memset(A5[:, 64:66 * QT:66], 1.0)
                b.inc(v.memset(A5[:, 2112 + 64:2112 + 66 * QT:66], 1.0), 'vp', 1)
            blk.vector(v_ones_r)
            b.barrier()

            r_tend = {}
            r_vend = {}
            r_send = {}
            for h in range(H2):
                coloff = (h % 2) * TOK
                vtoff = (h % 2) * 2112
                hb = 64 * (h % 2)
                hc = h // 2
                xeb = (32 * h) % 128
                xec = (32 * h) // 128
                xob = (1024 + 32 * h) % 128
                xoc = (1024 + 32 * h) // 128

                def q_h2(q, h=h, coloff=coloff):
                    if h >= 2:
                        q.wait_ge(_tp, r_tend[h - 2])
                    kv_load(q, h, coloff)
                blk.sync(q_h2)
                ld_mark = b.c[wsn[h % 2]]

                tpb = b.c['tp']
                vpb = b.c['vp']

                def t_v2(t, coloff=coloff, vtoff=vtoff, ld_mark=ld_mark,
                         tpb=tpb, vpb=vpb, h=h):
                    t.wait_ge(WS[h % 2], ld_mark)
                    if h >= 1:
                        t.wait_ge(_vp, r_vend[h - 1])
                        t.wait_ge(_sp, r_send[h - 1])
                    emit_vt(t, coloff, vtoff, tpb, vpb)
                blk.tensor(t_v2)

                def v_v2(v, vtoff=vtoff, tpb=tpb):
                    emit_vt_ev(v, vtoff, tpb)
                blk.vector(v_v2)
                vt_done = b.c['vp']

                def v_cq(v, xeb=xeb, xec=xec, xob=xob, xoc=xoc, vt_done=vt_done):
                    v.wait_ge(_vp, vt_done)
                    v.tensor_copy(out=A3[0:32, 896:896 + OWNLM],
                                  in_=A2[xeb:xeb + 32, 6144 + xec * OWNLM:6144 + (xec + 1) * OWNLM])
                    b.inc(v.tensor_copy(out=A3[32:64, 896:896 + OWNLM],
                                        in_=A2[xob:xob + 32,
                                               6144 + xoc * OWNLM:6144 + (xoc + 1) * OWNLM]),
                          'vp', 1)
                blk.vector(v_cq)
                cq_done = b.c['vp']

                def t_r(t, coloff=coloff, cq_done=cq_done):
                    t.wait_ge(_vp, cq_done)
                    for i in range(QT):
                        b.inc(t.matmul(PS0[0:128, 8 * i:8 * i + 8],
                                       A1[0:64, coloff + i * 128:coloff + (i + 1) * 128],
                                       A3[0:64, 896 + 8 * i:896 + 8 * i + 8],
                                       start=True, stop=True), 'tp', 1)
                blk.tensor(t_r)
                rsc_done = b.c['tp']

                def s_r(s, rsc_done=rsc_done):
                    s.wait_ge(_tp, rsc_done)
                    b.inc(s.activation(out=A3[:, 0:8 * QT], in_=PS0[0:128, 0:8 * QT],
                                       func=AF.Exp, scale=SC_B), 'sp', 1)
                blk.scalar(s_r)
                er_done = b.c['sp']

                def v_m(v, er_done=er_done):
                    v.wait_ge(_sp, er_done)
                    b.inc(v.tensor_mul(out=A3[:, 0:8 * QT], in0=A3[:, 0:8 * QT],
                                       in1=l2tm_sb[:]), 'vp', 1)
                blk.vector(v_m)
                vm_done = b.c['vp']

                def t_ro(t, vtoff=vtoff, vm_done=vm_done):
                    t.wait_ge(_vp, vm_done)
                    for i in range(QT):
                        for bb_ in range(2):
                            psB = PS1 if bb_ == 0 else PS5
                            b.inc(t.matmul(psB[0:65, 4 * i:4 * i + 4],
                                           A5[64 * bb_:64 * bb_ + 64,
                                              vtoff + 66 * i:vtoff + 66 * i + 65],
                                           A3[64 * bb_:64 * bb_ + 64,
                                              8 * i + 4 * bb_:8 * i + 4 * bb_ + 4],
                                           start=True, stop=True,
                                           tile_position=(64 * bb_, 0)), 'tp', 1)
                blk.tensor(t_ro)
                ro_done = b.c['tp']

                def v_r2(v, ro_done=ro_done):
                    v.wait_ge(_tp, ro_done)
                    v.reciprocal(out=rden[:, 0:4 * QT], in_=PS1[64:65, 0:4 * QT])
                    b.inc(v.reciprocal(out=rden[:, 256:256 + 4 * QT],
                                       in_=PS5[64:65, 0:4 * QT]), 'vp', 1)
                blk.vector(v_r2)
                r2_done = b.c['vp']

                def t_bc2(t, r2_done=r2_done):
                    t.wait_ge(_vp, r2_done)
                    t.matmul(PS4[0:64, 0:4 * QT], ones_r[64:65, 0:64],
                             rden[:, 0:4 * QT], start=True, stop=True,
                             tile_position=(64, 0))
                    b.inc(t.matmul(PS4[0:64, 256:256 + 4 * QT], ones_r[64:65, 0:64],
                                   rden[:, 256:256 + 4 * QT], start=True, stop=True,
                                   tile_position=(64, 0)), 'tp', 1)
                blk.tensor(t_bc2)
                bc2_done = b.c['tp']

                def s_c2(s, bc2_done=bc2_done):
                    s.wait_ge(_tp, bc2_done)
                    s.activation(out=A3[0:64, 256:256 + 4 * QT], in_=PS4[0:64, 0:4 * QT],
                                 func=AF.Copy)
                    b.inc(s.activation(out=A3[0:64, 384:384 + 4 * QT],
                                       in_=PS4[0:64, 256:256 + 4 * QT], func=AF.Copy), 'sp', 1)
                blk.scalar(s_c2)
                c2_done = b.c['sp']

                def v_o2(v, hb=hb, hc=hc, c2_done=c2_done):
                    v.wait_ge(_sp, c2_done)
                    ro_ = A2[hb:hb + 64, 2048 + hc * OWNLM:2048 + (hc + 1) * OWNLM]
                    ro3 = ro_.rearrange("p (i m) -> p i m", m=8)
                    v.tensor_mul(out=ro3[:, :, 0:4],
                                 in0=PS1[0:64, 0:4 * QT].rearrange("p (i m) -> p i m", m=4),
                                 in1=A3[0:64, 256:256 + 4 * QT].rearrange("p (i m) -> p i m", m=4))
                    b.inc(v.tensor_mul(out=ro3[:, :, 4:8],
                                       in0=PS5[0:64, 0:4 * QT].rearrange("p (i m) -> p i m", m=4),
                                       in1=A3[0:64, 384:384 + 4 * QT].rearrange("p (i m) -> p i m", m=4)),
                          'vp', 1)
                blk.vector(v_o2)
                r_tend[h] = b.c['tp']
                r_vend[h] = b.c['vp']
                r_send[h] = b.c['sp']
            b.barrier()

            if DEV_SUB == 33:
                def g_sub33(g):
                    b.dma(g, y[0:128, 0:2048], A2[:, 2048:4096], 'gq')
                    g.wait_ge(_gq, b.c['gq'])
                blk.gpsimd(g_sub33)
                b.barrier()
                return

            def dst_lmf2(fc):
                return A2[:, fc * OWNLM:(fc + 1) * OWNLM]

            emit_small_proj(wl2tout, bl2tout, 16, rhs_lo, dst_lmf2, mode='addres',
                            base=rhs_lmf1, bias_n=16)
            b.barrier()

            def rhs_lmf2(dc):
                return A2[:, dc * OWNLM:(dc + 1) * OWNLM]

            def dst_gkv(fc):
                return A5[:, fc * OWNLM:(fc + 1) * OWNLM]

            emit_small_proj(wlmkv, blmkv, 32, rhs_lmf2, dst_gkv, mode='copy',
                            store=(ag4_in, lambda fc: fc * 128), bias_n=32)
            b.barrier()
            if PHASE2:
                emit_ag(ag4_in, ag4_out)

            if DEV_DBG:
                def g_dbg2(g):
                    b.dma(g, dbg[:, 0:2048], A2[:, 2048:4096], 'gq')
                    b.dma(g, dbg[:, 2048:4096], A2[:, 0:2048], 'gq')
                    g.wait_ge(_gq, b.c['gq'])
                blk.gpsimd(g_dbg2)
                b.barrier()

            if DEV_SUB == 34:
                def g_sub34(g):
                    b.dma(g, y[0:2048, 0:128], ag4_in[0:2048, :], 'gq')
                    g.wait_ge(_gq, b.c['gq'])
                blk.gpsimd(g_sub34)
                b.barrier()
                return

            def q_gm(q):
                b.dma(q, A6[:, 0:QT * 128], c_gmask, 'sq')
            blk.sync(q_gm)
            gmask_sb = A6
            b.barrier()
            def v_ones_r2(v):
                v.memset(A5[:, 64:66 * QT:66], 1.0)
                b.inc(v.memset(A5[:, 2112 + 64:2112 + 66 * QT:66], 1.0), 'vp', 1)
            blk.vector(v_ones_r2)
            b.barrier()

            f_tend = {}
            f_vend = {}
            f_send = {}
            f_gend = {}
            for h in range(H2):
                coloff = (h % 2) * TOK
                qoff = coloff + 2 * TOK
                vtoff = (h % 2) * 2112
                hb = 64 * (h % 2)

                def q_h3(q, h=h, coloff=coloff):
                    if PHASE2 and h == 0:
                        q.wait_ge(_cc, b.c['cc'])
                    if h >= 2:
                        q.wait_ge(_tp, f_tend[h - 2])
                    kv_load(q, h, coloff, with_q=True)
                    if h >= 1:
                        q.wait_ge(_tp, f_tend[h - 1])
                    if PHASE2:
                        b.dma(q, A3[0:32, 896:896 + OWNLM], ag4_out[32 * h:32 * h + 32, :], wsn[h % 2])
                        b.dma(q, A3[0:32, 896 + OWNLM:896 + NL],
                              ag4_out[4096 + 32 * h:4096 + 32 * h + 32, :], wsn[h % 2])
                        b.dma(q, A3[32:64, 896:896 + OWNLM],
                              ag4_out[1024 + 32 * h:1024 + 32 * h + 32, :], wsn[h % 2])
                        b.dma(q, A3[32:64, 896 + OWNLM:896 + NL],
                              ag4_out[4096 + 1024 + 32 * h:4096 + 1024 + 32 * h + 32, :], wsn[h % 2])
                        b.dma(q, A3[64:128, 1024:1024 + OWNLM],
                              ag4_out[2048 + 64 * h:2048 + 64 * h + 64, :], wsn[h % 2])
                        b.dma(q, A3[64:128, 1024 + OWNLM:1024 + NL],
                              ag4_out[4096 + 2048 + 64 * h:4096 + 2048 + 64 * h + 64, :], wsn[h % 2])
                    else:
                        b.dma(q, A3[0:32, 896:896 + NL], ag4_in[32 * h:32 * h + 32, :], wsn[h % 2])
                        b.dma(q, A3[32:64, 896:896 + NL],
                              ag4_in[1024 + 32 * h:1024 + 32 * h + 32, :], wsn[h % 2])
                        b.dma(q, A3[64:128, 1024:1024 + NL],
                              ag4_in[2048 + 64 * h:2048 + 64 * h + 64, :], wsn[h % 2])
                blk.sync(q_h3)
                ld_mark = b.c[wsn[h % 2]]

                tpb = b.c['tp']
                vpb = b.c['vp']

                def t_v3(t, coloff=coloff, vtoff=vtoff, ld_mark=ld_mark,
                         tpb=tpb, vpb=vpb, h=h):
                    t.wait_ge(WS[h % 2], ld_mark)
                    if h >= 1:
                        t.wait_ge(_vp, f_vend[h - 1])
                        t.wait_ge(_sp, f_send[h - 1])
                    emit_vt(t, coloff, vtoff, tpb, vpb)
                    b.inc(t.transpose(PS5[0:128, 0:64], A3[64:128, 1024:1024 + NL],
                                      ident_sb[64:128, 64:128], tile_position=(64, 0)), 'tp', 1)
                blk.tensor(t_v3)
                gvt_tp = b.c['tp']

                def v_v3(v, vtoff=vtoff, tpb=tpb, gvt_tp=gvt_tp, h=h):
                    if h >= 2:
                        v.wait_ge(_g4, f_gend[h - 2])
                    emit_vt_ev(v, vtoff, tpb)
                    v.wait_ge(_tp, gvt_tp)
                    b.inc(v.tensor_copy(out=A3[:, 512:576], in_=PS5[0:128, 0:64]), 'vp', 1)
                blk.vector(v_v3)
                vt_done = b.c['vp']

                scm = {}
                t_base = b.c['tp']
                tpc2 = t_base
                for i in range(QT):
                    if i == 0:
                        tpc2 += 1; scm[('l', 0)] = tpc2
                        tpc2 += 1; scm[('g', 0)] = tpc2
                    if i + 1 < QT:
                        tpc2 += 1; scm[('l', i + 1)] = tpc2
                        tpc2 += 1; scm[('g', i + 1)] = tpc2
                    tpc2 += 1; scm[('ol', i)] = tpc2
                    tpc2 += 1; scm[('og', i)] = tpc2
                    tpc2 += 1; scm[('tr', i)] = tpc2

                vm = {}
                v_base = b.c['vp']
                for i in range(QT):
                    vm[i] = v_base + 11 * i
                sm_ = {}
                s_base = b.c['sp']
                for i in range(QT):
                    sm_[i] = s_base + 2 * i

                def t_a_all(t, h=h, coloff=coloff, qoff=qoff, vtoff=vtoff,
                            vt_done=vt_done):
                    t.wait_ge(_vp, vt_done)

                    def sc_pair(i):
                        psl = PS0 if i % 2 == 0 else PS6
                        psg = PS2 if i % 2 == 0 else PS7
                        if i >= 2:
                            t.wait_ge(_sp, sm_[i - 1])
                        b.inc(t.matmul(psl[0:128, 0:128],
                                       A1[0:64, coloff + i * 128:coloff + (i + 1) * 128],
                                       A1[0:64, qoff + i * 128:qoff + (i + 1) * 128],
                                       start=True, stop=True), 'tp', 1)
                        b.inc(t.matmul(psg[0:128, 0:128],
                                       A3[0:64, 896:896 + NL],
                                       A1[0:64, qoff + i * 128:qoff + (i + 1) * 128],
                                       start=True, stop=True), 'tp', 1)

                    sc_pair(0)
                    for i in range(QT):
                        if i + 1 < QT:
                            sc_pair(i + 1)
                        t.wait_ge(_vp, vm[i] + 1)
                        b.inc(t.matmul(PS1[0:128, 0:65], A3[:, 0:128],
                                       A5[:, vtoff + 66 * i:vtoff + 66 * i + 65],
                                       start=True, stop=True), 'tp', 1)
                        t.wait_ge(_vp, vm[i] + 2)
                        b.inc(t.matmul(PS3[0:128, 0:65], A3[:, 128:256],
                                       A3[:, 512:577], start=True, stop=True), 'tp', 1)
                        t.wait_ge(_vp, vm[i] + 10)
                        b.inc(t.transpose(PS4[0:64, 0:128], A3[:, 640:704],
                                          ident_sb[:]), 'tp', 1)
                blk.tensor(t_a_all)

                def s_a_all(s, h=h):
                    for i in range(QT):
                        psl = PS0 if i % 2 == 0 else PS6
                        psg = PS2 if i % 2 == 0 else PS7
                        s.wait_ge(_tp, scm[('l', i)])
                        if i >= 1:
                            s.wait_ge(_vp, vm[i - 1] + 2)
                            s.wait_ge(_tp, scm[('og', i - 1)])
                        b.inc(s.activation(out=A3[:, 0:128], in_=psl[0:128, 0:128],
                                           func=AF.Exp, scale=SC_B), 'sp', 1)
                        s.wait_ge(_tp, scm[('g', i)])
                        b.inc(s.activation(out=A3[:, 128:256], in_=psg[0:128, 0:128],
                                           func=AF.Exp, scale=SC_B), 'sp', 1)
                blk.scalar(s_a_all)

                for i in range(QT):
                    t0 = None
                    s0 = sm_[i]
                    v0 = vm[i]

                    def v_a(v, h=h, i=i, hb=hb, t0=t0, s0=s0, v0=v0):
                        v.wait_ge(_sp, s0 + 1)
                        b.inc(v.tensor_mul(out=A3[:, 0:128], in0=A3[:, 0:128],
                                           in1=blktri_sb[:]), 'vp', 1)
                        v.wait_ge(_sp, s0 + 2)
                        b.inc(v.tensor_mul(out=A3[:, 128:256], in0=A3[:, 128:256],
                                           in1=gmask_sb[:, i * 128:(i + 1) * 128]), 'vp', 1)
                        v.wait_ge(_tp, scm[('ol', i)])
                        m = b.inc(v.tensor_scalar_max(out=A3[:, 704:705],
                                                      in0=PS1[0:128, 64:65], scalar1=1e-9), 'vp', 1)
                        v.wait_ge(_vp, m)
                        m = b.inc(v.reciprocal(out=A3[:, 705:706], in_=A3[:, 704:705]), 'vp', 1)
                        v.wait_ge(_tp, scm[('og', i)])
                        m2 = b.inc(v.tensor_scalar_max(out=A3[:, 706:707],
                                                       in0=PS3[0:128, 64:65], scalar1=1e-9), 'vp', 1)
                        v.wait_ge(_vp, m2)
                        mg = b.inc(v.reciprocal(out=A3[:, 707:708], in_=A3[:, 706:707]), 'vp', 1)
                        m3 = b.inc(v.tensor_scalar_mul(out=A3[:, 768:832], in0=PS1[0:128, 0:64],
                                                       scalar1=A3[:, 705:706]), 'vp', 1)
                        v.wait_ge(_vp, mg)
                        b.inc(v.tensor_scalar_mul(out=A3[:, 832:896], in0=PS3[0:128, 0:64],
                                                  scalar1=A3[:, 707:708]), 'vp', 1)
                        v.wait_ge(_vp, m3 + 1)
                        m4 = b.inc(v.tensor_sub(out=A3[:, 640:704], in0=A3[:, 832:896],
                                                in1=A3[:, 768:832]), 'vp', 1)
                        v.wait_ge(_vp, m4)
                        b.inc(v.scalar_tensor_tensor(out=A3[:, 640:704], in0=A3[:, 640:704],
                                                     scalar=gate_t_sb[:, 32 * i + h:32 * i + h + 1],
                                                     in1=A3[:, 768:832], op0=AT.mult, op1=AT.add),
                              'vp', 1)
                        v.wait_ge(_tp, scm[('tr', i)])
                        b.inc(v.tensor_copy(out=A2[hb:hb + 64, i * 128:(i + 1) * 128],
                                            in_=PS4[0:64, 0:128]), 'vp', 1)
                    blk.vector(v_a)

                st_v = b.c['vp']

                def g_o(g, h=h, hb=hb, st_v=st_v):
                    g.wait_ge(_vp, st_v)
                    if h >= 1:
                        g.wait_ge(_g4, f_gend[h - 1])
                    b.dma(g, scr_ao[64 * h:64 * h + 64, :], A2[hb:hb + 64, 0:TOK], 'g4')
                blk.gpsimd(g_o)
                f_tend[h] = b.c['tp']
                f_vend[h] = b.c['vp']
                f_send[h] = b.c['sp']
                f_gend[h] = b.c['g4']
            b.barrier()

        def dst_qkv_a(fc):
            if fc < 16:
                return (scr_q, fc * 128)
            if fc < 32:
                return (ag1_in, (fc - 16) * 128)
            return (ag1_in, D + (fc - 32) * 128)

        def emit_ag(in_ap, out_ap):
            def g_ag(g):
                b.inc(g.collective_compute(
                    "AllGather", AT.bypass,
                    replica_groups=[[0, 1], [2, 3], [4, 5], [6, 7]],
                    ins=[in_ap.opt()], outs=[out_ap.opt()]), 'cc', 1)
            blk.gpsimd(g_ag)

        emit_qkv_stage(wqkv_a, bqkv_a, 48, dst_qkv_a, x0, ln_here=True)
        if PHASE2:
            emit_ag(ag1_in, ag1_out)
        emit_attn_a()
        dst1 = y if DEV_STOP == 1 else xres
        emit_proj_res(scr_ao, wo_a, bo_a, x0, dst1, want_stats=(DEV_STOP > 1))

        if DEV_STOP >= 2:
            dst2 = y if DEV_STOP == 2 else xres
            emit_ffn(xres, dst2, want_stats=(DEV_STOP > 2))

        if DEV_STOP >= 3:
            emit_attn_b()
            if DEV_SUB == 0:
                dst3 = y if DEV_STOP == 3 else xres
                emit_proj_res(scr_ao, wout, bout, xres, dst3, want_stats=(DEV_STOP > 3))

        if DEV_STOP >= 4 and DEV_SUB == 0:
            emit_ffn(xres, y, want_stats=False)

        def g_fin(g):
            g.wait_ge(_gq, b.c['gq'])
        blk.gpsimd(g_fin)

    return nc


_NC_CACHE = {}


def kernel(**inputs):
    in_maps = stage_inputs(inputs)
    key = (PHASE2, DEV_STOP, DEV_SUB)
    if key not in _NC_CACHE:
        _NC_CACHE[key] = build_program()
    nc = _NC_CACHE[key]
    res = run_bass_kernel_spmd(nc, in_maps, core_ids=list(range(8)))
    out = np.empty((Bsz, L, D), dtype=np.float32)
    for core in range(8):
        if not PHASE2 and core % 2 == 1:
            continue
        s = core // 2
        own_tiles = list(range(core % 2, 16, 2)) if PHASE2 else list(range(16))
        yc = res.results[core]['y']
        for ii, tt in enumerate(own_tiles):
            out[s, 128 * tt:128 * tt + 128, :] = yc[:, 128 * ii:128 * (ii + 1)].T
    return out


if __name__ == "__main__":
    pass
